# version 05
# baseline (speedup 1.0000x reference)
import numpy as np
import ml_dtypes
from contextlib import ExitStack

import concourse.bass as bass
import concourse.mybir as mybir
from concourse.bass_utils import run_bass_kernel_spmd

F32 = mybir.dt.float32
BF16 = mybir.dt.bfloat16
AF = mybir.ActivationFunctionType
ALU = mybir.AluOpType

T = 4096
D = 1024
NCH = D // 128
TG = 512
NTG = T // TG
NCORES = 8
NTT = T // 128

ENGINES = ("sync", "gpsimd", "vector", "scalar", "tensor")
SEMNAMES = ("ln_ld", "ln_sq", "ln_mm", "ln_ev", "ln_h", "ldg", "ld",
            "mm", "act", "ev", "st")



def _rope_tables_l0():
    HD = 64
    nb = HD // 4
    inv = 1.0 / 10000.0 ** (np.arange(nb, dtype=np.float64) / nb)
    ang = np.arange(T, dtype=np.float64)[:, None] * inv[None, :]
    sin_h = np.repeat(np.sin(ang), 2, axis=-1)
    cos_h = np.repeat(np.cos(ang), 2, axis=-1)
    sin = np.concatenate([sin_h, sin_h], -1)
    cos = np.concatenate([cos_h, cos_h], -1)
    cosT = cos.T.astype(np.float32)
    sinT = sin.T.astype(np.float32)
    ss = sinT.copy()
    ss[:32] *= -1.0
    cos2 = np.vstack([cosT, cosT])
    ss2 = np.vstack([ss, ss])
    return np.ascontiguousarray(cos2), np.ascontiguousarray(ss2)


def _rope_tables_l1():
    DH = 64
    inv = 1.0 / 10000.0 ** (np.arange(0, DH, 2, dtype=np.float64) / DH)
    ang = np.arange(T, dtype=np.float64)[:, None] * inv[None, :]
    sn = np.sin(ang).T
    cs = np.cos(ang).T
    cosT = np.vstack([cs, cs]).astype(np.float32)
    ss = np.vstack([-sn, sn]).astype(np.float32)
    cos2 = np.vstack([cosT, cosT])
    ss2 = np.vstack([ss, ss])
    return np.ascontiguousarray(cos2), np.ascontiguousarray(ss2)


def _partner_swap(tab):
    out = tab.copy()
    for b in range(4):
        out[32 * b:32 * b + 32] = tab[32 * (b ^ 1):32 * (b ^ 1) + 32]
    return np.ascontiguousarray(out)


def _evenodd_perm():
    p = []
    for h in range(16):
        base = h * 64
        p.extend(base + np.arange(0, 64, 2))
        p.extend(base + np.arange(1, 64, 2))
    return np.array(p)


QPAIRS = [(0, 4), (1, 5), (2, 6), (3, 7),
          (8, 12), (9, 13), (10, 14), (11, 15)]
HEADPOS = {}
for _c, (_a, _b) in enumerate(QPAIRS):
    HEADPOS[_a] = (_c, 0)
    HEADPOS[_b] = (_c, 64)


def _qperm():
    idx = []
    for (a, b) in QPAIRS:
        idx.extend(range(a * 64, a * 64 + 64))
        idx.extend(range(b * 64, b * 64 + 64))
    return np.array(idx)


def _l0_masks():
    HW, G = 12, 8
    uniq, masks, tilemap = {}, [], {}
    i_all = np.arange(T)
    for qg in range(NTG):
        qi = i_all[qg * TG:(qg + 1) * TG][None, :]
        for kt in range(NTT):
            kj = i_all[kt * 128:(kt + 1) * 128][:, None]
            allowed = (np.abs(qi - kj) <= HW) | (qi < G) | (kj < G)
            if not allowed.any():
                tilemap[(qg, kt)] = "skip"
            elif allowed.all():
                tilemap[(qg, kt)] = "full"
            else:
                key = allowed.tobytes()
                if key not in uniq:
                    uniq[key] = len(masks)
                    masks.append(allowed.astype(ml_dtypes.bfloat16))
                tilemap[(qg, kt)] = uniq[key]
    return np.stack(masks), tilemap



class Rec:
    def __init__(self):
        self.ops = {e: [] for e in ENGINES}
        self.cnt = {}

    def op(self, eng, meth, *a, waits=(), inc=None, inck=1, **kw):
        waits = [(s, v) for (s, v) in waits if v > 0]
        self.ops[eng].append((meth, a, kw, tuple(waits), inc, inck))
        if inc is not None:
            self.cnt[inc] = self.cnt.get(inc, 0) + inck
            return self.cnt[inc]
        return None

    def n(self, sem):
        return self.cnt.get(sem, 0)

    def replay(self, nc, sems):
        with nc.Block() as blk:
            def mk(eng_name):
                def body(eng):
                    for meth, a, kw, waits, inc, inck in self.ops[eng_name]:
                        for (s, v) in waits:
                            eng.wait_ge(sems[s], v)
                        if meth is None:
                            continue
                        try:
                            ins = getattr(eng, meth)(*a, **kw)
                        except Exception:
                            import sys
                            print(f"replay error: {eng_name}.{meth} "
                                  f"args={a} kw={kw}", file=sys.stderr)
                            raise
                        if inc is not None:
                            ins.then_inc(sems[inc], inck)
                return body
            for e in ENGINES:
                if self.ops[e]:
                    getattr(blk, e)(mk(e))


def fence(rec, sems):
    waits = [(s, rec.n(s)) for s in sems if rec.n(s) > 0]
    for e in ENGINES:
        rec.op(e, None, waits=waits)



def emit_ln(rec, B, x_hbm, h_sb, kind="ln", weight_cols=None, eps=1e-5,
            h_hbm=None):
    xt, row = B["xt"], B["row"]
    t1 = B["t1"][:, :]
    sqbufs = (B["sq0"], B["sq1"])
    pb = B["pb"]
    onescol, onesrow = B["onescol_f32"], B["onesrow_f32"]
    ln = kind == "ln"
    mm_after = {}
    h_after = {}
    prev_rows_ev = 0
    prev_h = 0
    for tg in range(NTG):
        c0 = tg * TG
        for c in range(NCH):
            buf = c % 2
            xts = xt[:, buf * TG:(buf + 1) * TG]
            sqs = sqbufs[buf][:, :]
            waits = []
            if c >= 2:
                waits.append(("ln_mm", mm_after[(tg, c - 2)]))
            elif tg > 0:
                waits.append(("ln_h", h_after[(tg - 1, NCH - 2 + c)]))
            ldv = rec.op("sync", "dma_start", out=xts,
                         in_=x_hbm[c * 128:(c + 1) * 128, c0:c0 + TG],
                         waits=waits, inc="ln_ld", inck=16)
            sqv = rec.op("scalar", "activation", sqs, xts, AF.Square,
                         waits=[("ln_ld", ldv)], inc="ln_sq")
            w0 = [("ln_ld", ldv)]
            if c == 0 and prev_rows_ev:
                w0.append(("ln_ev", prev_rows_ev))
            if ln:
                rec.op("tensor", "matmul", pb[0][0:1, :], onescol[:, 0:1], xts,
                       start=(c == 0), stop=(c == NCH - 1),
                       waits=w0, inc="ln_mm")
            rec.op("tensor", "matmul", pb[1][0:1, :], onescol[:, 0:1], sqs,
                   start=(c == 0), stop=(c == NCH - 1),
                   waits=[("ln_sq", sqv)] + (w0[1:] if c == 0 else []),
                   inc="ln_mm")
            mm_after[(tg, c)] = rec.n("ln_mm")
        mm_stats = rec.n("ln_mm")
        mrow = row[0:1, 0:TG]
        msq = row[0:1, TG:2 * TG]
        vrow = row[0:1, 2 * TG:3 * TG]
        rrow = row[0:1, 3 * TG:4 * TG]
        if ln:
            rec.op("vector", "tensor_scalar_mul", mrow, pb[0][0:1, :],
                   1.0 / D, waits=[("ln_mm", mm_stats)])
            rec.op("vector", "tensor_mul", msq, mrow, mrow)
            rec.op("vector", "scalar_tensor_tensor", vrow, pb[1][0:1, :],
                   1.0 / D, msq, ALU.mult, ALU.subtract, inc="ln_ev")
        else:
            rec.op("vector", "tensor_scalar_mul", vrow, pb[1][0:1, :],
                   1.0 / D, waits=[("ln_mm", mm_stats)], inc="ln_ev")
        sv = rec.op("scalar", "activation", vrow, vrow, AF.Sqrt,
                    bias=B["epscol"], waits=[("ln_ev", rec.n("ln_ev"))],
                    inc="ln_sq")
        rv = rec.op("vector", "reciprocal", rrow, vrow,
                    waits=[("ln_sq", sv)], inc="ln_ev")
        rv = rec.op("vector", "tensor_copy", t1[:, 0:256], t1[:, 256:512],
                    inc="ln_ev")
        prev_rows_ev = rv
        wbc = [("ln_ev", rv)]
        if prev_h:
            wbc.append(("ln_h", prev_h))
        if ln:
            rec.op("tensor", "matmul", pb[2][:, :], onesrow[0:1, :], mrow,
                   start=True, stop=True, waits=wbc, inc="ln_mm")
        rec.op("tensor", "matmul", pb[3][:, :], onesrow[0:1, :], rrow,
               start=True, stop=True, waits=wbc, inc="ln_mm")
        mm_bc = rec.n("ln_mm")
        for c in range(NCH):
            buf = c % 2
            xts = xt[:, buf * TG:(buf + 1) * TG]
            waits = []
            if c >= 2:
                waits.append(("ln_h", h_after[(tg, c - 2)]))
            else:
                waits.append(("ln_mm", mm_stats))
            ldv = rec.op("sync", "dma_start", out=xts,
                         in_=x_hbm[c * 128:(c + 1) * 128, c0:c0 + TG],
                         waits=waits, inc="ln_ld", inck=16)
            gi = tg * NCH + c
            if h_hbm is None:
                hs = h_sb[:, c * T + c0: c * T + c0 + TG]
            else:
                hs = h_sb[:, buf * TG:(buf + 1) * TG]
            w = [("ln_ld", ldv), ("ln_mm", mm_bc)]
            if h_hbm is not None and gi >= 2:
                w.append(("st", B["h_store"][gi - 2]))
            if ln:
                rec.op("vector", "tensor_sub", t1, xts, pb[2][:, :], waits=w)
                h_after[(tg, c)] = rec.op("vector", "tensor_mul", hs, t1,
                                          pb[3][:, :], inc="ln_h")
            else:
                if weight_cols is not None:
                    h_after[(tg, c)] = rec.op(
                        "vector", "scalar_tensor_tensor", hs, xts,
                        weight_cols[c], pb[3][:, :], ALU.mult, ALU.mult,
                        waits=w, inc="ln_h")
                else:
                    h_after[(tg, c)] = rec.op("vector", "tensor_mul", hs, xts,
                                              pb[3][:, :], waits=w, inc="ln_h")
            if h_hbm is not None:
                B["h_store"][gi] = rec.op(
                    "gpsimd", "dma_start",
                    out=h_hbm[c * 128:(c + 1) * 128, c0:c0 + TG], in_=hs,
                    waits=[("ln_h", h_after[(tg, c)])], inc="st", inck=16)
        prev_h = rec.n("ln_h")



def tt_off(kt):
    return kt * (4 * 65)


def build_l0_attn(nmasks, tilemap, ctx=None):
    if ctx is None:
        nc = bass.Bass("TRN2", target_bir_lowering=False)
        rec = Rec()
        x_hbm = nc.dram_tensor("x_fm", [D, T], F32,
                               kind="ExternalInput").ap()
        xo_hbm = nc.dram_tensor("x_out", [D, T], F32,
                                kind="ExternalOutput").ap()
        pre = ""
    else:
        nc, rec, sems, pb, x_hbm, xo_hbm, pre = ctx
    H, HKV, HD = 16, 4, 64
    NKV = HKV * HD

    wq_h = nc.dram_tensor(pre + "Wq", [D, D], F32, kind="ExternalInput").ap()
    bq_h = nc.dram_tensor(pre + "bq", [D], F32, kind="ExternalInput").ap()
    wk_h = nc.dram_tensor(pre + "Wk", [D, NKV], F32, kind="ExternalInput").ap()
    wv_h = nc.dram_tensor(pre + "Wv", [D, NKV], F32, kind="ExternalInput").ap()
    wo_h = nc.dram_tensor(pre + "Wo", [D, D], F32, kind="ExternalInput").ap()
    bo_h = nc.dram_tensor(pre + "bo", [D], F32, kind="ExternalInput").ap()
    hw_h = nc.dram_tensor(pre + "hw", [1, H], F32, kind="ExternalInput").ap()
    cos_h = nc.dram_tensor(pre + "cos2", [128, T], F32,
                           kind="ExternalInput").ap()
    ss_h = nc.dram_tensor(pre + "ss2", [128, T], F32,
                          kind="ExternalInput").ap()
    msk_h = nc.dram_tensor(pre + "masks", [nmasks, 128, TG], BF16,
                           kind="ExternalInput").ap()
    bkv_h = nc.dram_tensor(pre + "bkv", [2, NKV], F32,
                           kind="ExternalInput").ap()

    es = ExitStack()
    sbt = lambda name, shape, dt: es.enter_context(
        nc.sbuf_tensor(pre + name, shape, dt))
    with es:
        h_sb = sbt("h", [128, NCH * T], BF16)
        k_sb = sbt("k", [128, 2 * T], BF16)
        v_sb = sbt("v", [128, NTT * (HKV * (HD + 1))], BF16)
        q_sb = sbt("q", [128, NCH * TG], BF16)
        a_sb = sbt("a", [128, NCH * TG], BF16)
        wqb = sbt("wqb", [128, 2 * NCH * 128], BF16)
        wob = sbt("wob", [128, 2 * NCH * 128], BF16)
        wk_sb = sbt("wk", [128, NCH * NKV], BF16)
        wv_sb = sbt("wv", [128, NCH * NKV], BF16)
        cos_sb = sbt("cos", [128, T], F32)
        ss_sb = sbt("ss", [128, T], F32)
        msk_sb = sbt("msk", [128, nmasks * TG], BF16)
        bq_sb = sbt("bqc", [128, NCH], F32)
        bo_sb = sbt("boc", [128, NCH], F32)
        bk_sb = sbt("bkc", [128, 2], F32)
        bvrow = sbt("bvrow", [1, NKV], BF16)
        hw_sb = sbt("hwbc", [128, H], F32)
        hwq_sb = sbt("hwq", [128, NCH], F32)
        hwrow = sbt("hwrow", [1, H], F32)
        onescol_f32 = sbt("ones_c", [128, 1], F32)
        epscol = sbt("epscol", [128, 1], F32)
        onesrow_f32 = sbt("ones_r", [1, 128], F32)
        onesrow_bf = sbt("ones_rb", [1, 128], BF16)
        ones64_f32 = sbt("ones64", [1, 64], F32)
        xt = sbt("xt", [128, 2 * TG], F32)
        row = sbt("rowbuf", [1, 4 * TG], F32)
        s1 = sbt("s1", [128, TG], F32)
        s2 = sbt("s2", [128, TG], F32)
        s3 = sbt("s3", [128, TG], F32)
        e1, m1, m2 = s1, s2, s3
        u_sb = sbt("u", [128, 2 * TG], BF16)
        um_sb = sbt("um", [128, 2 * TG], BF16)
        zr_sb = sbt("zr", [1, TG], F32)
        zbc = sbt("zbc", [64, TG], F32)
        xres = sbt("xres", [128, 2 * TG], F32)
        xnew = xt

        if ctx is None:
            pb = [es.enter_context(nc.psum_tensor(f"pb{i}", [128, TG],
                                                  F32)) for i in range(8)]
            sems = {}
            for s in SEMNAMES:
                sems[s] = es.enter_context(nc.semaphore(s))

        rec.op("vector", "memset", onescol_f32[:], 1.0)
        rec.op("vector", "memset", epscol[:], 1e-5)
        rec.op("vector", "memset", onesrow_f32[:], 1.0)
        rec.op("vector", "memset", onesrow_bf[:], 1.0)
        rec.op("vector", "memset", ones64_f32[:], 1.0)
        v4 = v_sb[:].rearrange("p (tt kv d) -> p tt kv d", kv=HKV, d=HD + 1)
        rec.op("vector", "memset", v4[:, :, :, HD:HD + 1], 1.0, inc="ev")

        rec.op("sync", "dma_start", out=cos_sb[:], in_=cos_h, inc="ld", inck=16)
        rec.op("sync", "dma_start", out=ss_sb[:], in_=ss_h, inc="ld", inck=16)
        ld_hw = rec.op("sync", "dma_start", out=hwrow[:], in_=hw_h,
                       inc="ld", inck=16)
        bq3 = bq_h.rearrange("(c p o) -> c p o", p=128, o=1)
        bo3 = bo_h.rearrange("(c p o) -> c p o", p=128, o=1)
        for c in range(NCH):
            rec.op("sync", "dma_start", out=bq_sb[:, c:c + 1], in_=bq3[c],
                   inc="ld", inck=16)
            rec.op("sync", "dma_start", out=bo_sb[:, c:c + 1], in_=bo3[c],
                   inc="ld", inck=16)
        bk3 = bkv_h[0].rearrange("(c p o) -> c p o", p=128, o=1)
        for c in range(2):
            rec.op("sync", "dma_start", out=bk_sb[:, c:c + 1], in_=bk3[c],
                   inc="ld", inck=16)
        for i in range(nmasks):
            rec.op("sync", "dma_start", out=msk_sb[:, i * TG:(i + 1) * TG],
                   in_=msk_h[i], inc="ld", inck=16)
        ld_pre = rec.n("ld")

        rec.op("gpsimd", "dma_start",
               out=wk_sb[:].rearrange("p (c n) -> p c n", n=NKV),
               in_=wk_h.rearrange("(c p) n -> p c n", p=128),
               inc="ldg", inck=16)
        rec.op("gpsimd", "dma_start",
               out=wv_sb[:].rearrange("p (c n) -> p c n", n=NKV),
               in_=wv_h.rearrange("(c p) n -> p c n", p=128),
               inc="ldg", inck=16)
        rec.op("gpsimd", "dma_start", out=bvrow[:], in_=bkv_h[1:2, :],
               inc="ldg", inck=16)
        ldg_kv = rec.n("ldg")

        hw_mm = rec.op("tensor", "matmul", pb[4][:, 0:H], onesrow_f32[0:1, :],
                       hwrow[:], start=True, stop=True,
                       waits=[("ld", ld_hw)], inc="mm")
        hwcp = rec.op("vector", "tensor_copy", hw_sb[:], pb[4][:, 0:H],
                      waits=[("mm", hw_mm)], inc="ev")
        for c, (ha, hb) in enumerate(QPAIRS):
            rec.op("vector", "tensor_copy", hwq_sb[0:64, c:c + 1],
                   hw_sb[0:64, ha:ha + 1],
                   waits=[("ev", hwcp)] if c == 0 else ())
            last = (c == NCH - 1)
            rec.op("vector", "tensor_copy", hwq_sb[64:128, c:c + 1],
                   hw_sb[64:128, hb:hb + 1],
                   inc="ev" if last else None)
        ev_hwq = rec.n("ev")
        rec.op("vector", None, waits=[("ev", ev_hwq)])

        emit_ln(rec, dict(xt=xt, sq0=s1, sq1=s2, row=row, t1=s3, pb=pb,
                          onescol_f32=onescol_f32, onesrow_f32=onesrow_f32,
                          epscol=epscol[0:1, 0:1]),
                x_hbm, h_sb, kind="ln")
        ln_done = [("ln_h", rec.n("ln_h"))]

        def rope_emit(ps_ap, bias_col, hw_col, out_ap, tc0, extra_waits):
            w = list(extra_waits)
            if hw_col is None:
                rec.op("vector", "tensor_scalar_add", e1[:], ps_ap, bias_col,
                       waits=w)
            else:
                rec.op("vector", "tensor_scalar", e1[:], ps_ap, bias_col,
                       hw_col, ALU.add, ALU.mult, waits=w)
            rec.op("vector", "tensor_mul", m1[:], e1[:],
                   cos_sb[:, tc0:tc0 + TG])
            for b in range(4):
                pbk = b ^ 1
                rec.op("vector", "tensor_mul", m2[32 * b:32 * b + 32, :],
                       e1[32 * pbk:32 * pbk + 32, :],
                       ss_sb[32 * pbk:32 * pbk + 32, tc0:tc0 + TG])
            return rec.op("vector", "tensor_add", out_ap, m1[:], m2[:],
                          inc="ev")

        rope_ev = {}
        g = 0
        for kc in range(2):
            for tg in range(NTG):
                c0 = tg * TG
                bank = pb[g % 2]
                for ic in range(NCH):
                    w = []
                    if ic == 0:
                        w = [("ldg", 16)] + ln_done
                        if g >= 2:
                            w.append(("ev", rope_ev[g - 2]))
                    rec.op("tensor", "matmul", bank[:, :],
                           wk_sb[:, ic * NKV + kc * 128:
                                 ic * NKV + kc * 128 + 128],
                           h_sb[:, ic * T + c0: ic * T + c0 + TG],
                           start=(ic == 0), stop=(ic == NCH - 1),
                           waits=w, inc="mm")
                mmv = rec.n("mm")
                rope_ev[g] = rope_emit(bank[:, :], bk_sb[:, kc:kc + 1], None,
                                       k_sb[:, kc * T + c0: kc * T + c0 + TG],
                                       c0, [("mm", mmv), ("ld", ld_pre)])
                g += 1
        k_ready = rec.n("ev")

        v_ev = {}
        for tt in range(NTT):
            bank = pb[2 + tt % 2]
            for ic in range(NCH):
                w = []
                if ic == 0:
                    w = [("ldg", 32)] + ln_done
                    if tt >= 2:
                        w.append(("ev", v_ev[tt - 2]))
                rec.op("tensor", "matmul", bank[:, 0:NKV],
                       h_sb[:, ic * T + tt * 128: ic * T + tt * 128 + 128],
                       wv_sb[:, ic * NKV:(ic + 1) * NKV],
                       start=(ic == 0), stop=False, waits=w, inc="mm")
            rec.op("tensor", "matmul", bank[:, 0:NKV], onesrow_bf[0:1, :],
                   bvrow[:], start=False, stop=True,
                   waits=[("ldg", ldg_kv)], inc="mm")
            mmv = rec.n("mm")
            v_ev[tt] = rec.op("vector", "tensor_copy", v4[:, tt, :, 0:HD],
                              bank[:, 0:NKV].rearrange("p (kv d) -> p kv d",
                                                       d=HD),
                              waits=[("mm", mmv)], inc="ev")
        v_ready = rec.n("ev")

        wq4 = wq_h.rearrange("(ic p) o -> p ic o", p=128)
        wo4 = wo_h.rearrange("(ic p) o -> p ic o", p=128)
        wqb4 = wqb[:].rearrange("p (b ic o) -> p b ic o", b=2, o=128)
        wob4 = wob[:].rearrange("p (b ic o) -> p b ic o", b=2, o=128)

        qrope_ev = {}
        wq_mm = {}
        wo_mm = {}
        xres_ev = {}
        wo_store = {}
        wo_mm_last = 0
        epi_ev_last = 0

        for qg in range(NTG):
            qc0 = qg * TG
            for oc in range(NCH):
                gi = qg * NCH + oc
                buf = gi % 2
                w = [("mm", wq_mm[gi - 2])] if gi >= 2 else []
                ldv = rec.op("gpsimd", "dma_start", out=wqb4[:, buf],
                             in_=wq4[:, :, oc * 128:(oc + 1) * 128],
                             waits=w, inc="ldg", inck=16)
                bank = pb[4 + oc % 2]
                for ic in range(NCH):
                    w = []
                    if ic == 0:
                        w = [("ldg", ldv)] + ln_done + [("ev", ev_hwq)]
                        if gi >= 2:
                            w.append(("ev", qrope_ev[gi - 2]))
                    rec.op("tensor", "matmul", bank[:, :],
                           wqb4[:, buf, ic, :],
                           h_sb[:, ic * T + qc0: ic * T + qc0 + TG],
                           start=(ic == 0), stop=(ic == NCH - 1),
                           waits=w, inc="mm")
                wq_mm[gi] = rec.n("mm")
                qrope_ev[gi] = rope_emit(
                    bank[:, :], bq_sb[:, oc:oc + 1], hwq_sb[:, oc:oc + 1],
                    q_sb[:, oc * TG:(oc + 1) * TG], qc0,
                    [("mm", wq_mm[gi]), ("ld", ld_pre)])
            q_ready = rec.n("ev")

            kts = [(kt, tilemap[(qg, kt)]) for kt in range(NTT)
                   if tilemap[(qg, kt)] != "skip"]
            n_t = len(kts)
            for h16 in range(H):
                kv = h16 // 4
                kchunk = kv // 2
                krow = (kv % 2) * 64
                qchunk, qrow = HEADPOS[h16]
                act_head_start = rec.n("act")
                exp_cnt = {}
                av_rhs = {}

                def emit_su(i):
                    kt, mi = kts[i]
                    sbank = pb[i % 2]
                    w = [("ev", max(q_ready, k_ready, v_ready))]
                    if i >= 2:
                        w.append(("act", exp_cnt[i - 2]))
                    else:
                        w.append(("act", act_head_start))
                    if i == 0 and epi_ev_last:
                        w.append(("ev", epi_ev_last))
                    rec.op("tensor", "matmul", sbank[:, :],
                           k_sb[krow:krow + 64,
                                kchunk * T + kt * 128:
                                kchunk * T + kt * 128 + 128],
                           q_sb[qrow:qrow + 64,
                                qchunk * TG:(qchunk + 1) * TG],
                           start=True, stop=True, waits=w, inc="mm")
                    smm = rec.n("mm")
                    ub = u_sb[:, (i % 2) * TG:(i % 2 + 1) * TG]
                    exp_cnt[i] = rec.op("scalar", "activation", ub,
                                        sbank[:, :], AF.Exp, scale=0.125,
                                        waits=[("mm", smm)], inc="act")
                    if mi != "full":
                        umb = um_sb[:, (i % 2) * TG:(i % 2 + 1) * TG]
                        mev = rec.op("vector", "tensor_mul", umb, ub,
                                     msk_sb[:, mi * TG:(mi + 1) * TG],
                                     waits=[("act", exp_cnt[i]),
                                            ("ld", ld_pre)], inc="ev")
                        av_rhs[i] = ("ev", mev, umb)
                    else:
                        av_rhs[i] = ("act", exp_cnt[i], ub)

                emit_su(0)
                if n_t > 1:
                    emit_su(1)
                for i, (kt, mi) in enumerate(kts):
                    sem, val, rhs = av_rhs[i]
                    rec.op("tensor", "matmul", pb[2][0:HD + 1, :],
                           v_sb[:, tt_off(kt) + kv * (HD + 1):
                                tt_off(kt) + kv * (HD + 1) + HD + 1],
                           rhs, start=(i == 0), stop=(i == n_t - 1),
                           waits=[(sem, val)], inc="mm")
                    if i + 2 < n_t:
                        emit_su(i + 2)
                av_mm = rec.n("mm")
                rec.op("vector", "reciprocal", zr_sb[:, :],
                       pb[2][HD:HD + 1, :], waits=[("mm", av_mm)], inc="ev")
                flv = rec.op("vector", "tensor_copy", m1[:, 0:256],
                             m2[:, 0:256], inc="ev")
                zmm = rec.op("tensor", "matmul", pb[3][0:64, :],
                             ones64_f32[0:1, :], zr_sb[:, :],
                             start=True, stop=True,
                             waits=[("ev", flv)], inc="mm")
                rec.op("vector", "tensor_copy", zbc[:, :], pb[3][0:64, :],
                       waits=[("mm", zmm)])
                w = [("mm", wo_mm_last)] if wo_mm_last else []
                epi_ev_last = rec.op(
                    "vector", "tensor_mul",
                    a_sb[qrow:qrow + 64, qchunk * TG:(qchunk + 1) * TG],
                    pb[2][0:64, :], zbc[:, :], waits=w, inc="ev")
            a_ready = rec.n("ev")

            for oc in range(NCH):
                gi = qg * NCH + oc
                buf = gi % 2
                w = [("mm", wo_mm[gi - 2])] if gi >= 2 else []
                ldv = rec.op("gpsimd", "dma_start", out=wob4[:, buf],
                             in_=wo4[:, :, oc * 128:(oc + 1) * 128],
                             waits=w, inc="ldg", inck=16)
                wx = [("ev", xres_ev[gi - 2])] if gi >= 2 else []
                ldx = rec.op("sync", "dma_start",
                             out=xres[:, buf * TG:(buf + 1) * TG],
                             in_=x_hbm[oc * 128:(oc + 1) * 128,
                                       qc0:qc0 + TG],
                             waits=wx, inc="ld", inck=16)
                bank = pb[6 + oc % 2]
                for ic in range(NCH):
                    w3 = []
                    if ic == 0:
                        w3 = [("ldg", ldv), ("ev", a_ready)]
                        if gi >= 2:
                            w3.append(("ev", xres_ev[gi - 2]))
                    rec.op("tensor", "matmul", bank[:, :],
                           wob4[:, buf, ic, :],
                           a_sb[:, ic * TG:(ic + 1) * TG],
                           start=(ic == 0), stop=(ic == NCH - 1),
                           waits=w3, inc="mm")
                wo_mm[gi] = rec.n("mm")
                wo_mm_last = wo_mm[gi]
                wst = [("st", wo_store[gi - 2])] if gi >= 2 else []
                xres_ev[gi] = rec.op(
                    "vector", "scalar_tensor_tensor",
                    xnew[:, buf * TG:(buf + 1) * TG], bank[:, :],
                    bo_sb[:, oc:oc + 1],
                    xres[:, buf * TG:(buf + 1) * TG], ALU.add, ALU.add,
                    waits=[("mm", wo_mm[gi]), ("ld", ldx)] + wst, inc="ev")
                wo_store[gi] = rec.op(
                    "gpsimd", "dma_start",
                    out=xo_hbm[oc * 128:(oc + 1) * 128, qc0:qc0 + TG],
                    in_=xnew[:, buf * TG:(buf + 1) * TG],
                    waits=[("ev", xres_ev[gi])], inc="st", inck=16)

        rec.op("gpsimd", None, waits=[("st", rec.n("st"))])
        if ctx is None:
            rec.replay(nc, sems)
        else:
            fence(rec, sems)
    return nc



def build_l0_ffn(debug=False, ctx=None):
    if ctx is None:
        nc = bass.Bass("TRN2", target_bir_lowering=False)
        rec = Rec()
        x_hbm = nc.dram_tensor("x_fm", [D, T], F32,
                               kind="ExternalInput").ap()
        xo_hbm = nc.dram_tensor("x_out", [D, T], F32,
                                kind="ExternalOutput").ap()
        pre = ""
    else:
        nc, rec, sems, pb, x_hbm, xo_hbm, pre = ctx
    scratch_kind = "ExternalOutput" if debug else "Internal"
    E, F = 8, 4096
    NFC = F // 128

    wr_h = nc.dram_tensor(pre + "wr", [D, E], F32, kind="ExternalInput").ap()
    w1_h = nc.dram_tensor(pre + "w1", [E, D, F], F32,
                          kind="ExternalInput").ap()
    b1_h = nc.dram_tensor(pre + "b1", [E, F], F32, kind="ExternalInput").ap()
    w2_h = nc.dram_tensor(pre + "w2", [E, F, D], F32,
                          kind="ExternalInput").ap()
    b2_h = nc.dram_tensor(pre + "b2", [E, D], F32, kind="ExternalInput").ap()
    id_h = nc.dram_tensor(pre + "idm", [128, 128], F32,
                          kind="ExternalInput").ap()
    h_hbm = nc.dram_tensor(pre + "h_scr", [D, T], BF16,
                           kind=scratch_kind).ap()
    wt_hbm = nc.dram_tensor(pre + "wt_scr", [E, T], F32,
                            kind=scratch_kind).ap()
    xa_hbm = nc.dram_tensor(pre + "xa_scr", [D, T], F32,
                            kind=scratch_kind).ap()
    if debug:
        dbg_h = nc.dram_tensor("dbg", [128, 4 * E + 4], F32,
                               kind="ExternalOutput").ap()

    es = ExitStack()
    sbt = lambda name, shape, dt: es.enter_context(
        nc.sbuf_tensor(pre + name, shape, dt))
    with es:
        w1sb = sbt("w1sb", [128, NCH * F], BF16)
        w2sb = sbt("w2sb", [128, NFC * D], BF16)
        h1_sb = sbt("h1", [128, NFC * TG], BF16)
        htile = sbt("htile", [128, 2 * NCH * TG], BF16)
        hstage = sbt("hstage", [128, 2 * TG], BF16)
        hrt = htile
        wr_sb = sbt("wrsb", [128, NCH * E], BF16)
        id_sb = sbt("idsb", [128, 128], F32)
        wtt = sbt("wtt", [8, 128], F32)
        lg_sb = sbt("lg", [128, E], F32)
        ex_sb = sbt("ex", [128, E], F32)
        wts_t = sbt("wtst", [128, E], F32)
        rmax = sbt("rmax", [128, 1], F32)
        nmax = sbt("nmax", [128, 1], F32)
        sume = sbt("sume", [128, 1], F32)
        rsum = sbt("rsum", [128, 1], F32)
        b1c = sbt("b1c", [128, NFC], F32)
        b2c = sbt("b2c", [128, NCH], F32)
        xout = sbt("xout", [128, 2 * TG], F32)
        xt = sbt("xt", [128, 2 * TG], F32)
        row = sbt("rowbuf", [1, 4 * TG], F32)
        s1 = sbt("s1", [128, TG], F32)
        s2 = sbt("s2", [128, TG], F32)
        s3 = sbt("s3", [128, TG], F32)
        wrow = row[0:1, 0:TG]
        wbc = s2[:, :]
        tf32 = s1[:, :]
        xres = xt
        onescol_f32 = sbt("ones_c", [128, 1], F32)
        onesrow_f32 = sbt("ones_r", [1, 128], F32)
        epscol = sbt("epscol", [128, 1], F32)

        if ctx is None:
            pb = [es.enter_context(nc.psum_tensor(f"pb{i}", [128, TG],
                                                  F32)) for i in range(8)]
            sems = {}
            for s in SEMNAMES:
                sems[s] = es.enter_context(nc.semaphore(s))

        rec.op("vector", "memset", onescol_f32[:], 1.0)
        rec.op("vector", "memset", epscol[:], 1e-5)
        rec.op("vector", "memset", onesrow_f32[:], 1.0)
        rec.op("sync", "dma_start", out=id_sb[:], in_=id_h, inc="ld", inck=16)
        ld_pre = rec.n("ld")
        rec.op("gpsimd", "dma_start",
               out=wr_sb[:].rearrange("p (c e) -> p c e", e=E),
               in_=wr_h.rearrange("(c p) e -> p c e", p=128),
               inc="ldg", inck=16)
        ldg_wr = rec.n("ldg")

        lnB = dict(xt=xt, sq0=s1, sq1=s2, row=row, t1=s3, pb=pb,
                   onescol_f32=onescol_f32, onesrow_f32=onesrow_f32,
                   epscol=epscol[0:1, 0:1], h_store={})
        emit_ln(rec, lnB, x_hbm, hstage, kind="ln", h_hbm=h_hbm)
        h_stores_done = rec.n("st")

        hrt4 = hrt[:, 0:2 * NCH * 128].rearrange("p (b c n) -> p b c n",
                                                 b=2, n=128)
        rt_mm = {}
        wt_store = {}
        for tt in range(NTT):
            buf = tt % 2
            w = [("st", h_stores_done)]
            if tt >= 2:
                w.append(("mm", rt_mm[tt - 2]))
            ldv = rec.op("sync", "dma_start", out=hrt4[:, buf],
                         in_=h_hbm.rearrange("(c p) t -> p c t", p=128)
                         [:, :, tt * 128:(tt + 1) * 128],
                         waits=w, inc="ld", inck=16)
            bank = pb[4]
            for ic in range(NCH):
                w2_ = [("ld", ldv), ("ldg", ldg_wr)]
                if ic == 0 and tt > 0:
                    w2_.append(("ev", rec.n("ev")))
                rec.op("tensor", "matmul", bank[:, 0:E],
                       hrt4[:, buf, ic, :], wr_sb[:, ic * E:(ic + 1) * E],
                       start=(ic == 0), stop=(ic == NCH - 1),
                       waits=w2_, inc="mm")
            rt_mm[tt] = rec.n("mm")
            cpv = rec.op("vector", "tensor_copy", lg_sb[:], bank[:, 0:E],
                         waits=[("mm", rt_mm[tt])], inc="ev")
            rxv = rec.op("vector", "tensor_reduce", rmax[:, 0:1],
                         bank[:, 0:E], mybir.AxisListType.X, ALU.max,
                         inc="ev")
            nmv = rec.op("vector", "tensor_scalar_mul", nmax[:, 0:1],
                         rmax[:, 0:1], -1.0, waits=[("ev", rxv)], inc="ev")
            nmv = rec.op("vector", "tensor_copy", s3[:, 0:256],
                         s3[:, 256:512], inc="ev")
            sv = rec.op("scalar", "activation", ex_sb[:], lg_sb[:], AF.Exp,
                        bias=nmax[:, 0:1],
                        waits=[("ev", nmv)], inc="act")
            smv = rec.op("vector", "tensor_reduce", sume[:, 0:1], ex_sb[:],
                         mybir.AxisListType.X, ALU.add, waits=[("act", sv)],
                         inc="ev")
            rsv = rec.op("vector", "reciprocal", rsum[:, 0:1], sume[:, 0:1],
                         waits=[("ev", smv)], inc="ev")
            wev = rec.op("vector", "tensor_scalar", wts_t[:], lg_sb[:],
                         rmax[:, 0:1], rsum[:, 0:1], ALU.is_equal, ALU.mult,
                         waits=[("ev", rsv)], inc="ev")
            wev = rec.op("vector", "tensor_copy", s3[:, 0:256],
                         xout[:, 0:256], inc="ev")
            wcevp = [("ev", cev_prev)] if tt > 0 else []
            tmm = rec.op("tensor", "matmul", pb[5][0:8, :128], wts_t[:],
                         id_sb[:], start=True, stop=True,
                         waits=[("ev", wev), ("ld", ld_pre)] + wcevp,
                         inc="mm")
            wstp = [("st", wt_store[tt - 1])] if tt > 0 else []
            cev = rec.op("vector", "tensor_copy", wtt[:], pb[5][0:8, :128],
                         waits=[("mm", tmm)] + wstp, inc="ev")
            cev_prev = cev
            wt_store[tt] = rec.op("gpsimd", "dma_start",
                                  out=wt_hbm[:, tt * 128:(tt + 1) * 128],
                                  in_=wtt[:], waits=[("ev", cev)],
                                  inc="st", inck=16)
            if debug and tt == 0:
                for j, buf_ in enumerate((lg_sb, ex_sb, wts_t)):
                    rec.op("gpsimd", "dma_start",
                           out=dbg_h[:, j * E:(j + 1) * E], in_=buf_[:],
                           waits=[("ev", cev)], inc="st", inck=16)
                rec.op("gpsimd", "dma_start", out=dbg_h[:, 3 * E:3 * E + 1],
                       in_=rmax[:, 0:1], inc="st", inck=16,
                       allow_slow_non_contiguous=True)
                rec.op("gpsimd", "dma_start",
                       out=dbg_h[:, 3 * E + 1:3 * E + 2],
                       in_=sume[:, 0:1], inc="st", inck=16,
                       allow_slow_non_contiguous=True)
                rec.op("gpsimd", "dma_start",
                       out=dbg_h[:, 3 * E + 2:3 * E + 3],
                       in_=rsum[:, 0:1], inc="st", inck=16,
                       allow_slow_non_contiguous=True)
                rec.op("vector", None, waits=[("st", rec.n("st"))])
        wt_done = rec.n("st")
        rt_done = rec.n("mm")

        htile4 = htile[:].rearrange("p (b c n) -> p b c n", b=2, n=TG)
        b13 = b1_h.rearrange("e (fc p o) -> e fc p o", p=128, o=1)
        b23 = b2_h.rearrange("e (c p o) -> e c p o", p=128, o=1)
        exp_mm_done = rt_done
        acc_w = {}
        xres_ev = {}
        out_st = {}
        ht_ld_guard = {}
        prev_relu_last = 0
        last_tg_ev = 0
        h1_last_mm = 0
        for e in range(E):
            wld1 = [("mm", h1_last_mm)] if e > 0 else []
            for ic in range(NCH):
                rec.op("gpsimd", "dma_start",
                       out=w1sb[:, ic * F:(ic + 1) * F],
                       in_=w1_h[e, ic * 128:(ic + 1) * 128, :],
                       waits=wld1 if ic == 0 else (), inc="ldg", inck=16)
            w1load = rec.n("ldg")
            wld = [("mm", exp_mm_done)]
            for fc in range(NFC):
                rec.op("gpsimd", "dma_start",
                       out=w2sb[:, fc * D:(fc + 1) * D],
                       in_=w2_h[e, fc * 128:(fc + 1) * 128, :],
                       waits=wld if fc == 0 else (), inc="ldg", inck=16)
            wload = rec.n("ldg")
            bias_guard = list(wld)
            if e > 0:
                bias_guard.append(("act", prev_relu_last))
                bias_guard.append(("ev", rec.n("ev")))
            for fc in range(NFC):
                rec.op("sync", "dma_start", out=b1c[:, fc:fc + 1],
                       in_=b13[e, fc], waits=bias_guard if fc == 0 else (),
                       inc="ld", inck=16)
            for c in range(NCH):
                rec.op("sync", "dma_start", out=b2c[:, c:c + 1],
                       in_=b23[e, c], inc="ld", inck=16)
            bias_ld = rec.n("ld")

            for tg in range(NTG):
                ei = e * NTG + tg
                buf = ei % 2
                qc0 = tg * TG
                w = [("st", h_stores_done)]
                if ei >= 2:
                    w.append(("mm", ht_ld_guard[ei - 2]))
                else:
                    w.append(("mm", rt_done))
                ldh = rec.op("sync", "dma_start", out=htile4[:, buf],
                             in_=h_hbm.rearrange("(c p) t -> p c t", p=128)
                             [:, :, qc0:qc0 + TG],
                             waits=w, inc="ld", inck=16)
                wbw = [("st", wt_done)]
                if last_tg_ev:
                    wbw.append(("ev", last_tg_ev))
                ldwbc = rec.op("sync", "dma_start", out=wbc,
                               in_=wt_hbm[e:e + 1, qc0:qc0 + TG]
                               .broadcast_to([128, TG]),
                               waits=wbw, inc="ld", inck=16)
                relu_after = {}
                for fc in range(NFC):
                    bank = pb[fc % 2]
                    for ic in range(NCH):
                        w3 = []
                        if ic == 0:
                            w3 = [("ld", max(ldh, bias_ld)),
                                  ("ldg", w1load)]
                            if fc >= 2:
                                w3.append(("act", relu_after[fc - 2]))
                            elif ei > 0:
                                w3.append(("act", prev_relu_last))
                        rec.op("tensor", "matmul", bank[:, :],
                               w1sb[:, ic * F + fc * 128:
                                    ic * F + fc * 128 + 128],
                               htile4[:, buf, ic, :],
                               start=(ic == 0), stop=(ic == NCH - 1),
                               waits=w3, inc="mm")
                    relu_after[fc] = rec.op(
                        "scalar", "activation",
                        h1_sb[:, fc * TG:(fc + 1) * TG], bank[:, :],
                        AF.Relu, bias=b1c[:, fc:fc + 1],
                        waits=[("mm", rec.n("mm"))], inc="act")
                prev_relu_last = relu_after[NFC - 1]
                h1_mm_last = rec.n("mm")
                for oc in range(NCH):
                    gi2 = ei * NCH + oc
                    buf2 = gi2 % 2
                    bank = pb[2 + oc % 2]
                    for fc in range(NFC):
                        w4 = []
                        if fc == 0:
                            w4 = [("act", prev_relu_last),
                                  ("ldg", wload)]
                            if gi2 >= 2:
                                w4.append(("ev", xres_ev[gi2 - 2]))
                        rec.op("tensor", "matmul", bank[:, :],
                               w2sb[:, fc * D + oc * 128:
                                    fc * D + oc * 128 + 128],
                               h1_sb[:, fc * TG:(fc + 1) * TG],
                               start=(fc == 0), stop=(fc == NFC - 1),
                               waits=w4, inc="mm")
                    ymm = rec.n("mm")
                    wx = []
                    if gi2 >= 2:
                        wx.append(("ev", xres_ev[gi2 - 2]))
                    if e > 0:
                        wx.append(("st", acc_w[(oc, tg)]))
                    src = x_hbm if e == 0 else xa_hbm
                    ldx = rec.op("sync", "dma_start",
                                 out=xres[:, buf2 * TG:(buf2 + 1) * TG],
                                 in_=src[oc * 128:(oc + 1) * 128,
                                         qc0:qc0 + TG],
                                 waits=wx, inc="ld", inck=16)
                    rec.op("vector", "scalar_tensor_tensor", tf32,
                           bank[:, :], b2c[:, oc:oc + 1], wbc,
                           ALU.add, ALU.mult,
                           waits=[("mm", ymm), ("ld", max(bias_ld, ldwbc))])
                    wst = []
                    if gi2 >= 2:
                        wst.append(("st", out_st[gi2 - 2]))
                    xres_ev[gi2] = rec.op(
                        "vector", "tensor_add",
                        xout[:, buf2 * TG:(buf2 + 1) * TG], tf32,
                        xres[:, buf2 * TG:(buf2 + 1) * TG],
                        waits=[("ld", ldx)] + wst, inc="ev")
                    dst = xo_hbm if e == E - 1 else xa_hbm
                    out_st[gi2] = rec.op(
                        "gpsimd", "dma_start",
                        out=dst[oc * 128:(oc + 1) * 128, qc0:qc0 + TG],
                        in_=xout[:, buf2 * TG:(buf2 + 1) * TG],
                        waits=[("ev", xres_ev[gi2])], inc="st", inck=16)
                    acc_w[(oc, tg)] = out_st[gi2]
                ht_ld_guard[ei] = h1_mm_last
                last_tg_ev = rec.n("ev")
                h1_last_mm = h1_mm_last
            exp_mm_done = rec.n("mm")

        rec.op("gpsimd", None, waits=[("st", rec.n("st"))])
        if ctx is None:
            rec.replay(nc, sems)
        else:
            fence(rec, sems)
    return nc


def run_l0_ffn(x_fm_list, ins, debug=False, ncores=NCORES):
    nc = build_l0_ffn(debug=debug)
    shared = dict(
        wr=ins["L0_ffn_w_router"], w1=ins["L0_ffn_w1"], b1=ins["L0_ffn_b1"],
        w2=ins["L0_ffn_w2"], b2=ins["L0_ffn_b2"],
        idm=np.eye(128, dtype=np.float32),
    )
    in_maps = []
    for b in range(ncores):
        m = dict(shared)
        m["x_fm"] = x_fm_list[b]
        in_maps.append(m)
    res = run_bass_kernel_spmd(nc, in_maps, core_ids=list(range(ncores)))
    if debug:
        return res.results
    return [r["x_out"] for r in res.results]



def build_l1_attn(ctx=None):
    if ctx is None:
        nc = bass.Bass("TRN2", target_bir_lowering=False)
        rec = Rec()
        x_hbm = nc.dram_tensor("x_fm", [D, T], F32,
                               kind="ExternalInput").ap()
        xo_hbm = nc.dram_tensor("x_out", [D, T], F32,
                                kind="ExternalOutput").ap()
        pre = ""
    else:
        nc, rec, sems, pb, x_hbm, xo_hbm, pre = ctx
    C_GATE = 8.0

    win_h = nc.dram_tensor(pre + "WinT", [D, D], F32,
                           kind="ExternalInput").ap()
    wig_h = nc.dram_tensor(pre + "WigT", [D, D], F32,
                           kind="ExternalInput").ap()
    wrg_h = nc.dram_tensor(pre + "WrgT", [D, D], F32,
                           kind="ExternalInput").ap()
    wout_h = nc.dram_tensor(pre + "WoutT", [D, D], F32,
                            kind="ExternalInput").ap()
    big_h = nc.dram_tensor(pre + "big", [D], F32, kind="ExternalInput").ap()
    brg_h = nc.dram_tensor(pre + "brg", [D], F32, kind="ExternalInput").ap()
    lam_h = nc.dram_tensor(pre + "lam", [D], F32, kind="ExternalInput").ap()
    h0_h = nc.dram_tensor(pre + "h0v", [D], F32, kind="ExternalInput").ap()
    nw_h = nc.dram_tensor(pre + "nw", [D], F32, kind="ExternalInput").ap()

    es = ExitStack()
    sbt = lambda name, shape, dt: es.enter_context(
        nc.sbuf_tensor(pre + name, shape, dt))
    with es:
        h_sb = sbt("h", [128, NCH * T], BF16)
        hs_sb = sbt("hs", [128, NCH * T], BF16)
        a_fm = sbt("afm", [128, T], F32)
        g_fm = sbt("gfm", [128, T], F32)
        wblk = sbt("wblk", [128, 2 * 3 * NCH * 128], BF16)
        woblk = sbt("woblk", [128, 2 * NCH * 128], BF16)
        big_c = sbt("bigc", [128, NCH], F32)
        brg_c = sbt("brgc", [128, NCH], F32)
        lam_c = sbt("lamc", [128, NCH], F32)
        clog_c = sbt("clogc", [128, NCH], F32)
        h0_c = sbt("h0c", [128, NCH], F32)
        nw_c = sbt("nwc", [128, NCH], F32)
        xt = sbt("xt", [128, 2 * TG], F32)
        row = sbt("rowbuf", [1, 4 * TG], F32)
        s1 = sbt("s1", [128, TG], F32)
        s2 = sbt("s2", [128, TG], F32)
        s3 = sbt("s3", [128, TG], F32)
        rbf = sbt("rbf", [128, TG], BF16)
        xres = sbt("xres", [128, 2 * TG], F32)
        onescol_f32 = sbt("ones_c", [128, 1], F32)
        onesrow_f32 = sbt("ones_r", [1, 128], F32)
        epscol = sbt("epscol", [128, 1], F32)
        eps2col = sbt("eps2col", [128, 1], F32)
        xout = xt

        if ctx is None:
            pb = [es.enter_context(nc.psum_tensor(f"pb{i}", [128, TG],
                                                  F32)) for i in range(8)]
            sems = {}
            for s in SEMNAMES:
                sems[s] = es.enter_context(nc.semaphore(s))

        rec.op("vector", "memset", onescol_f32[:], 1.0)
        rec.op("vector", "memset", epscol[:], 1e-5)
        rec.op("vector", "memset", eps2col[:], 1e-6)
        rec.op("vector", "memset", onesrow_f32[:], 1.0)

        big3 = big_h.rearrange("(c p o) -> c p o", p=128, o=1)
        brg3 = brg_h.rearrange("(c p o) -> c p o", p=128, o=1)
        lam3 = lam_h.rearrange("(c p o) -> c p o", p=128, o=1)
        h03 = h0_h.rearrange("(c p o) -> c p o", p=128, o=1)
        nw3 = nw_h.rearrange("(c p o) -> c p o", p=128, o=1)
        for c in range(NCH):
            rec.op("sync", "dma_start", out=big_c[:, c:c + 1], in_=big3[c],
                   inc="ld", inck=16)
            rec.op("sync", "dma_start", out=brg_c[:, c:c + 1], in_=brg3[c],
                   inc="ld", inck=16)
            rec.op("sync", "dma_start", out=lam_c[:, c:c + 1], in_=lam3[c],
                   inc="ld", inck=16)
            rec.op("sync", "dma_start", out=h0_c[:, c:c + 1], in_=h03[c],
                   inc="ld", inck=16)
            rec.op("sync", "dma_start", out=nw_c[:, c:c + 1], in_=nw3[c],
                   inc="ld", inck=16)
        ld_par = rec.n("ld")
        sg = rec.op("scalar", "activation", clog_c[:], lam_c[:], AF.Sigmoid,
                    waits=[("ld", ld_par)], inc="act")
        sg = rec.op("scalar", "activation", s2[:, 0:256], s2[:, 0:256],
                    AF.Square, inc="act")
        ad = rec.op("vector", "tensor_scalar_add", clog_c[:], clog_c[:],
                    1e-8, waits=[("act", sg)], inc="ev")
        ad = rec.op("vector", "tensor_copy", s3[:, 0:256], s3[:, 256:512],
                    inc="ev")
        ln_ = rec.op("scalar", "activation", clog_c[:], clog_c[:], AF.Ln,
                     waits=[("ev", ad)], inc="act")
        ln_ = rec.op("scalar", "activation", s2[:, 0:256], s2[:, 0:256],
                     AF.Square, inc="act")
        cl = rec.op("vector", "tensor_scalar_mul", clog_c[:], clog_c[:],
                    C_GATE, waits=[("act", ln_)], inc="ev")
        cl = rec.op("vector", "tensor_copy", s3[:, 0:256], s3[:, 256:512],
                    inc="ev")
        ev_clog = cl

        emit_ln(rec, dict(xt=xt, sq0=s1, sq1=s2, row=row, t1=s3, pb=pb,
                          onescol_f32=onescol_f32, onesrow_f32=onesrow_f32,
                          epscol=epscol[0:1, 0:1]),
                x_hbm, h_sb, kind="ln")
        ln_done = [("ln_h", rec.n("ln_h"))]

        win4 = win_h.rearrange("(ic p) o -> p ic o", p=128)
        wig4 = wig_h.rearrange("(ic p) o -> p ic o", p=128)
        wrg4 = wrg_h.rearrange("(ic p) o -> p ic o", p=128)
        wout4 = wout_h.rearrange("(ic p) o -> p ic o", p=128)
        wblk6 = wblk[:].rearrange("p (b w ic o) -> p b w ic o", b=2, w=3,
                                  o=128)
        woblk4 = woblk[:].rearrange("p (b ic o) -> p b ic o", b=2, o=128)

        scan_ev = {}
        wld_mm = {}
        prev_oc_gate_ev = 0
        for oc in range(NCH):
            buf = oc % 2
            w = [("mm", wld_mm[oc - 2])] if oc >= 2 else []
            ldv = rec.op("gpsimd", "dma_start", out=wblk6[:, buf, 0],
                         in_=win4[:, :, oc * 128:(oc + 1) * 128],
                         waits=w, inc="ldg", inck=16)
            ldv = rec.op("gpsimd", "dma_start", out=wblk6[:, buf, 1],
                         in_=wig4[:, :, oc * 128:(oc + 1) * 128],
                         inc="ldg", inck=16)
            ldv = rec.op("gpsimd", "dma_start", out=wblk6[:, buf, 2],
                         in_=wrg4[:, :, oc * 128:(oc + 1) * 128],
                         inc="ldg", inck=16)
            gate_ev_prev = 0
            gate_ev_tg = {}
            for tg in range(NTG):
                c0 = tg * TG
                pxp = pb[tg % 2]
                pi = pb[2 + tg % 2]
                pr = pb[4 + tg % 2]
                for w_i, bank in ((0, pxp), (1, pi), (2, pr)):
                    for ic in range(NCH):
                        w3 = []
                        if ic == 0:
                            w3 = [("ldg", ldv)] + ln_done
                            if tg >= 2:
                                w3.append(("ev", gate_ev_tg[tg - 2]))
                            elif prev_oc_gate_ev:
                                w3.append(("ev", prev_oc_gate_ev))
                        rec.op("tensor", "matmul", bank[:, :],
                               wblk6[:, buf, w_i, ic, :],
                               h_sb[:, ic * T + c0: ic * T + c0 + TG],
                               start=(ic == 0), stop=(ic == NCH - 1),
                               waits=w3, inc="mm")
                mmv = rec.n("mm")
                wg_ = [("mm", mmv), ("ev", ev_clog)]
                if tg == 0 and oc >= 1:
                    wg_.append(("ev", scan_ev[oc - 1]))
                rv_ = rec.op("scalar", "activation", s1[:, :], pr[:, :],
                             AF.Sigmoid, bias=brg_c[:, oc:oc + 1],
                             waits=wg_, inc="act")
                mv = rec.op("vector", "tensor_scalar_mul", s2[:, :],
                            s1[:, :], clog_c[:, oc:oc + 1],
                            waits=[("act", rv_)], inc="ev")
                av_ = rec.op("scalar", "activation",
                             a_fm[:, c0:c0 + TG], s2[:, :], AF.Exp,
                             waits=[("ev", mv)], inc="act")
                iv_ = rec.op("scalar", "activation", s3[:, :], pi[:, :],
                             AF.Sigmoid, bias=big_c[:, oc:oc + 1],
                             waits=[("mm", mmv)], inc="act")
                t1v = rec.op("vector", "tensor_mul", s1[:, :],
                             a_fm[:, c0:c0 + TG], a_fm[:, c0:c0 + TG],
                             waits=[("act", av_)])
                rec.op("vector", "tensor_scalar", s1[:, :], s1[:, :],
                       -1.0, 1.000001, ALU.mult, ALU.add, inc="ev")
                sq_ = rec.op("scalar", "activation", s1[:, :], s1[:, :],
                             AF.Sqrt, waits=[("ev", rec.n("ev"))],
                             inc="act")
                rec.op("vector", "tensor_mul", s2[:, :], s3[:, :],
                       pxp[:, :], waits=[("act", sq_)])
                gate_ev_prev = rec.op("vector", "tensor_mul",
                                      g_fm[:, c0:c0 + TG], s1[:, :],
                                      s2[:, :], inc="ev")
                gate_ev_tg[tg] = gate_ev_prev
            wld_mm[oc] = rec.n("mm")
            prev_oc_gate_ev = gate_ev_prev
            scan_ev[oc] = rec.op("vector", "tensor_tensor_scan",
                                 hs_sb[:, oc * T:(oc + 1) * T],
                                 a_fm[:, :], g_fm[:, :],
                                 h0_c[:, oc:oc + 1], ALU.mult, ALU.add,
                                 waits=[("ev", gate_ev_prev),
                                        ("ld", ld_par)], inc="ev")
        scan_all = rec.n("ev")

        rms_mm = {}
        prev_rows = 0
        hn_after = {}
        for tg in range(NTG):
            c0 = tg * TG
            for c in range(NCH):
                w = [("ev", scan_all)]
                if (tg, c - 2) in rms_mm:
                    w.append(("ln_mm", rms_mm[(tg, c - 2)]))
                elif tg > 0 and (tg - 1, NCH - 2 + c) in rms_mm:
                    w.append(("ln_mm", rms_mm[(tg - 1, NCH - 2 + c)]))
                sqv = rec.op("scalar", "activation",
                             s1[:, :] if c % 2 == 0 else s2[:, :],
                             hs_sb[:, c * T + c0: c * T + c0 + TG],
                             AF.Square, waits=w, inc="ln_sq")
                w0 = [("ln_sq", sqv)]
                if c == 0 and prev_rows:
                    w0.append(("ln_ev", prev_rows))
                rec.op("tensor", "matmul", pb[1][0:1, :],
                       onescol_f32[:, 0:1],
                       s1[:, :] if c % 2 == 0 else s2[:, :],
                       start=(c == 0), stop=(c == NCH - 1), waits=w0,
                       inc="ln_mm")
                rms_mm[(tg, c)] = rec.n("ln_mm")
            mm_stats = rec.n("ln_mm")
            vrow = row[0:1, 2 * TG:3 * TG]
            rrow = row[0:1, 3 * TG:4 * TG]
            rec.op("vector", "tensor_scalar_mul", vrow, pb[1][0:1, :],
                   1.0 / D, waits=[("ln_mm", mm_stats)], inc="ln_ev")
            sv = rec.op("scalar", "activation", vrow, vrow, AF.Sqrt,
                        bias=eps2col[0:1, 0:1],
                        waits=[("ln_ev", rec.n("ln_ev"))], inc="ln_sq")
            rvv = rec.op("vector", "reciprocal", rrow, vrow,
                         waits=[("ln_sq", sv)], inc="ln_ev")
            rvv = rec.op("vector", "tensor_copy", s3[:, 0:256],
                         s3[:, 256:512], inc="ln_ev")
            prev_rows = rvv
            wbc_ = [("ln_ev", rvv)]
            if tg > 0:
                wbc_.append(("ln_h", rec.n("ln_h")))
            rec.op("tensor", "matmul", pb[3][:, :], onesrow_f32[0:1, :],
                   rrow, start=True, stop=True, waits=wbc_, inc="ln_mm")
            mm_bc = rec.n("ln_mm")
            rec.op("vector", "tensor_copy", rbf[:, :], pb[3][:, :],
                   waits=[("ln_mm", mm_bc)], inc="ln_ev")
            for c in range(NCH):
                hn_after[(tg, c)] = rec.op(
                    "vector", "tensor_mul",
                    h_sb[:, c * T + c0: c * T + c0 + TG],
                    hs_sb[:, c * T + c0: c * T + c0 + TG], rbf[:, :],
                    inc="ln_h")
        hn_done = rec.n("ln_h")

        wo_mm = {}
        xres_ev = {}
        wo_store = {}
        woblk_ev = {}
        for qg in range(NTG):
            qc0 = qg * TG
            for oc in range(NCH):
                gi = qg * NCH + oc
                bufw = gi % 2
                w = [("mm", wo_mm[gi - 2])] if gi >= 2 else []
                ldv = rec.op("gpsimd", "dma_start", out=woblk4[:, bufw],
                             in_=wout4[:, :, oc * 128:(oc + 1) * 128],
                             waits=w, inc="ldg", inck=16)
                for ic in range(NCH):
                    wv = [("ldg", ldv), ("ld", ld_par)] if ic == 0 else ()
                    woblk_ev[gi] = rec.op(
                        "vector", "tensor_scalar_mul",
                        woblk4[:, bufw, ic, :], woblk4[:, bufw, ic, :],
                        nw_c[:, ic:ic + 1], waits=wv, inc="ev")
                wx = [("ev", xres_ev[gi - 2])] if gi >= 2 else []
                ldx = rec.op("sync", "dma_start",
                             out=xres[:, bufw * TG:(bufw + 1) * TG],
                             in_=x_hbm[oc * 128:(oc + 1) * 128,
                                       qc0:qc0 + TG],
                             waits=wx, inc="ld", inck=16)
                bank = pb[6 + oc % 2]
                for ic in range(NCH):
                    w3 = []
                    if ic == 0:
                        w3 = [("ev", woblk_ev[gi]),
                              ("ln_h", hn_done)]
                        if gi >= 2:
                            w3.append(("ev", xres_ev[gi - 2]))
                    rec.op("tensor", "matmul", bank[:, :],
                           woblk4[:, bufw, ic, :],
                           h_sb[:, ic * T + qc0: ic * T + qc0 + TG],
                           start=(ic == 0), stop=(ic == NCH - 1),
                           waits=w3, inc="mm")
                wo_mm[gi] = rec.n("mm")
                wst = [("st", wo_store[gi - 2])] if gi >= 2 else []
                xres_ev[gi] = rec.op(
                    "vector", "tensor_add",
                    xout[:, bufw * TG:(bufw + 1) * TG], bank[:, :],
                    xres[:, bufw * TG:(bufw + 1) * TG],
                    waits=[("mm", wo_mm[gi]), ("ld", ldx)] + wst,
                    inc="ev")
                wo_store[gi] = rec.op(
                    "gpsimd", "dma_start",
                    out=xo_hbm[oc * 128:(oc + 1) * 128, qc0:qc0 + TG],
                    in_=xout[:, bufw * TG:(bufw + 1) * TG],
                    waits=[("ev", xres_ev[gi])], inc="st", inck=16)

        rec.op("gpsimd", None, waits=[("st", rec.n("st"))])
        if ctx is None:
            rec.replay(nc, sems)
        else:
            fence(rec, sems)
    return nc


def run_l1_attn(x_fm_list, ins):
    nc = build_l1_attn()
    shared = dict(
        WinT=np.ascontiguousarray(ins["L1_attn_W_in"].T),
        WigT=np.ascontiguousarray(ins["L1_attn_W_ig"].T),
        WrgT=np.ascontiguousarray(ins["L1_attn_W_rg"].T),
        WoutT=np.ascontiguousarray(ins["L1_attn_W_out"].T),
        big=ins["L1_attn_b_ig"], brg=ins["L1_attn_b_rg"],
        lam=ins["L1_attn_lam"],
        h0v=np.ascontiguousarray(ins["L1_attn_h0"].reshape(-1)),
        nw=ins["L1_attn_norm_w"],
    )
    in_maps = []
    for b in range(NCORES):
        m = dict(shared)
        m["x_fm"] = x_fm_list[b]
        in_maps.append(m)
    res = run_bass_kernel_spmd(nc, in_maps, core_ids=list(range(NCORES)))
    return [r["x_out"] for r in res.results]



def build_l1_ffn(debug=False, ctx=None):
    if ctx is None:
        nc = bass.Bass("TRN2", target_bir_lowering=False)
        rec = Rec()
        x_hbm = nc.dram_tensor("x_fm", [D, T], F32,
                               kind="ExternalInput").ap()
        xo_hbm = nc.dram_tensor("x_out", [D, T], F32,
                                kind="ExternalOutput").ap()
        pre = ""
    else:
        nc, rec, sems, pb, x_hbm, xo_hbm, pre = ctx
    sck = "ExternalOutput" if debug else "Internal"
    H16, DH, DC, HID, E = 16, 64, 256, 2048, 8
    NHC = HID // 128

    wq_h = nc.dram_tensor(pre + "WqP", [D, D], F32,
                          kind="ExternalInput").ap()
    wkc_h = nc.dram_tensor(pre + "Wkc", [D, DC], F32,
                           kind="ExternalInput").ap()
    wvc_h = nc.dram_tensor(pre + "Wvc", [D, DC], F32,
                           kind="ExternalInput").ap()
    wk_h = nc.dram_tensor(pre + "WkP", [DC, D], F32,
                          kind="ExternalInput").ap()
    wv_h = nc.dram_tensor(pre + "Wv", [DC, D], F32,
                          kind="ExternalInput").ap()
    wo_h = nc.dram_tensor(pre + "Wo", [D, D], F32,
                          kind="ExternalInput").ap()
    wsi_h = nc.dram_tensor(pre + "Wsi", [D, 2 * HID], F32,
                           kind="ExternalInput").ap()
    wso_h = nc.dram_tensor(pre + "Wso", [HID, D], F32,
                           kind="ExternalInput").ap()
    w1_h = nc.dram_tensor(pre + "W1s", [D, 2 * HID], F32,
                          kind="ExternalInput").ap()
    w2_h = nc.dram_tensor(pre + "W2e", [E, HID, D], F32,
                          kind="ExternalInput").ap()
    wg_h = nc.dram_tensor(pre + "Wg", [D, 2], F32,
                          kind="ExternalInput").ap()
    we_h = nc.dram_tensor(pre + "We", [D, E], F32,
                          kind="ExternalInput").ap()
    gb_h = nc.dram_tensor(pre + "gb", [1, 2], F32,
                          kind="ExternalInput").ap()
    eb_h = nc.dram_tensor(pre + "eb", [1, E], F32,
                          kind="ExternalInput").ap()
    n1w_h = nc.dram_tensor(pre + "n1w", [D], F32,
                           kind="ExternalInput").ap()
    n2w_h = nc.dram_tensor(pre + "n2w", [D], F32,
                           kind="ExternalInput").ap()
    cos_h = nc.dram_tensor(pre + "cos2", [128, T], F32,
                           kind="ExternalInput").ap()
    ss_h = nc.dram_tensor(pre + "ss2", [128, T], F32,
                          kind="ExternalInput").ap()
    id_h = nc.dram_tensor(pre + "idm", [128, 128], F32,
                          kind="ExternalInput").ap()

    src_hbm = nc.dram_tensor(pre + "src_scr", [D, T], BF16, kind=sck).ap()
    a_hbm = nc.dram_tensor(pre + "a_scr", [D, T], BF16, kind=sck).ap()
    xx_hbm = nc.dram_tensor(pre + "xx_scr", [D, T], BF16, kind=sck).ap()
    wt_hbm = nc.dram_tensor(pre + "wt_scr", [E, T], F32, kind=sck).ap()
    if debug:
        dbg_h = nc.dram_tensor("dbg", [128, 64], F32,
                               kind="ExternalOutput").ap()

    src3 = src_hbm.rearrange("(c p) t -> p c t", p=128)
    a3 = a_hbm.rearrange("(c p) t -> p c t", p=128)
    xx3 = xx_hbm.rearrange("(c p) t -> p c t", p=128)

    es = ExitStack()
    sbt = lambda name, shape, dt: es.enter_context(
        nc.sbuf_tensor(pre + name, shape, dt))
    with es:
        xt = sbt("xt", [128, 2 * TG], F32)
        row = sbt("rowbuf", [1, 4 * TG], F32)
        s1 = sbt("s1", [128, TG], F32)
        s2 = sbt("s2", [128, TG], F32)
        s3 = sbt("s3", [128, TG], F32)
        onescol_f32 = sbt("ones_c", [128, 1], F32)
        onesrow_f32 = sbt("ones_r", [1, 128], F32)
        onesrow_bf = sbt("ones_rb", [1, 128], BF16)
        ones64_f32 = sbt("ones64", [1, 64], F32)
        epscol = sbt("epscol", [128, 1], F32)
        eps2col = sbt("eps2col", [128, 1], F32)
        hstage = sbt("hstage", [128, 2 * TG], BF16)

        if ctx is None:
            pb = [es.enter_context(nc.psum_tensor(f"pb{i}", [128, TG],
                                                  F32)) for i in range(8)]
            sems = {}
            for s in SEMNAMES:
                sems[s] = es.enter_context(nc.semaphore(s))

        rec.op("vector", "memset", onescol_f32[:], 1.0)
        rec.op("vector", "memset", epscol[:], 1e-5)
        rec.op("vector", "memset", eps2col[:], 1e-6)
        rec.op("vector", "memset", onesrow_f32[:], 1.0)
        rec.op("vector", "memset", onesrow_bf[:], 1.0)
        rec.op("vector", "memset", ones64_f32[:], 1.0)

        lnB = dict(xt=xt, sq0=s1, sq1=s2, row=row, t1=s3, pb=pb,
                   onescol_f32=onescol_f32, onesrow_f32=onesrow_f32,
                   epscol=epscol[0:1, 0:1], h_store={})
        emit_ln(rec, lnB, x_hbm, hstage, kind="ln", h_hbm=src_hbm)
        src_done = rec.n("st")

        esKV = ExitStack()
        kc_sb = esKV.enter_context(
            nc.sbuf_tensor(pre + "kc", [128, 2 * T], BF16))
        vc_sb = esKV.enter_context(
            nc.sbuf_tensor(pre + "vc", [128, 2 * T], BF16))
        with ExitStack() as esB:
            sbB = lambda n, s_, d: esB.enter_context(nc.sbuf_tensor(pre + n, s_, d))
            srct = sbB("srctB", [128, 2 * NCH * TG], BF16)
            wkc_sb = sbB("wkcsb", [128, NCH * DC], BF16)
            wvc_sb = sbB("wvcsb", [128, NCH * DC], BF16)
            srct4 = srct[:].rearrange("p (b c n) -> p b c n", b=2, n=TG)
            rec.op("gpsimd", "dma_start",
                   out=wkc_sb[:].rearrange("p (c n) -> p c n", n=DC),
                   in_=wkc_h.rearrange("(c p) n -> p c n", p=128),
                   inc="ldg", inck=16)
            rec.op("gpsimd", "dma_start",
                   out=wvc_sb[:].rearrange("p (c n) -> p c n", n=DC),
                   in_=wvc_h.rearrange("(c p) n -> p c n", p=128),
                   inc="ldg", inck=16)
            ldg_w = rec.n("ldg")
            evac_ev = {}
            ld_guard = {}
            for tg in range(NTG):
                c0 = tg * TG
                buf = tg % 2
                w = [("st", src_done)]
                if tg >= 2:
                    w.append(("mm", ld_guard[tg - 2]))
                ldv = rec.op("sync", "dma_start", out=srct4[:, buf],
                             in_=src3[:, :, c0:c0 + TG], waits=w,
                             inc="ld", inck=16)
                for kind_i, (bank0, wsb, dst) in enumerate(
                        ((0, wkc_sb, kc_sb), (2, wvc_sb, vc_sb))):
                    for kchunk in range(2):
                        bank = pb[bank0 + kchunk]
                        for ic in range(NCH):
                            w3 = []
                            if ic == 0:
                                w3 = [("ldg", ldg_w), ("ld", ldv)]
                                if (tg - 2, kind_i, kchunk) in evac_ev:
                                    w3.append(("ev", evac_ev[(tg - 2,
                                                              kind_i,
                                                              kchunk)]))
                            rec.op("tensor", "matmul", bank[:, :],
                                   wsb[:, ic * DC + kchunk * 128:
                                       ic * DC + kchunk * 128 + 128],
                                   srct4[:, buf, ic, :],
                                   start=(ic == 0), stop=(ic == NCH - 1),
                                   waits=w3, inc="mm")
                        evac_ev[(tg, kind_i, kchunk)] = rec.op(
                            "vector", "tensor_copy",
                            dst[:, kchunk * T + c0: kchunk * T + c0 + TG],
                            bank[:, :], waits=[("mm", rec.n("mm"))],
                            inc="ev")
                ld_guard[tg] = rec.n("mm")
            kcvc_ev = rec.n("ev")
        fence(rec, sems)

        for half in range(2):
            hbase = half * 8
            cbase = half * 4
            with ExitStack() as esH:
                sbH = lambda n, s_, d, _h=half: esH.enter_context(
                    nc.sbuf_tensor(f"{pre}{n}_{_h}", s_, d))
                q_sb = sbH("qh", [128, 4 * T], BF16)
                k_sb = sbH("kh", [128, 4 * T], BF16)
                v_sb = sbH("vh", [128, NTT * 8 * (DH + 1)], BF16)
                cos_sb = sbH("cosb", [128, T], BF16)
                ss_sb = sbH("ssb", [128, T], BF16)
                wqb = sbH("wqb", [128, 2 * NCH * 128], BF16)
                wk_sb = sbH("wkh", [128, 2 * 512], BF16)
                wv_sb = sbH("wvh", [128, 2 * 512], BF16)
                srct = sbH("srctH", [128, 2 * NCH * TG], BF16)
                ebf = sbH("ebf", [128, TG], BF16)
                m1b = sbH("m1b", [128, TG], BF16)
                m2b = sbH("m2b", [128, TG], BF16)
                u_sb = sbH("u", [128, 2 * TG], BF16)
                zr_sb = sbH("zr", [1, TG], F32)
                zbc = sbH("zbc", [64, TG], F32)
                a_sb = sbH("ah", [128, 4 * TG], BF16)
                srct4 = srct[:].rearrange("p (b c n) -> p b c n", b=2, n=TG)
                wqb4 = wqb[:].rearrange("p (b ic o) -> p b ic o", b=2,
                                        o=128)
                v4 = v_sb[:].rearrange("p (tt h d) -> p tt h d", h=8,
                                       d=DH + 1)

                rec.op("gpsimd", "dma_start", out=cos_sb[:], in_=cos_h,
                       inc="ldg", inck=16)
                rec.op("gpsimd", "dma_start", out=ss_sb[:], in_=ss_h,
                       inc="ldg", inck=16)
                rec.op("gpsimd", "dma_start",
                       out=wk_sb[:].rearrange("p (c n) -> p c n", n=512),
                       in_=wk_h.rearrange("(c p) n -> p c n", p=128)
                       [:, :, hbase * 64:(hbase + 8) * 64],
                       inc="ldg", inck=16)
                rec.op("gpsimd", "dma_start",
                       out=wv_sb[:].rearrange("p (c n) -> p c n", n=512),
                       in_=wv_h.rearrange("(c p) n -> p c n", p=128)
                       [:, :, hbase * 64:(hbase + 8) * 64],
                       inc="ldg", inck=16)
                ldg_h = rec.n("ldg")
                rec.op("vector", "memset", v4[:, :, :, DH:DH + 1], 1.0,
                       inc="ev")

                def rope_bf(ps_ap, out_ap, tc0, waits_):
                    rec.op("vector", "tensor_copy", ebf[:, :], ps_ap,
                           waits=list(waits_))
                    rec.op("vector", "tensor_mul", m1b[:, :], ebf[:, :],
                           cos_sb[:, tc0:tc0 + TG])
                    for b_ in range(4):
                        pk = b_ ^ 1
                        rec.op("vector", "tensor_mul",
                               m2b[32 * b_:32 * b_ + 32, :],
                               ebf[32 * pk:32 * pk + 32, :],
                               ss_sb[32 * pk:32 * pk + 32, tc0:tc0 + TG])
                    return rec.op("vector", "tensor_add", out_ap,
                                  m1b[:, :], m2b[:, :], inc="ev")

                kv_ev = {}
                g = 0
                for oc in range(4):
                    for tg in range(NTG):
                        c0 = tg * TG
                        bank = pb[g % 2]
                        for ic in range(2):
                            w3 = []
                            if ic == 0:
                                w3 = [("ldg", ldg_h), ("ev", kcvc_ev)]
                                if g >= 2:
                                    w3.append(("ev", kv_ev[g - 2]))
                            rec.op("tensor", "matmul", bank[:, :],
                                   wk_sb[:, ic * 512 + oc * 128:
                                         ic * 512 + oc * 128 + 128],
                                   kc_sb[:, ic * T + c0: ic * T + c0 + TG],
                                   start=(ic == 0), stop=(ic == 1),
                                   waits=w3, inc="mm")
                        kv_ev[g] = rope_bf(
                            bank[:, :],
                            k_sb[:, oc * T + c0: oc * T + c0 + TG], c0,
                            [("mm", rec.n("mm"))])
                        g += 1
                v_ev = {}
                for tt in range(NTT):
                    bank = pb[2 + tt % 2]
                    for ic in range(2):
                        w3 = []
                        if ic == 0:
                            w3 = [("ldg", ldg_h), ("ev", kcvc_ev)]
                            if tt >= 2:
                                w3.append(("ev", v_ev[tt - 2]))
                        rec.op("tensor", "matmul", bank[:, 0:512],
                               vc_sb[:, ic * T + tt * 128:
                                     ic * T + tt * 128 + 128],
                               wv_sb[:, ic * 512:(ic + 1) * 512],
                               start=(ic == 0), stop=(ic == 1),
                               waits=w3, inc="mm")
                    v_ev[tt] = rec.op(
                        "vector", "tensor_copy", v4[:, tt, :, 0:DH],
                        bank[:, 0:512].rearrange("p (h d) -> p h d", d=DH),
                        waits=[("mm", rec.n("mm"))], inc="ev")
                kv_ready = rec.n("ev")

                wq4 = wq_h.rearrange("(ic p) o -> p ic o", p=128)
                qrope_ev = {}
                wq_mm = {}
                epi_ev_last = 0
                a_store = {}
                for qg in range(NTG):
                    qc0 = qg * TG
                    bufs_ = qg % 2
                    wsl = ([("mm", wq_mm[(qg - 2, 3)])] if qg >= 2 else [])
                    lds = rec.op("sync", "dma_start", out=srct4[:, bufs_],
                                 in_=src3[:, :, qc0:qc0 + TG],
                                 waits=wsl + [("st", src_done)],
                                 inc="ld", inck=16)
                    for oc in range(4):
                        gi = qg * 4 + oc
                        bufw = gi % 2
                        w = ([("mm", wq_mm[(qg - 2, oc)])]
                             if qg >= 2 else [])
                        ldv = rec.op("gpsimd", "dma_start",
                                     out=wqb4[:, bufw],
                                     in_=wq4[:, :,
                                             (cbase + oc) * 128:
                                             (cbase + oc) * 128 + 128],
                                     waits=w, inc="ldg", inck=16)
                        bank = pb[4 + oc % 2]
                        for ic in range(NCH):
                            w3 = []
                            if ic == 0:
                                w3 = [("ldg", ldv), ("ld", lds)]
                                if gi >= 2:
                                    w3.append(("ev", qrope_ev[gi - 2]))
                            rec.op("tensor", "matmul", bank[:, :],
                                   wqb4[:, bufw, ic, :],
                                   srct4[:, bufs_, ic, :],
                                   start=(ic == 0), stop=(ic == NCH - 1),
                                   waits=w3, inc="mm")
                        wq_mm[(qg, oc)] = rec.n("mm")
                        qrope_ev[gi] = rope_bf(
                            bank[:, :],
                            q_sb[:, oc * T + qc0: oc * T + qc0 + TG],
                            qc0, [("mm", wq_mm[(qg, oc)])])
                    q_ready = rec.n("ev")

                    for hh in range(8):
                        hchunk = hh // 2
                        hrow = (hh % 2) * 64
                        act_head_start = rec.n("act")
                        exp_cnt = {}

                        def emit_su4(i):
                            sbank = pb[i % 2]
                            w = [("ev", max(q_ready, kv_ready))]
                            if i >= 2:
                                w.append(("act", exp_cnt[i - 2]))
                            else:
                                w.append(("act", act_head_start))
                            if i == 0 and epi_ev_last:
                                w.append(("ev", epi_ev_last))
                            rec.op("tensor", "matmul", sbank[:, :],
                                   k_sb[hrow:hrow + 64,
                                        hchunk * T + i * 128:
                                        hchunk * T + i * 128 + 128],
                                   q_sb[hrow:hrow + 64,
                                        hchunk * T + qc0:
                                        hchunk * T + qc0 + TG],
                                   start=True, stop=True, waits=w,
                                   inc="mm")
                            smm = rec.n("mm")
                            ub = u_sb[:, (i % 2) * TG:(i % 2 + 1) * TG]
                            exp_cnt[i] = rec.op(
                                "scalar", "activation", ub, sbank[:, :],
                                AF.Exp, scale=0.125,
                                waits=[("mm", smm)], inc="act")
                            return ub

                        ubs = {0: emit_su4(0), 1: emit_su4(1)}
                        for i in range(NTT):
                            rec.op("tensor", "matmul",
                                   pb[2][0:DH + 1, :],
                                   v_sb[:, tt_off8(i) + hh * (DH + 1):
                                        tt_off8(i) + hh * (DH + 1)
                                        + DH + 1],
                                   ubs[i], start=(i == 0),
                                   stop=(i == NTT - 1),
                                   waits=[("act", exp_cnt[i])], inc="mm")
                            del ubs[i]
                            if i + 2 < NTT:
                                ubs[i + 2] = emit_su4(i + 2)
                        av_mm = rec.n("mm")
                        rec.op("vector", "reciprocal", zr_sb[:, :],
                               pb[2][DH:DH + 1, :],
                               waits=[("mm", av_mm)], inc="ev")
                        flv = rec.op("vector", "tensor_copy",
                                     m1b[:, 0:256], m2b[:, 0:256],
                                     inc="ev")
                        zmm = rec.op("tensor", "matmul", pb[3][0:64, :],
                                     ones64_f32[0:1, :], zr_sb[:, :],
                                     start=True, stop=True,
                                     waits=[("ev", flv)], inc="mm")
                        rec.op("vector", "tensor_copy", zbc[:, :],
                               pb[3][0:64, :], waits=[("mm", zmm)])
                        w = ([("st", a_store[hchunk])]
                             if hchunk in a_store else [])
                        epi_ev_last = rec.op(
                            "vector", "tensor_mul",
                            a_sb[hrow:hrow + 64,
                                 hchunk * TG:(hchunk + 1) * TG],
                            pb[2][0:64, :], zbc[:, :], waits=w, inc="ev")
                        if hh % 2 == 1:
                            a_store[hchunk] = rec.op(
                                "gpsimd", "dma_start",
                                out=a_hbm[(cbase + hchunk) * 128:
                                          (cbase + hchunk) * 128 + 128,
                                          qc0:qc0 + TG],
                                in_=a_sb[:,
                                         hchunk * TG:(hchunk + 1) * TG],
                                waits=[("ev", epi_ev_last)],
                                inc="st", inck=16)
            fence(rec, sems)
        esKV.close()
        a_done = rec.n("st")

        with ExitStack() as esE:
            sbE = lambda n, s_, d: esE.enter_context(nc.sbuf_tensor(pre + n, s_, d))
            wob = sbE("wob", [128, 2 * NCH * 128], BF16)
            at_sb = sbE("at", [128, 2 * NCH * TG], BF16)
            srcf = sbE("srcf", [128, 2 * NCH * TG], F32)
            s1q = sbE("s1q", [128, NCH * TG], F32)
            n1c = sbE("n1c", [128, NCH], F32)
            xxst = sbE("xxst", [128, 2 * TG], BF16)
            wob4 = wob[:].rearrange("p (b ic o) -> p b ic o", b=2, o=128)
            at4 = at_sb[:].rearrange("p (b c n) -> p b c n", b=2, n=TG)
            srcf4 = srcf[:].rearrange("p (b c n) -> p b c n", b=2, n=TG)
            wo4 = wo_h.rearrange("(ic p) o -> p ic o", p=128)
            n13 = n1w_h.rearrange("(c p o) -> c p o", p=128, o=1)
            for c in range(NCH):
                rec.op("sync", "dma_start", out=n1c[:, c:c + 1],
                       in_=n13[c], inc="ld", inck=16)
            n1_ld = rec.n("ld")
            womm = {}
            s1add_ev = {}
            xxcp_ev = {}
            xx_st = {}
            wo_all_mm = {}
            sq_ln_mm = {}
            prev_rows = 0
            for qg in range(NTG):
                qc0 = qg * TG
                bufq = qg % 2
                w = [("st", a_done)]
                if qg >= 2:
                    w.append(("mm", wo_all_mm[qg - 2]))
                lda = rec.op("sync", "dma_start", out=at4[:, bufq],
                             in_=a3[:, :, qc0:qc0 + TG], waits=w,
                             inc="ld", inck=16)
                wsrc = ([("ev", s1add_ev[(qg - 2, NCH - 1)])]
                        if qg >= 2 else [])
                lds = rec.op("gpsimd", "dma_start", out=srcf4[:, bufq],
                             in_=src3[:, :, qc0:qc0 + TG],
                             waits=wsrc + [("st", src_done)],
                             inc="ldg", inck=16)
                for oc in range(NCH):
                    gi = qg * NCH + oc
                    bufw = gi % 2
                    w = [("mm", womm[gi - 2])] if gi >= 2 else []
                    ldw = rec.op("gpsimd", "dma_start", out=wob4[:, bufw],
                                 in_=wo4[:, :, oc * 128:(oc + 1) * 128],
                                 waits=w, inc="ldg", inck=16)
                    bank = pb[4 + oc % 2]
                    for ic in range(NCH):
                        w3 = []
                        if ic == 0:
                            w3 = [("ldg", ldw), ("ld", lda)]
                            if gi >= 2:
                                w3.append(("ev", s1add_ev[(qg, oc - 2)]
                                           if oc >= 2 else
                                           s1add_ev[(qg - 1, NCH - 2 + oc)]))
                        rec.op("tensor", "matmul", bank[:, :],
                               wob4[:, bufw, ic, :], at4[:, bufq, ic, :],
                               start=(ic == 0), stop=(ic == NCH - 1),
                               waits=w3, inc="mm")
                    womm[gi] = rec.n("mm")
                    s1add_ev[(qg, oc)] = rec.op(
                        "vector", "tensor_add",
                        s1q[:, oc * TG:(oc + 1) * TG], bank[:, :],
                        srcf4[:, bufq, oc, :],
                        waits=[("mm", womm[gi]), ("ldg", lds)], inc="ev")
                    sqv = rec.op("scalar", "activation",
                                 s1[:, :] if oc % 2 == 0 else s2[:, :],
                                 s1q[:, oc * TG:(oc + 1) * TG], AF.Square,
                                 waits=[("ev", s1add_ev[(qg, oc)])]
                                 + ([("ln_mm", sq_ln_mm[(qg, oc - 2)])]
                                    if oc >= 2 else
                                    ([("ln_mm", sq_ln_mm[(qg - 1,
                                                          NCH - 2 + oc)])]
                                     if qg > 0 else [])),
                                 inc="ln_sq")
                    w0 = [("ln_sq", sqv)]
                    if oc == 0 and prev_rows:
                        w0.append(("ln_ev", prev_rows))
                    rec.op("tensor", "matmul", pb[1][0:1, :],
                           onescol_f32[:, 0:1],
                           s1[:, :] if oc % 2 == 0 else s2[:, :],
                           start=(oc == 0), stop=(oc == NCH - 1),
                           waits=w0, inc="ln_mm")
                    sq_ln_mm[(qg, oc)] = rec.n("ln_mm")
                wo_all_mm[qg] = rec.n("mm")
                mm_stats = rec.n("ln_mm")
                vrow = row[0:1, 2 * TG:3 * TG]
                rrow = row[0:1, 3 * TG:4 * TG]
                rec.op("vector", "tensor_scalar_mul", vrow, pb[1][0:1, :],
                       1.0 / D, waits=[("ln_mm", mm_stats)], inc="ln_ev")
                sv = rec.op("scalar", "activation", vrow, vrow, AF.Sqrt,
                            bias=eps2col[0:1, 0:1],
                            waits=[("ln_ev", rec.n("ln_ev"))],
                            inc="ln_sq")
                rec.op("vector", "reciprocal", rrow, vrow,
                       waits=[("ln_sq", sv)], inc="ln_ev")
                flv = rec.op("vector", "tensor_copy", s3[:, 0:256],
                             s3[:, 256:512], inc="ln_ev")
                prev_rows = flv
                wbc_ = [("ln_ev", flv)]
                if qg > 0:
                    wbc_.append(("ev", xxcp_ev[(qg - 1, NCH - 1)]))
                rec.op("tensor", "matmul", pb[3][:, :],
                       onesrow_f32[0:1, :], rrow, start=True, stop=True,
                       waits=wbc_, inc="ln_mm")
                mm_bc = rec.n("ln_mm")
                for oc in range(NCH):
                    gi = qg * NCH + oc
                    bufx = gi % 2
                    rec.op("vector", "scalar_tensor_tensor",
                           s1q[:, oc * TG:(oc + 1) * TG],
                           s1q[:, oc * TG:(oc + 1) * TG],
                           n1c[:, oc:oc + 1], pb[3][:, :],
                           ALU.mult, ALU.mult,
                           waits=[("ld", n1_ld), ("ln_mm", mm_bc)])
                    wx = [("st", xx_st[gi - 2])] if gi >= 2 else []
                    xxcp_ev[(qg, oc)] = rec.op(
                        "vector", "tensor_copy",
                        xxst[:, bufx * TG:(bufx + 1) * TG],
                        s1q[:, oc * TG:(oc + 1) * TG],
                        waits=wx, inc="ev")
                    xx_st[gi] = rec.op(
                        "gpsimd", "dma_start",
                        out=xx_hbm[oc * 128:(oc + 1) * 128, qc0:qc0 + TG],
                        in_=xxst[:, bufx * TG:(bufx + 1) * TG],
                        waits=[("ev", xxcp_ev[(qg, oc)])], inc="st",
                        inck=16)
            xx_done = rec.n("st")
        fence(rec, sems)

        with ExitStack() as esG:
            sbG = lambda n, s_, d: esG.enter_context(nc.sbuf_tensor(pre + n, s_, d))
            xxr = sbG("xxr", [128, 2 * NCH * 128], BF16)
            wg_sb = sbG("wgsb", [128, NCH * 2], BF16)
            we_sb = sbG("wesb", [128, NCH * E], BF16)
            gbrow = sbG("gbrow", [1, 2], BF16)
            ebrow = sbG("ebrow", [1, E], BF16)
            id_sb = sbG("idsb", [128, 128], F32)
            lg_sb = sbG("lgG", [128, E], F32)
            glg = sbG("glg", [128, 2], F32)
            grep = sbG("grep", [128, E], F32)
            gmax = sbG("gmaxG", [128, 1], F32)
            ngmax = sbG("ngmax", [128, 1], F32)
            gex = sbG("gex", [128, 2], F32)
            gsum = sbG("gsum", [128, 1], F32)
            wgp = sbG("wgp", [128, 1], F32)
            neginf = sbG("neginf", [128, E], F32)
            elm = sbG("elm", [128, E], F32)
            emax = sbG("emaxG", [128, 1], F32)
            nemax = sbG("nemaxG", [128, 1], F32)
            eex = sbG("eex", [128, E], F32)
            esum = sbG("esumG", [128, 1], F32)
            erec = sbG("erecG", [128, 1], F32)
            pG = sbG("pG", [128, E], F32)
            top1 = sbG("top1", [128, 1], F32)
            m1G = sbG("m1G", [128, E], F32)
            p2G = sbG("p2G", [128, E], F32)
            top2 = sbG("top2", [128, 1], F32)
            m2G = sbG("m2G", [128, E], F32)
            wts_t = sbG("wtstG", [128, E], F32)
            wtt = sbG("wttG", [8, 128], F32)
            xxr4 = xxr[:].rearrange("p (b c n) -> p b c n", b=2, n=128)

            rec.op("vector", "memset", neginf[:], -1e30)
            rec.op("sync", "dma_start", out=id_sb[:], in_=id_h, inc="ld",
                   inck=16)
            id_ld = rec.n("ld")
            rec.op("gpsimd", "dma_start",
                   out=wg_sb[:].rearrange("p (c n) -> p c n", n=2),
                   in_=wg_h.rearrange("(c p) n -> p c n", p=128),
                   inc="ldg", inck=16)
            rec.op("gpsimd", "dma_start",
                   out=we_sb[:].rearrange("p (c n) -> p c n", n=E),
                   in_=we_h.rearrange("(c p) n -> p c n", p=128),
                   inc="ldg", inck=16)
            rec.op("gpsimd", "dma_start", out=gbrow[:], in_=gb_h,
                   inc="ldg", inck=16)
            rec.op("gpsimd", "dma_start", out=ebrow[:], in_=eb_h,
                   inc="ldg", inck=16)
            g_ldg = rec.n("ldg")
            rt_mm = {}
            wt_store = {}
            cev_prev = 0
            for tt in range(NTT):
                buf = tt % 2
                w = [("st", xx_done), ("ldg", g_ldg)]
                if tt >= 2:
                    w.append(("mm", rt_mm[tt - 2]))
                ldv = rec.op("sync", "dma_start", out=xxr4[:, buf],
                             in_=xx3[:, :, tt * 128:(tt + 1) * 128],
                             waits=w, inc="ld", inck=16)
                bank = pb[4]
                gbank = pb[5]
                for ic in range(NCH):
                    w2_ = [("ld", ldv), ("ldg", g_ldg)]
                    if ic == 0 and tt > 0:
                        w2_.append(("ev", rec.n("ev")))
                    rec.op("tensor", "matmul", bank[:, 0:E],
                           xxr4[:, buf, ic, :],
                           we_sb[:, ic * E:(ic + 1) * E],
                           start=(ic == 0), stop=False, waits=w2_,
                           inc="mm")
                    rec.op("tensor", "matmul", gbank[:, 0:2],
                           xxr4[:, buf, ic, :],
                           wg_sb[:, ic * 2:(ic + 1) * 2],
                           start=(ic == 0), stop=False,
                           waits=w2_ if ic == 0 else (), inc="mm")
                rec.op("tensor", "matmul", bank[:, 0:E],
                       onesrow_bf[0:1, :], ebrow[:], start=False,
                       stop=True, inc="mm")
                rec.op("tensor", "matmul", gbank[:, 0:2],
                       onesrow_bf[0:1, :], gbrow[:], start=False,
                       stop=True, inc="mm")
                rt_mm[tt] = rec.n("mm")
                c1 = rec.op("vector", "tensor_copy", lg_sb[:],
                            bank[:, 0:E], waits=[("mm", rt_mm[tt])],
                            inc="ev")
                c2 = rec.op("vector", "tensor_copy", glg[:],
                            gbank[:, 0:2], inc="ev")
                gm = rec.op("vector", "tensor_reduce", gmax[:, 0:1],
                            gbank[:, 0:2], mybir.AxisListType.X, ALU.max,
                            inc="ev")
                ng = rec.op("vector", "tensor_scalar_mul", ngmax[:, 0:1],
                            gmax[:, 0:1], -1.0, waits=[("ev", gm)],
                            inc="ev")
                ng = rec.op("vector", "tensor_copy", s3[:, 0:256],
                            s3[:, 256:512], inc="ev")
                ge = rec.op("scalar", "activation", gex[:], glg[:],
                            AF.Exp, bias=ngmax[:, 0:1],
                            waits=[("ev", ng)], inc="act")
                gs = rec.op("vector", "tensor_reduce", gsum[:, 0:1],
                            gex[:], mybir.AxisListType.X, ALU.add,
                            waits=[("act", ge)], inc="ev")
                wg_ = rec.op("vector", "reciprocal", wgp[:, 0:1],
                             gsum[:, 0:1], waits=[("ev", gs)], inc="ev")
                gr = rec.op("vector", "tensor_copy",
                            grep[:].rearrange("p (g r) -> p g r", r=4),
                            glg[:].rearrange("p (g o) -> p g o", o=1)
                            .broadcast_to([128, 2, 4]),
                            waits=[("ev", c2)], inc="ev")
                mk = rec.op("vector", "tensor_scalar", grep[:], grep[:],
                            gmax[:, 0:1], None, ALU.is_equal,
                            waits=[("ev", gr)], inc="ev")
                msk_ = rec.op("vector", "scalar_tensor_tensor", elm[:],
                              lg_sb[:], 4096.0, grep[:], ALU.add, ALU.mult,
                              waits=[("ev", mk), ("ev", c1)], inc="ev")
                se = rec.op("vector", "tensor_scalar_add", elm[:], elm[:],
                            -4096.0, waits=[("ev", msk_)], inc="ev")
                em = rec.op("vector", "tensor_reduce", emax[:, 0:1],
                            elm[:], mybir.AxisListType.X, ALU.max,
                            waits=[("ev", se)], inc="ev")
                ne = rec.op("vector", "tensor_scalar_mul",
                            nemax[:, 0:1], emax[:, 0:1], -1.0,
                            waits=[("ev", em)], inc="ev")
                ne = rec.op("vector", "tensor_copy", s3[:, 0:256],
                            s3[:, 256:512], inc="ev")
                ee = rec.op("scalar", "activation", eex[:], elm[:],
                            AF.Exp, bias=nemax[:, 0:1],
                            waits=[("ev", ne)], inc="act")
                es_ = rec.op("vector", "tensor_reduce", esum[:, 0:1],
                             eex[:], mybir.AxisListType.X, ALU.add,
                             waits=[("act", ee)], inc="ev")
                er = rec.op("vector", "reciprocal", erec[:, 0:1],
                            esum[:, 0:1], waits=[("ev", es_)], inc="ev")
                pp = rec.op("vector", "tensor_scalar", pG[:], eex[:],
                            erec[:, 0:1], wgp[:, 0:1], ALU.mult,
                            ALU.mult, waits=[("ev", er), ("ev", wg_)],
                            inc="ev")
                t1_ = rec.op("vector", "tensor_reduce", top1[:, 0:1],
                             pG[:], mybir.AxisListType.X, ALU.max,
                             waits=[("ev", pp)], inc="ev")
                m1_ = rec.op("vector", "tensor_scalar", m1G[:], pG[:],
                             top1[:, 0:1], -1.0, ALU.is_equal, ALU.mult,
                             waits=[("ev", t1_)], inc="ev")
                p2_ = rec.op("vector", "tensor_scalar", m1G[:], m1G[:],
                             1.0, None, ALU.add, waits=[("ev", m1_)],
                             inc="ev")
                p2b = rec.op("vector", "tensor_mul", p2G[:], pG[:],
                             m1G[:], waits=[("ev", p2_)], inc="ev")
                t2_ = rec.op("vector", "tensor_reduce", top2[:, 0:1],
                             p2G[:], mybir.AxisListType.X, ALU.max,
                             waits=[("ev", p2b)], inc="ev")
                wv_ = rec.op("vector", "tensor_scalar", m2G[:], pG[:],
                             top2[:, 0:1], None, ALU.is_ge,
                             waits=[("ev", t2_)], inc="ev")
                rec.op("vector", "tensor_mul", wts_t[:], pG[:], m2G[:],
                       waits=[("ev", wv_)], inc="ev")
                wf = rec.op("vector", "tensor_copy", s3[:, 0:256],
                            s3[:, 256:512], inc="ev")
                wcevp = [("ev", cev_prev)] if tt > 0 else []
                tmm = rec.op("tensor", "matmul", pb[6][0:8, :128],
                             wts_t[:], id_sb[:], start=True, stop=True,
                             waits=[("ev", wf), ("ld", id_ld)] + wcevp,
                             inc="mm")
                wstp = [("st", wt_store[tt - 1])] if tt > 0 else []
                cev_prev = rec.op("vector", "tensor_copy", wtt[:],
                                  pb[6][0:8, :128],
                                  waits=[("mm", tmm)] + wstp, inc="ev")
                wt_store[tt] = rec.op(
                    "gpsimd", "dma_start",
                    out=wt_hbm[:, tt * 128:(tt + 1) * 128], in_=wtt[:],
                    waits=[("ev", cev_prev)], inc="st", inck=16)
                if debug and tt == 0:
                    for j, buf_ in enumerate((lg_sb, glg, grep, elm, eex,
                                              pG, m2G)):
                        nel = buf_.shape[1] if hasattr(buf_, "shape") else 8
                        rec.op("gpsimd", "dma_start",
                               out=dbg_h[:, j * 8:j * 8 + nel],
                               in_=buf_[:], waits=[("ev", cev_prev)],
                               inc="st", inck=16,
                               allow_slow_non_contiguous=True)
                    rec.op("vector", None, waits=[("st", rec.n("st"))])
            wt_done = rec.n("st")
        fence(rec, sems)

        TH = T // 2
        NTGH = TH // TG
        with ExitStack() as esF:
            sbF = lambda n, s_, d: esF.enter_context(nc.sbuf_tensor(pre + n, s_, d))
            xxt = sbF("xxt", [128, NCH * TG], BF16)
            hsw = sbF("hsw", [128, NHC * TH], BF16)
            outh = sbF("outh", [128, NCH * TH], BF16)
            w2sb = sbF("w2sb", [128, NHC * D], BF16)
            wsib = sbF("wsib", [128, 2 * 2 * NCH * 128], BF16)
            w1b = sbF("w1b", [128, 2 * 2 * NCH * 128], BF16)
            wsob = sbF("wsob", [128, 2 * NHC * 128], BF16)
            swt = sbF("swt", [128, NHC * TG], BF16)
            rbf2 = hstage[:, 0:TG]
            tbf = hstage[:, TG:2 * TG]
            wbcF = sbF("wbcF", [128, TG], F32)
            n2c = sbF("n2c", [128, NCH], F32)
            xres = xt
            xoutF = sbF("xoutF", [128, 2 * TG], F32)
            xxt3 = xxt[:].rearrange("p (c n) -> p c n", n=TG)
            wsib5 = wsib[:].rearrange("p (b a ic o) -> p b a ic o", b=2,
                                      a=2, o=128)
            w1b5 = w1b[:].rearrange("p (b a ic o) -> p b a ic o", b=2,
                                    a=2, o=128)
            wsob4 = wsob[:].rearrange("p (b ic o) -> p b ic o", b=2,
                                      o=128)
            wsi4 = wsi_h.rearrange("(ic p) o -> p ic o", p=128)
            w14 = w1_h.rearrange("(ic p) o -> p ic o", p=128)
            wso4 = wso_h.rearrange("(ic p) o -> p ic o", p=128)
            n23 = n2w_h.rearrange("(c p o) -> c p o", p=128, o=1)
            for c in range(NCH):
                rec.op("sync", "dma_start", out=n2c[:, c:c + 1],
                       in_=n23[c], inc="ld", inck=16)
            n2_ld = rec.n("ld")

            silu_mm = {}
            sho_mm = {}
            sho_ev = {}
            xx_guard = {}
            fout_ev = {}
            fout_st = {}
            route_ev = []
            ev1_prev = 0
            ev2_prev = 0
            for half in range(2):
                fence(rec, sems)
                for tgl in range(NTGH):
                    tg = half * NTGH + tgl
                    c0 = tg * TG
                    w = [("st", xx_done)]
                    if tg >= 1 and (tg - 1) in xx_guard:
                        w.append(("ev", xx_guard[tg - 1]))
                    ldx = rec.op("sync", "dma_start", out=xxt3[:],
                                 in_=xx3[:, :, c0:c0 + TG], waits=w,
                                 inc="ld", inck=16)
                    for fch in range(NHC):
                        gi = tg * NHC + fch
                        bufw = gi % 2
                        wld = ([("mm", silu_mm[gi - 2])]
                               if gi - 2 in silu_mm else [])
                        rec.op("gpsimd", "dma_start",
                               out=wsib5[:, bufw, 0],
                               in_=wsi4[:, :, fch * 128:fch * 128 + 128],
                               waits=wld, inc="ldg", inck=16)
                        rec.op("gpsimd", "dma_start",
                               out=wsib5[:, bufw, 1],
                               in_=wsi4[:, :, HID + fch * 128:
                                       HID + fch * 128 + 128],
                               inc="ldg", inck=16)
                        rec.op("gpsimd", "dma_start",
                               out=w1b5[:, bufw, 0],
                               in_=w14[:, :, fch * 128:fch * 128 + 128],
                               inc="ldg", inck=16)
                        ldw = rec.op("gpsimd", "dma_start",
                                     out=w1b5[:, bufw, 1],
                                     in_=w14[:, :, HID + fch * 128:
                                             HID + fch * 128 + 128],
                                     inc="ldg", inck=16)
                        banks = (pb[0], pb[1], pb[2], pb[3])
                        srcs = (wsib5[:, bufw, 0], wsib5[:, bufw, 1],
                                w1b5[:, bufw, 0], w1b5[:, bufw, 1])
                        for bi, (bank, wsl) in enumerate(zip(banks,
                                                             srcs)):
                            for ic in range(NCH):
                                w3 = []
                                if ic == 0:
                                    w3 = [("ldg", ldw), ("ld", ldx)]
                                    if bi < 2 and ev1_prev:
                                        w3.append(("ev", ev1_prev))
                                    if bi >= 2 and ev2_prev:
                                        w3.append(("ev", ev2_prev))
                                rec.op("tensor", "matmul", bank[:, :],
                                       wsl[:, ic, :],
                                       xxt3[:, ic, :],
                                       start=(ic == 0),
                                       stop=(ic == NCH - 1),
                                       waits=w3, inc="mm")
                        silu_mm[gi] = rec.n("mm")
                        sva = rec.op("scalar", "activation", s1[:, :],
                                     pb[0][:, :], AF.Silu,
                                     waits=[("mm", silu_mm[gi])]
                                     + ([("ev", ev1_prev)]
                                        if ev1_prev else []),
                                     inc="act")
                        wsw = [("act", sva)]
                        if fch == 0 and sho_mm:
                            wsw.append(("mm", max(sho_mm.values())))
                        ev1_prev = rec.op(
                            "vector", "tensor_mul",
                            swt[:, fch * TG:(fch + 1) * TG], s1[:, :],
                            pb[1][:, :], waits=wsw, inc="ev")
                        svb = rec.op("scalar", "activation", s2[:, :],
                                     pb[2][:, :], AF.Silu,
                                     waits=[("mm", silu_mm[gi])]
                                     + ([("ev", ev2_prev)]
                                        if ev2_prev else []),
                                     inc="act")
                        ev2_prev = rec.op(
                            "vector", "tensor_mul",
                            hsw[:, fch * TH + tgl * TG:
                                fch * TH + tgl * TG + TG],
                            s2[:, :], pb[3][:, :],
                            waits=[("act", svb)], inc="ev")
                    sw_done = rec.n("ev")
                    for oc in range(NCH):
                        gi2 = tg * NCH + oc
                        bufw = gi2 % 2
                        wld = ([("mm", sho_mm[gi2 - 2])]
                               if gi2 - 2 in sho_mm else [])
                        ldw = rec.op("gpsimd", "dma_start",
                                     out=wsob4[:, bufw],
                                     in_=wso4[:, :,
                                             oc * 128:(oc + 1) * 128],
                                     waits=wld, inc="ldg", inck=16)
                        bank = pb[4 + oc % 2]
                        for fc in range(NHC):
                            w3 = []
                            if fc == 0:
                                w3 = [("ldg", ldw), ("ev", sw_done)]
                                if gi2 - 2 in sho_ev:
                                    w3.append(("ev", sho_ev[gi2 - 2]))
                            rec.op("tensor", "matmul", bank[:, :],
                                   wsob4[:, bufw, fc, :],
                                   swt[:, fc * TG:(fc + 1) * TG],
                                   start=(fc == 0),
                                   stop=(fc == NHC - 1), waits=w3,
                                   inc="mm")
                        sho_mm[gi2] = rec.n("mm")
                        sho_ev[gi2] = rec.op(
                            "vector", "scalar_tensor_tensor",
                            outh[:, oc * TH + tgl * TG:
                                 oc * TH + tgl * TG + TG],
                            bank[:, :], 1.0, xxt3[:, oc, :],
                            ALU.mult, ALU.add,
                            waits=[("mm", sho_mm[gi2])], inc="ev")
                    xx_guard[tg] = sho_ev[tg * NCH + NCH - 1]
                sh_all = rec.n("ev")

                for e in range(E):
                    wld = [("mm", rec.n("mm"))]
                    for hc in range(NHC):
                        rec.op("gpsimd", "dma_start",
                               out=w2sb[:, hc * D:(hc + 1) * D],
                               in_=w2_h[e, hc * 128:(hc + 1) * 128, :],
                               waits=wld if hc == 0 else (),
                               inc="ldg", inck=16)
                    w2_ld = rec.n("ldg")
                    for tgl in range(NTGH):
                        tg = half * NTGH + tgl
                        c0 = tg * TG
                        wbw = [("st", wt_done)]
                        if route_ev:
                            wbw.append(("ev", route_ev[-1]))
                        ldwbc = rec.op("sync", "dma_start", out=wbcF[:],
                                       in_=wt_hbm[e:e + 1, c0:c0 + TG]
                                       .broadcast_to([128, TG]),
                                       waits=wbw, inc="ld", inck=16)
                        for oc in range(NCH):
                            bank = pb[oc % 2]
                            for fc in range(NHC):
                                w3 = []
                                if fc == 0:
                                    w3 = [("ldg", w2_ld),
                                          ("ev", sh_all)]
                                    if len(route_ev) >= 2:
                                        w3.append(("ev",
                                                   route_ev[-2]))
                                rec.op("tensor", "matmul", bank[:, :],
                                       w2sb[:, fc * D + oc * 128:
                                            fc * D + oc * 128 + 128],
                                       hsw[:, fc * TH + tgl * TG:
                                           fc * TH + tgl * TG + TG],
                                       start=(fc == 0),
                                       stop=(fc == NHC - 1),
                                       waits=w3, inc="mm")
                            ymm = rec.n("mm")
                            rec.op("vector", "scalar_tensor_tensor",
                                   tbf, bank[:, :], 1.0, wbcF[:],
                                   ALU.mult, ALU.mult,
                                   waits=[("mm", ymm), ("ld", ldwbc)])
                            rev = rec.op(
                                "vector", "tensor_add",
                                outh[:, oc * TH + tgl * TG:
                                     oc * TH + tgl * TG + TG],
                                tbf,
                                outh[:, oc * TH + tgl * TG:
                                     oc * TH + tgl * TG + TG],
                                inc="ev")
                            route_ev.append(rev)
                routed_done = rec.n("ev")

                prev_rows2 = 0
                f4_ln_mm = {}
                for tgl in range(NTGH):
                    tg = half * NTGH + tgl
                    c0 = tg * TG
                    for oc in range(NCH):
                        w = [("ev", routed_done)]
                        if oc >= 2:
                            w.append(("ln_mm", f4_ln_mm[oc - 2]))
                        elif tgl > 0 or half > 0:
                            w.append(("ln_mm", rec.n("ln_mm")))
                        if oc < 2 and fout_ev:
                            w.append(("ev", max(fout_ev.values())))
                        sqv = rec.op("scalar", "activation",
                                     s1[:, :] if oc % 2 == 0
                                     else s2[:, :],
                                     outh[:, oc * TH + tgl * TG:
                                          oc * TH + tgl * TG + TG],
                                     AF.Square, waits=w, inc="ln_sq")
                        w0 = [("ln_sq", sqv)]
                        if oc == 0 and prev_rows2:
                            w0.append(("ln_ev", prev_rows2))
                        rec.op("tensor", "matmul", pb[1][0:1, :],
                               onescol_f32[:, 0:1],
                               s1[:, :] if oc % 2 == 0 else s2[:, :],
                               start=(oc == 0), stop=(oc == NCH - 1),
                               waits=w0, inc="ln_mm")
                        f4_ln_mm[oc] = rec.n("ln_mm")
                    mm_stats = rec.n("ln_mm")
                    vrow = row[0:1, 2 * TG:3 * TG]
                    rrow = row[0:1, 3 * TG:4 * TG]
                    rec.op("vector", "tensor_scalar_mul", vrow,
                           pb[1][0:1, :], 1.0 / D,
                           waits=[("ln_mm", mm_stats)], inc="ln_ev")
                    sv = rec.op("scalar", "activation", vrow, vrow,
                                AF.Sqrt, bias=eps2col[0:1, 0:1],
                                waits=[("ln_ev", rec.n("ln_ev"))],
                                inc="ln_sq")
                    rec.op("vector", "reciprocal", rrow, vrow,
                           waits=[("ln_sq", sv)], inc="ln_ev")
                    flv = rec.op("vector", "tensor_copy", s3[:, 0:256],
                                 s3[:, 256:512], inc="ln_ev")
                    prev_rows2 = flv
                    wbcw = [("ln_ev", flv)]
                    rec.op("tensor", "matmul", pb[3][:, :],
                           onesrow_f32[0:1, :], rrow, start=True,
                           stop=True, waits=wbcw, inc="ln_mm")
                    mm_bc = rec.n("ln_mm")
                    rcp = rec.op("vector", "tensor_copy", rbf2,
                                 pb[3][:, :], waits=[("ln_mm", mm_bc)],
                                 inc="ln_ev")
                    for oc in range(NCH):
                        gi3 = tg * NCH + oc
                        bufx = gi3 % 2
                        wx = ([("ev", fout_ev[gi3 - 2])]
                              if gi3 - 2 in fout_ev else [])
                        ldr = rec.op("sync", "dma_start",
                                     out=xres[:,
                                              bufx * TG:(bufx + 1) * TG],
                                     in_=x_hbm[oc * 128:(oc + 1) * 128,
                                               c0:c0 + TG],
                                     waits=wx, inc="ld", inck=16)
                        rec.op("vector", "tensor_mul", tbf,
                               outh[:, oc * TH + tgl * TG:
                                    oc * TH + tgl * TG + TG],
                               rbf2, waits=[("ln_ev", rcp)])
                        rec.op("vector", "tensor_scalar_mul", s1[:, :],
                               tbf, n2c[:, oc:oc + 1],
                               waits=[("ld", n2_ld)])
                        wst = ([("st", fout_st[gi3 - 2])]
                               if gi3 - 2 in fout_st else [])
                        fout_ev[gi3] = rec.op(
                            "vector", "tensor_add",
                            xoutF[:, bufx * TG:(bufx + 1) * TG],
                            s1[:, :],
                            xres[:, bufx * TG:(bufx + 1) * TG],
                            waits=[("ld", ldr)] + wst, inc="ev")
                        fout_st[gi3] = rec.op(
                            "gpsimd", "dma_start",
                            out=xo_hbm[oc * 128:(oc + 1) * 128,
                                       c0:c0 + TG],
                            in_=xoutF[:, bufx * TG:(bufx + 1) * TG],
                            waits=[("ev", fout_ev[gi3])], inc="st",
                            inck=16)

        rec.op("gpsimd", None, waits=[("st", rec.n("st"))])
        if ctx is None:
            rec.replay(nc, sems)
        else:
            fence(rec, sems)
    return nc


def tt_off8(kt):
    return kt * (8 * 65)


def run_l1_ffn(x_fm_list, ins, debug=False, ncores=NCORES):
    perm = _evenodd_perm()
    cos2, ss2 = _rope_tables_l1()
    ss2 = _partner_swap(ss2)
    nc = build_l1_ffn(debug=debug)
    shared = dict(
        WqP=np.ascontiguousarray(ins["L1_ffn_Wq"][:, perm]),
        Wkc=ins["L1_ffn_Wk_c"], Wvc=ins["L1_ffn_Wv_c"],
        WkP=np.ascontiguousarray(ins["L1_ffn_Wk"][:, perm]),
        Wv=ins["L1_ffn_Wv"], Wo=ins["L1_ffn_Wo"],
        Wsi=ins["L1_ffn_W_shared_in"], Wso=ins["L1_ffn_W_shared_out"],
        W1s=ins["L1_ffn_W1_shared"], W2e=ins["L1_ffn_W2_experts"],
        Wg=ins["L1_ffn_Wg_gate"], We=ins["L1_ffn_We_gate"],
        gb=np.ascontiguousarray(ins["L1_ffn_group_bias"].reshape(1, 2)),
        eb=np.ascontiguousarray(ins["L1_ffn_expert_bias"].reshape(1, 8)),
        n1w=ins["L1_ffn_norm1_w"], n2w=ins["L1_ffn_norm2_w"],
        cos2=cos2, ss2=ss2, idm=np.eye(128, dtype=np.float32),
    )
    in_maps = []
    for b in range(ncores):
        m = dict(shared)
        m["x_fm"] = x_fm_list[b]
        in_maps.append(m)
    res = run_bass_kernel_spmd(nc, in_maps, core_ids=list(range(ncores)))
    if debug:
        return res.results
    return [r["x_out"] for r in res.results]



_L0_MASKS, _L0_TILEMAP = None, None


def _get_masks():
    global _L0_MASKS, _L0_TILEMAP
    if _L0_MASKS is None:
        _L0_MASKS, _L0_TILEMAP = _l0_masks()
    return _L0_MASKS, _L0_TILEMAP


def run_l0_attn(x_fm_list, ins):
    masks, tilemap = _get_masks()
    cos2, ss2 = _rope_tables_l0()
    ss2 = _partner_swap(ss2)
    qp = _qperm()
    nc = build_l0_attn(masks.shape[0], tilemap)
    shared = dict(
        Wq=np.ascontiguousarray(ins["L0_attn_Wq"][:, qp]),
        bq=np.ascontiguousarray(ins["L0_attn_bq"][qp]),
        Wk=ins["L0_attn_Wk"],
        Wv=ins["L0_attn_Wv"],
        Wo=np.ascontiguousarray(ins["L0_attn_Wo"][qp, :]),
        bo=ins["L0_attn_bo"],
        hw=np.ascontiguousarray(ins["L0_attn_head_weights"].reshape(1, 16)),
        cos2=cos2, ss2=ss2, masks=masks,
        bkv=np.ascontiguousarray(
            np.stack([ins["L0_attn_bk"], ins["L0_attn_bv"]])),
    )
    in_maps = []
    for b in range(NCORES):
        m = dict(shared)
        m["x_fm"] = x_fm_list[b]
        in_maps.append(m)
    res = run_bass_kernel_spmd(nc, in_maps, core_ids=list(range(NCORES)))
    return [r["x_out"] for r in res.results]


def _host_l0_attn(ins):
    masks, tilemap = _get_masks()
    cos2, ss2 = _rope_tables_l0()
    ss2 = _partner_swap(ss2)
    qp = _qperm()
    return dict(
        Wq=np.ascontiguousarray(ins["L0_attn_Wq"][:, qp]),
        bq=np.ascontiguousarray(ins["L0_attn_bq"][qp]),
        Wk=ins["L0_attn_Wk"],
        Wv=ins["L0_attn_Wv"],
        Wo=np.ascontiguousarray(ins["L0_attn_Wo"][qp, :]),
        bo=ins["L0_attn_bo"],
        hw=np.ascontiguousarray(ins["L0_attn_head_weights"].reshape(1, 16)),
        cos2=cos2, ss2=ss2, masks=masks,
        bkv=np.ascontiguousarray(
            np.stack([ins["L0_attn_bk"], ins["L0_attn_bv"]])),
    )


def _host_l0_ffn(ins):
    return dict(
        wr=ins["L0_ffn_w_router"], w1=ins["L0_ffn_w1"], b1=ins["L0_ffn_b1"],
        w2=ins["L0_ffn_w2"], b2=ins["L0_ffn_b2"],
        idm=np.eye(128, dtype=np.float32),
    )


def _host_l1_attn(ins):
    return dict(
        WinT=np.ascontiguousarray(ins["L1_attn_W_in"].T),
        WigT=np.ascontiguousarray(ins["L1_attn_W_ig"].T),
        WrgT=np.ascontiguousarray(ins["L1_attn_W_rg"].T),
        WoutT=np.ascontiguousarray(ins["L1_attn_W_out"].T),
        big=ins["L1_attn_b_ig"], brg=ins["L1_attn_b_rg"],
        lam=ins["L1_attn_lam"],
        h0v=np.ascontiguousarray(ins["L1_attn_h0"].reshape(-1)),
        nw=ins["L1_attn_norm_w"],
    )


def _host_l1_ffn(ins):
    perm = _evenodd_perm()
    cos2, ss2 = _rope_tables_l1()
    ss2 = _partner_swap(ss2)
    return dict(
        WqP=np.ascontiguousarray(ins["L1_ffn_Wq"][:, perm]),
        Wkc=ins["L1_ffn_Wk_c"], Wvc=ins["L1_ffn_Wv_c"],
        WkP=np.ascontiguousarray(ins["L1_ffn_Wk"][:, perm]),
        Wv=ins["L1_ffn_Wv"], Wo=ins["L1_ffn_Wo"],
        Wsi=ins["L1_ffn_W_shared_in"], Wso=ins["L1_ffn_W_shared_out"],
        W1s=ins["L1_ffn_W1_shared"], W2e=ins["L1_ffn_W2_experts"],
        Wg=ins["L1_ffn_Wg_gate"], We=ins["L1_ffn_We_gate"],
        gb=np.ascontiguousarray(ins["L1_ffn_group_bias"].reshape(1, 2)),
        eb=np.ascontiguousarray(ins["L1_ffn_expert_bias"].reshape(1, 8)),
        n1w=ins["L1_ffn_norm1_w"], n2w=ins["L1_ffn_norm2_w"],
        cos2=cos2, ss2=ss2, idm=np.eye(128, dtype=np.float32),
    )


def build_fused(nmasks, tilemap, nsec=4, debug_d=False):
    nc = bass.Bass("TRN2", target_bir_lowering=False)
    rec = Rec()
    x0 = nc.dram_tensor("x_fm", [D, T], F32, kind="ExternalInput").ap()
    xout = nc.dram_tensor("x_out", [D, T], F32, kind="ExternalOutput").ap()
    xs = [x0]
    for i in range(1, nsec):
        xs.append(nc.dram_tensor(f"x{i}_scr", [D, T], F32,
                                 kind="Internal").ap())
    xs.append(xout)
    es = ExitStack()
    with es:
        pb = [es.enter_context(nc.psum_tensor(f"pb{i}", [128, TG], F32))
              for i in range(8)]
        sems = {s: es.enter_context(nc.semaphore(s)) for s in SEMNAMES}
        emitters = [
            lambda c: build_l0_attn(nmasks, tilemap, ctx=c),
            lambda c: build_l0_ffn(ctx=c),
            lambda c: build_l1_attn(ctx=c),
            lambda c: build_l1_ffn(ctx=c, debug=debug_d),
        ]
        pres = ["A_", "B_", "C_", "D_"]
        for i in range(nsec):
            emitters[i]((nc, rec, sems, pb, xs[i], xs[i + 1], pres[i]))
        rec.replay(nc, sems)
    return nc


def kernel(**ins):
    return _kernel_n(ins, 4)


def _kernel_n(ins, nsec):
    masks, tilemap = _get_masks()
    nc = build_fused(masks.shape[0], tilemap, nsec=nsec)
    shared = {}
    hosts = (("A_", _host_l0_attn), ("B_", _host_l0_ffn),
             ("C_", _host_l1_attn), ("D_", _host_l1_ffn))
    for pre, fn in hosts[:nsec]:
        for k, v in fn(ins).items():
            shared[pre + k] = v
    x = ins["L0_attn_x"]
    in_maps = []
    for b in range(NCORES):
        m = dict(shared)
        m["x_fm"] = np.ascontiguousarray(x[b].T)
        in_maps.append(m)
    res = run_bass_kernel_spmd(nc, in_maps, core_ids=list(range(NCORES)))
    return np.stack([np.ascontiguousarray(r["x_out"].T)
                     for r in res.results])


# version 08
# speedup vs baseline: 1.0799x; 1.0799x over previous
import numpy as np
import ml_dtypes
from contextlib import ExitStack

import concourse.bass as bass
import concourse.mybir as mybir
from concourse.bass_utils import run_bass_kernel_spmd

F32 = mybir.dt.float32
BF16 = mybir.dt.bfloat16
AF = mybir.ActivationFunctionType
ALU = mybir.AluOpType

T = 4096
D = 1024
NCH = D // 128
TG = 512
NTG = T // TG
NCORES = 8
NTT = T // 128

ENGINES = ("sync", "gpsimd", "vector", "scalar", "tensor")
SEMNAMES = ("ln_ld", "ln_sq", "ln_mm", "ln_ev", "ln_h", "ldg", "ld",
            "mm", "act", "ev", "st", "st2")



def _rope_tables_l0():
    HD = 64
    nb = HD // 4
    inv = 1.0 / 10000.0 ** (np.arange(nb, dtype=np.float64) / nb)
    ang = np.arange(T, dtype=np.float64)[:, None] * inv[None, :]
    sin_h = np.repeat(np.sin(ang), 2, axis=-1)
    cos_h = np.repeat(np.cos(ang), 2, axis=-1)
    sin = np.concatenate([sin_h, sin_h], -1)
    cos = np.concatenate([cos_h, cos_h], -1)
    cosT = cos.T.astype(np.float32)
    sinT = sin.T.astype(np.float32)
    ss = sinT.copy()
    ss[:32] *= -1.0
    cos2 = np.vstack([cosT, cosT])
    ss2 = np.vstack([ss, ss])
    return np.ascontiguousarray(cos2), np.ascontiguousarray(ss2)


def _rope_tables_l1():
    DH = 64
    inv = 1.0 / 10000.0 ** (np.arange(0, DH, 2, dtype=np.float64) / DH)
    ang = np.arange(T, dtype=np.float64)[:, None] * inv[None, :]
    sn = np.sin(ang).T
    cs = np.cos(ang).T
    cosT = np.vstack([cs, cs]).astype(np.float32)
    ss = np.vstack([-sn, sn]).astype(np.float32)
    cos2 = np.vstack([cosT, cosT])
    ss2 = np.vstack([ss, ss])
    return np.ascontiguousarray(cos2), np.ascontiguousarray(ss2)


def _partner_swap(tab):
    out = tab.copy()
    for b in range(4):
        out[32 * b:32 * b + 32] = tab[32 * (b ^ 1):32 * (b ^ 1) + 32]
    return np.ascontiguousarray(out)


def _evenodd_perm():
    p = []
    for h in range(16):
        base = h * 64
        p.extend(base + np.arange(0, 64, 2))
        p.extend(base + np.arange(1, 64, 2))
    return np.array(p)


QPAIRS = [(0, 4), (1, 5), (2, 6), (3, 7),
          (8, 12), (9, 13), (10, 14), (11, 15)]
HEADPOS = {}
for _c, (_a, _b) in enumerate(QPAIRS):
    HEADPOS[_a] = (_c, 0)
    HEADPOS[_b] = (_c, 64)


def _qperm():
    idx = []
    for (a, b) in QPAIRS:
        idx.extend(range(a * 64, a * 64 + 64))
        idx.extend(range(b * 64, b * 64 + 64))
    return np.array(idx)


def _l0_masks():
    HW, G = 12, 8
    uniq, masks, tilemap = {}, [], {}
    i_all = np.arange(T)
    for qg in range(NTG):
        qi = i_all[qg * TG:(qg + 1) * TG][None, :]
        for kt in range(NTT):
            kj = i_all[kt * 128:(kt + 1) * 128][:, None]
            allowed = (np.abs(qi - kj) <= HW) | (qi < G) | (kj < G)
            if not allowed.any():
                tilemap[(qg, kt)] = "skip"
            elif allowed.all():
                tilemap[(qg, kt)] = "full"
            else:
                key = allowed.tobytes()
                if key not in uniq:
                    uniq[key] = len(masks)
                    masks.append(allowed.astype(ml_dtypes.bfloat16))
                tilemap[(qg, kt)] = uniq[key]
    return np.stack(masks), tilemap



class Rec:
    def __init__(self):
        self.ops = {e: [] for e in ENGINES}
        self.cnt = {}

    def op(self, eng, meth, *a, waits=(), inc=None, inck=1, **kw):
        waits = [(s, v) for (s, v) in waits if v > 0]
        self.ops[eng].append((meth, a, kw, tuple(waits), inc, inck))
        if inc is not None:
            self.cnt[inc] = self.cnt.get(inc, 0) + inck
            return self.cnt[inc]
        return None

    def n(self, sem):
        return self.cnt.get(sem, 0)

    def replay(self, nc, sems):
        with nc.Block() as blk:
            def mk(eng_name):
                def body(eng):
                    for meth, a, kw, waits, inc, inck in self.ops[eng_name]:
                        for (s, v) in waits:
                            eng.wait_ge(sems[s], v)
                        if meth is None:
                            continue
                        try:
                            ins = getattr(eng, meth)(*a, **kw)
                        except Exception:
                            import sys
                            print(f"replay error: {eng_name}.{meth} "
                                  f"args={a} kw={kw}", file=sys.stderr)
                            raise
                        if inc is not None:
                            ins.then_inc(sems[inc], inck)
                return body
            for e in ENGINES:
                if self.ops[e]:
                    getattr(blk, e)(mk(e))


def fence(rec, sems):
    waits = [(s, rec.n(s)) for s in sems if rec.n(s) > 0]
    for e in ENGINES:
        rec.op(e, None, waits=waits)



def emit_ln(rec, B, x_hbm, h_sb, kind="ln", weight_cols=None, eps=1e-5,
            h_hbm=None):
    xt, row = B["xt"], B["row"]
    t1 = B["t1"][:, :]
    sqbufs = (B["sq0"], B["sq1"])
    pb = B["pb"]
    onescol, onesrow = B["onescol_f32"], B["onesrow_f32"]
    ln = kind == "ln"
    mm_after = {}
    h_after = {}
    prev_rows_ev = 0
    prev_h = 0
    for tg in range(NTG):
        c0 = tg * TG
        for c in range(NCH):
            buf = c % 2
            xts = xt[:, buf * TG:(buf + 1) * TG]
            sqs = sqbufs[buf][:, :]
            waits = []
            if c >= 2:
                waits.append(("ln_mm", mm_after[(tg, c - 2)]))
            elif tg > 0:
                waits.append(("ln_h", h_after[(tg - 1, NCH - 2 + c)]))
            ldv = rec.op("sync", "dma_start", out=xts,
                         in_=x_hbm[c * 128:(c + 1) * 128, c0:c0 + TG],
                         waits=waits, inc="ln_ld", inck=16)
            sqv = rec.op("scalar", "activation", sqs, xts, AF.Square,
                         waits=[("ln_ld", ldv)], inc="ln_sq")
            w0 = [("ln_ld", ldv)]
            if c == 0 and prev_rows_ev:
                w0.append(("ln_ev", prev_rows_ev))
            if ln:
                rec.op("tensor", "matmul", pb[0][0:1, :], onescol[:, 0:1], xts,
                       start=(c == 0), stop=(c == NCH - 1),
                       waits=w0, inc="ln_mm")
            rec.op("tensor", "matmul", pb[1][0:1, :], onescol[:, 0:1], sqs,
                   start=(c == 0), stop=(c == NCH - 1),
                   waits=[("ln_sq", sqv)] + (w0[1:] if c == 0 else []),
                   inc="ln_mm")
            mm_after[(tg, c)] = rec.n("ln_mm")
        mm_stats = rec.n("ln_mm")
        mrow = row[0:1, 0:TG]
        msq = row[0:1, TG:2 * TG]
        vrow = row[0:1, 2 * TG:3 * TG]
        rrow = row[0:1, 3 * TG:4 * TG]
        if ln:
            rec.op("vector", "tensor_scalar_mul", mrow, pb[0][0:1, :],
                   1.0 / D, waits=[("ln_mm", mm_stats)])
            rec.op("vector", "tensor_mul", msq, mrow, mrow)
            rec.op("vector", "scalar_tensor_tensor", vrow, pb[1][0:1, :],
                   1.0 / D, msq, ALU.mult, ALU.subtract, inc="ln_ev")
        else:
            rec.op("vector", "tensor_scalar_mul", vrow, pb[1][0:1, :],
                   1.0 / D, waits=[("ln_mm", mm_stats)], inc="ln_ev")
        sv = rec.op("scalar", "activation", vrow, vrow, AF.Sqrt,
                    bias=B["epscol"], waits=[("ln_ev", rec.n("ln_ev"))],
                    inc="ln_sq")
        rv = rec.op("vector", "reciprocal", rrow, vrow,
                    waits=[("ln_sq", sv)], inc="ln_ev")
        rv = rec.op("vector", "tensor_copy", t1[:, 0:256], t1[:, 256:512],
                    inc="ln_ev")
        prev_rows_ev = rv
        wbc = [("ln_ev", rv)]
        if prev_h:
            wbc.append(("ln_h", prev_h))
        if ln:
            rec.op("tensor", "matmul", pb[2][:, :], onesrow[0:1, :], mrow,
                   start=True, stop=True, waits=wbc, inc="ln_mm")
        rec.op("tensor", "matmul", pb[3][:, :], onesrow[0:1, :], rrow,
               start=True, stop=True, waits=wbc, inc="ln_mm")
        mm_bc = rec.n("ln_mm")
        for c in range(NCH):
            buf = c % 2
            xts = xt[:, buf * TG:(buf + 1) * TG]
            waits = []
            if c >= 2:
                waits.append(("ln_h", h_after[(tg, c - 2)]))
            else:
                waits.append(("ln_mm", mm_stats))
            ldv = rec.op("sync", "dma_start", out=xts,
                         in_=x_hbm[c * 128:(c + 1) * 128, c0:c0 + TG],
                         waits=waits, inc="ln_ld", inck=16)
            gi = tg * NCH + c
            if h_hbm is None:
                hs = h_sb[:, c * T + c0: c * T + c0 + TG]
            else:
                hs = h_sb[:, buf * TG:(buf + 1) * TG]
            w = [("ln_ld", ldv), ("ln_mm", mm_bc)]
            if h_hbm is not None and gi >= 2:
                w.append(("st", B["h_store"][gi - 2]))
            if ln:
                rec.op("vector", "tensor_sub", t1, xts, pb[2][:, :], waits=w)
                h_after[(tg, c)] = rec.op("vector", "tensor_mul", hs, t1,
                                          pb[3][:, :], inc="ln_h")
            else:
                if weight_cols is not None:
                    h_after[(tg, c)] = rec.op(
                        "vector", "scalar_tensor_tensor", hs, xts,
                        weight_cols[c], pb[3][:, :], ALU.mult, ALU.mult,
                        waits=w, inc="ln_h")
                else:
                    h_after[(tg, c)] = rec.op("vector", "tensor_mul", hs, xts,
                                              pb[3][:, :], waits=w, inc="ln_h")
            if h_hbm is not None:
                B["h_store"][gi] = rec.op(
                    "gpsimd", "dma_start",
                    out=h_hbm[c * 128:(c + 1) * 128, c0:c0 + TG], in_=hs,
                    waits=[("ln_h", h_after[(tg, c)])], inc="st", inck=16)
        prev_h = rec.n("ln_h")



def tt_off(kt):
    return kt * (4 * 65)


def build_l0_attn(nmasks, tilemap, ctx=None):
    if ctx is None:
        nc = bass.Bass("TRN2", target_bir_lowering=False)
        rec = Rec()
        x_hbm = nc.dram_tensor("x_fm", [D, T], F32,
                               kind="ExternalInput").ap()
        xo_hbm = nc.dram_tensor("x_out", [D, T], F32,
                                kind="ExternalOutput").ap()
        pre = ""
    else:
        nc, rec, sems, pb, x_hbm, xo_hbm, pre = ctx
    H, HKV, HD = 16, 4, 64
    NKV = HKV * HD

    wq_h = nc.dram_tensor(pre + "Wq", [D, D], F32, kind="ExternalInput").ap()
    bq_h = nc.dram_tensor(pre + "bq", [D], F32, kind="ExternalInput").ap()
    wk_h = nc.dram_tensor(pre + "Wk", [D, NKV], F32, kind="ExternalInput").ap()
    wv_h = nc.dram_tensor(pre + "Wv", [D, NKV], F32, kind="ExternalInput").ap()
    wo_h = nc.dram_tensor(pre + "Wo", [D, D], F32, kind="ExternalInput").ap()
    bo_h = nc.dram_tensor(pre + "bo", [D], F32, kind="ExternalInput").ap()
    hw_h = nc.dram_tensor(pre + "hw", [1, H], F32, kind="ExternalInput").ap()
    cos_h = nc.dram_tensor(pre + "cos2", [128, T], F32,
                           kind="ExternalInput").ap()
    ss_h = nc.dram_tensor(pre + "ss2", [128, T], F32,
                          kind="ExternalInput").ap()
    msk_h = nc.dram_tensor(pre + "masks", [nmasks, 128, TG], BF16,
                           kind="ExternalInput").ap()
    bkv_h = nc.dram_tensor(pre + "bkv", [2, NKV], F32,
                           kind="ExternalInput").ap()

    es = ExitStack()
    sbt = lambda name, shape, dt: es.enter_context(
        nc.sbuf_tensor(pre + name, shape, dt))
    with es:
        h_sb = sbt("h", [128, NCH * T], BF16)
        k_sb = sbt("k", [128, 2 * T], BF16)
        v_sb = sbt("v", [128, NTT * (HKV * (HD + 1))], BF16)
        q_sb = sbt("q", [128, NCH * TG], BF16)
        a_sb = sbt("a", [128, NCH * TG], BF16)
        wqb = sbt("wqb", [128, 2 * NCH * 128], BF16)
        wob = sbt("wob", [128, 2 * NCH * 128], BF16)
        wk_sb = sbt("wk", [128, NCH * NKV], BF16)
        wv_sb = sbt("wv", [128, NCH * NKV], BF16)
        cos_sb = sbt("cos", [128, T], F32)
        ss_sb = sbt("ss", [128, T], F32)
        msk_sb = sbt("msk", [128, nmasks * TG], BF16)
        bq_sb = sbt("bqc", [128, NCH], F32)
        bo_sb = sbt("boc", [128, NCH], F32)
        bk_sb = sbt("bkc", [128, 2], F32)
        bvrow = sbt("bvrow", [1, NKV], BF16)
        hw_sb = sbt("hwbc", [128, H], F32)
        hwq_sb = sbt("hwq", [128, NCH], F32)
        hwrow = sbt("hwrow", [1, H], F32)
        onescol_f32 = sbt("ones_c", [128, 1], F32)
        epscol = sbt("epscol", [128, 1], F32)
        onesrow_f32 = sbt("ones_r", [1, 128], F32)
        onesrow_bf = sbt("ones_rb", [1, 128], BF16)
        ones64_f32 = sbt("ones64", [1, 64], F32)
        xt = sbt("xt", [128, 2 * TG], F32)
        row = sbt("rowbuf", [1, 4 * TG], F32)
        s1 = sbt("s1", [128, TG], F32)
        s2 = sbt("s2", [128, TG], F32)
        s3 = sbt("s3", [128, TG], F32)
        e1, m1, m2 = s1, s2, s3
        u_sb = sbt("u", [128, 2 * TG], BF16)
        um_sb = sbt("um", [128, 2 * TG], BF16)
        zr_sb = sbt("zr", [1, TG], F32)
        zbc = sbt("zbc", [64, TG], F32)
        xres = sbt("xres", [128, 2 * TG], F32)
        xnew = xt

        if ctx is None:
            pb = [es.enter_context(nc.psum_tensor(f"pb{i}", [128, TG],
                                                  F32)) for i in range(8)]
            sems = {}
            for s in SEMNAMES:
                sems[s] = es.enter_context(nc.semaphore(s))

        rec.op("vector", "memset", onescol_f32[:], 1.0)
        rec.op("vector", "memset", epscol[:], 1e-5)
        rec.op("vector", "memset", onesrow_f32[:], 1.0)
        rec.op("vector", "memset", onesrow_bf[:], 1.0)
        rec.op("vector", "memset", ones64_f32[:], 1.0)
        v4 = v_sb[:].rearrange("p (tt kv d) -> p tt kv d", kv=HKV, d=HD + 1)
        rec.op("vector", "memset", v4[:, :, :, HD:HD + 1], 1.0, inc="ev")

        rec.op("sync", "dma_start", out=cos_sb[:], in_=cos_h, inc="ld", inck=16)
        rec.op("sync", "dma_start", out=ss_sb[:], in_=ss_h, inc="ld", inck=16)
        ld_hw = rec.op("sync", "dma_start", out=hwrow[:], in_=hw_h,
                       inc="ld", inck=16)
        bq3 = bq_h.rearrange("(c p o) -> c p o", p=128, o=1)
        bo3 = bo_h.rearrange("(c p o) -> c p o", p=128, o=1)
        for c in range(NCH):
            rec.op("sync", "dma_start", out=bq_sb[:, c:c + 1], in_=bq3[c],
                   inc="ld", inck=16)
            rec.op("sync", "dma_start", out=bo_sb[:, c:c + 1], in_=bo3[c],
                   inc="ld", inck=16)
        bk3 = bkv_h[0].rearrange("(c p o) -> c p o", p=128, o=1)
        for c in range(2):
            rec.op("sync", "dma_start", out=bk_sb[:, c:c + 1], in_=bk3[c],
                   inc="ld", inck=16)
        for i in range(nmasks):
            rec.op("sync", "dma_start", out=msk_sb[:, i * TG:(i + 1) * TG],
                   in_=msk_h[i], inc="ld", inck=16)
        ld_pre = rec.n("ld")

        rec.op("gpsimd", "dma_start",
               out=wk_sb[:].rearrange("p (c n) -> p c n", n=NKV),
               in_=wk_h.rearrange("(c p) n -> p c n", p=128),
               inc="ldg", inck=16)
        rec.op("gpsimd", "dma_start",
               out=wv_sb[:].rearrange("p (c n) -> p c n", n=NKV),
               in_=wv_h.rearrange("(c p) n -> p c n", p=128),
               inc="ldg", inck=16)
        rec.op("gpsimd", "dma_start", out=bvrow[:], in_=bkv_h[1:2, :],
               inc="ldg", inck=16)
        ldg_kv = rec.n("ldg")

        hw_mm = rec.op("tensor", "matmul", pb[4][:, 0:H], onesrow_f32[0:1, :],
                       hwrow[:], start=True, stop=True,
                       waits=[("ld", ld_hw)], inc="mm")
        hwcp = rec.op("vector", "tensor_copy", hw_sb[:], pb[4][:, 0:H],
                      waits=[("mm", hw_mm)], inc="ev")
        for c, (ha, hb) in enumerate(QPAIRS):
            rec.op("vector", "tensor_copy", hwq_sb[0:64, c:c + 1],
                   hw_sb[0:64, ha:ha + 1],
                   waits=[("ev", hwcp)] if c == 0 else ())
            last = (c == NCH - 1)
            rec.op("vector", "tensor_copy", hwq_sb[64:128, c:c + 1],
                   hw_sb[64:128, hb:hb + 1],
                   inc="ev" if last else None)
        ev_hwq = rec.n("ev")
        rec.op("vector", None, waits=[("ev", ev_hwq)])

        emit_ln(rec, dict(xt=xt, sq0=s1, sq1=s2, row=row, t1=s3, pb=pb,
                          onescol_f32=onescol_f32, onesrow_f32=onesrow_f32,
                          epscol=epscol[0:1, 0:1]),
                x_hbm, h_sb, kind="ln")
        ln_done = [("ln_h", rec.n("ln_h"))]

        def rope_emit(ps_ap, bias_col, hw_col, out_ap, tc0, extra_waits):
            w = list(extra_waits)
            if hw_col is None:
                rec.op("vector", "tensor_scalar_add", e1[:], ps_ap, bias_col,
                       waits=w)
            else:
                rec.op("vector", "tensor_scalar", e1[:], ps_ap, bias_col,
                       hw_col, ALU.add, ALU.mult, waits=w)
            rec.op("vector", "tensor_mul", m1[:], e1[:],
                   cos_sb[:, tc0:tc0 + TG])
            for b in range(4):
                pbk = b ^ 1
                rec.op("vector", "tensor_mul", m2[32 * b:32 * b + 32, :],
                       e1[32 * pbk:32 * pbk + 32, :],
                       ss_sb[32 * pbk:32 * pbk + 32, tc0:tc0 + TG])
            return rec.op("vector", "tensor_add", out_ap, m1[:], m2[:],
                          inc="ev")

        rope_ev = {}
        g = 0
        for kc in range(2):
            for tg in range(NTG):
                c0 = tg * TG
                bank = pb[g % 2]
                for ic in range(NCH):
                    w = []
                    if ic == 0:
                        w = [("ldg", 16)] + ln_done
                        if g >= 2:
                            w.append(("ev", rope_ev[g - 2]))
                    rec.op("tensor", "matmul", bank[:, :],
                           wk_sb[:, ic * NKV + kc * 128:
                                 ic * NKV + kc * 128 + 128],
                           h_sb[:, ic * T + c0: ic * T + c0 + TG],
                           start=(ic == 0), stop=(ic == NCH - 1),
                           waits=w, inc="mm")
                mmv = rec.n("mm")
                rope_ev[g] = rope_emit(bank[:, :], bk_sb[:, kc:kc + 1], None,
                                       k_sb[:, kc * T + c0: kc * T + c0 + TG],
                                       c0, [("mm", mmv), ("ld", ld_pre)])
                g += 1
        k_ready = rec.n("ev")

        v_ev = {}
        for tt in range(NTT):
            bank = pb[2 + tt % 2]
            for ic in range(NCH):
                w = []
                if ic == 0:
                    w = [("ldg", 32)] + ln_done
                    if tt >= 2:
                        w.append(("ev", v_ev[tt - 2]))
                rec.op("tensor", "matmul", bank[:, 0:NKV],
                       h_sb[:, ic * T + tt * 128: ic * T + tt * 128 + 128],
                       wv_sb[:, ic * NKV:(ic + 1) * NKV],
                       start=(ic == 0), stop=False, waits=w, inc="mm")
            rec.op("tensor", "matmul", bank[:, 0:NKV], onesrow_bf[0:1, :],
                   bvrow[:], start=False, stop=True,
                   waits=[("ldg", ldg_kv)], inc="mm")
            mmv = rec.n("mm")
            v_ev[tt] = rec.op("vector", "tensor_copy", v4[:, tt, :, 0:HD],
                              bank[:, 0:NKV].rearrange("p (kv d) -> p kv d",
                                                       d=HD),
                              waits=[("mm", mmv)], inc="ev")
        v_ready = rec.n("ev")

        wq4 = wq_h.rearrange("(ic p) o -> p ic o", p=128)
        wo4 = wo_h.rearrange("(ic p) o -> p ic o", p=128)
        wqb4 = wqb[:].rearrange("p (b ic o) -> p b ic o", b=2, o=128)
        wob4 = wob[:].rearrange("p (b ic o) -> p b ic o", b=2, o=128)

        qrope_ev = {}
        wq_mm = {}
        wo_mm = {}
        xres_ev = {}
        wo_store = {}
        wo_mm_last = 0
        epi_ev_last = 0

        for qg in range(NTG):
            qc0 = qg * TG
            for oc in range(NCH):
                gi = qg * NCH + oc
                buf = gi % 2
                w = [("mm", wq_mm[gi - 2])] if gi >= 2 else []
                ldv = rec.op("gpsimd", "dma_start", out=wqb4[:, buf],
                             in_=wq4[:, :, oc * 128:(oc + 1) * 128],
                             waits=w, inc="ldg", inck=16)
                bank = pb[4 + oc % 2]
                for ic in range(NCH):
                    w = []
                    if ic == 0:
                        w = [("ldg", ldv)] + ln_done + [("ev", ev_hwq)]
                        if gi >= 2:
                            w.append(("ev", qrope_ev[gi - 2]))
                    rec.op("tensor", "matmul", bank[:, :],
                           wqb4[:, buf, ic, :],
                           h_sb[:, ic * T + qc0: ic * T + qc0 + TG],
                           start=(ic == 0), stop=(ic == NCH - 1),
                           waits=w, inc="mm")
                wq_mm[gi] = rec.n("mm")
                qrope_ev[gi] = rope_emit(
                    bank[:, :], bq_sb[:, oc:oc + 1], hwq_sb[:, oc:oc + 1],
                    q_sb[:, oc * TG:(oc + 1) * TG], qc0,
                    [("mm", wq_mm[gi]), ("ld", ld_pre)])
            q_ready = rec.n("ev")

            kts = [(kt, tilemap[(qg, kt)]) for kt in range(NTT)
                   if tilemap[(qg, kt)] != "skip"]
            n_t = len(kts)
            for h16 in range(H):
                kv = h16 // 4
                kchunk = kv // 2
                krow = (kv % 2) * 64
                qchunk, qrow = HEADPOS[h16]
                act_head_start = rec.n("act")
                exp_cnt = {}
                av_rhs = {}

                def emit_su(i):
                    kt, mi = kts[i]
                    sbank = pb[i % 2]
                    w = [("ev", max(q_ready, k_ready, v_ready))]
                    if i >= 2:
                        w.append(("act", exp_cnt[i - 2]))
                    else:
                        w.append(("act", act_head_start))
                    rec.op("tensor", "matmul", sbank[:, :],
                           k_sb[krow:krow + 64,
                                kchunk * T + kt * 128:
                                kchunk * T + kt * 128 + 128],
                           q_sb[qrow:qrow + 64,
                                qchunk * TG:(qchunk + 1) * TG],
                           start=True, stop=True, waits=w, inc="mm")
                    smm = rec.n("mm")
                    ub = u_sb[:, (i % 2) * TG:(i % 2 + 1) * TG]
                    exp_cnt[i] = rec.op("scalar", "activation", ub,
                                        sbank[:, :], AF.Exp, scale=0.125,
                                        waits=[("mm", smm)], inc="act")
                    if mi != "full":
                        umb = um_sb[:, (i % 2) * TG:(i % 2 + 1) * TG]
                        mev = rec.op("vector", "tensor_mul", umb, ub,
                                     msk_sb[:, mi * TG:(mi + 1) * TG],
                                     waits=[("act", exp_cnt[i]),
                                            ("ld", ld_pre)], inc="ev")
                        av_rhs[i] = ("ev", mev, umb)
                    else:
                        av_rhs[i] = ("act", exp_cnt[i], ub)

                emit_su(0)
                if n_t > 1:
                    emit_su(1)
                for i, (kt, mi) in enumerate(kts):
                    sem, val, rhs = av_rhs[i]
                    wv5 = [(sem, val)]
                    if i == 0 and epi_ev_last:
                        wv5.append(("ev", epi_ev_last))
                    rec.op("tensor", "matmul", pb[2][0:HD + 1, :],
                           v_sb[:, tt_off(kt) + kv * (HD + 1):
                                tt_off(kt) + kv * (HD + 1) + HD + 1],
                           rhs, start=(i == 0), stop=(i == n_t - 1),
                           waits=wv5, inc="mm")
                    if i + 2 < n_t:
                        emit_su(i + 2)
                av_mm = rec.n("mm")
                rec.op("vector", "reciprocal", zr_sb[:, :],
                       pb[2][HD:HD + 1, :], waits=[("mm", av_mm)], inc="ev")
                flv = rec.op("vector", "tensor_copy", m1[:, 0:256],
                             m2[:, 0:256], inc="ev")
                zmm = rec.op("tensor", "matmul", pb[3][0:64, :],
                             ones64_f32[0:1, :], zr_sb[:, :],
                             start=True, stop=True,
                             waits=[("ev", flv)], inc="mm")
                rec.op("vector", "tensor_copy", zbc[:, :], pb[3][0:64, :],
                       waits=[("mm", zmm)])
                w = [("mm", wo_mm_last)] if wo_mm_last else []
                epi_ev_last = rec.op(
                    "vector", "tensor_mul",
                    a_sb[qrow:qrow + 64, qchunk * TG:(qchunk + 1) * TG],
                    pb[2][0:64, :], zbc[:, :], waits=w, inc="ev")
            a_ready = rec.n("ev")

            for oc in range(NCH):
                gi = qg * NCH + oc
                buf = gi % 2
                w = [("mm", wo_mm[gi - 2])] if gi >= 2 else []
                ldv = rec.op("gpsimd", "dma_start", out=wob4[:, buf],
                             in_=wo4[:, :, oc * 128:(oc + 1) * 128],
                             waits=w, inc="ldg", inck=16)
                wx = [("ev", xres_ev[gi - 2])] if gi >= 2 else []
                ldx = rec.op("sync", "dma_start",
                             out=xres[:, buf * TG:(buf + 1) * TG],
                             in_=x_hbm[oc * 128:(oc + 1) * 128,
                                       qc0:qc0 + TG],
                             waits=wx, inc="ld", inck=16)
                bank = pb[6 + oc % 2]
                for ic in range(NCH):
                    w3 = []
                    if ic == 0:
                        w3 = [("ldg", ldv), ("ev", a_ready)]
                        if gi >= 2:
                            w3.append(("ev", xres_ev[gi - 2]))
                    rec.op("tensor", "matmul", bank[:, :],
                           wob4[:, buf, ic, :],
                           a_sb[:, ic * TG:(ic + 1) * TG],
                           start=(ic == 0), stop=(ic == NCH - 1),
                           waits=w3, inc="mm")
                wo_mm[gi] = rec.n("mm")
                wo_mm_last = wo_mm[gi]
                wst = [("st2", wo_store[gi - 2])] if gi >= 2 else []
                xres_ev[gi] = rec.op(
                    "vector", "scalar_tensor_tensor",
                    xnew[:, buf * TG:(buf + 1) * TG], bank[:, :],
                    bo_sb[:, oc:oc + 1],
                    xres[:, buf * TG:(buf + 1) * TG], ALU.add, ALU.add,
                    waits=[("mm", wo_mm[gi]), ("ld", ldx)] + wst, inc="ev")
                wo_store[gi] = rec.op(
                    "sync", "dma_start",
                    out=xo_hbm[oc * 128:(oc + 1) * 128, qc0:qc0 + TG],
                    in_=xnew[:, buf * TG:(buf + 1) * TG],
                    waits=[("ev", xres_ev[gi])], inc="st2", inck=16)

        rec.op("gpsimd", None, waits=[("st", rec.n("st")),
                                      ("st2", rec.n("st2"))])
        if ctx is None:
            rec.replay(nc, sems)
        else:
            fence(rec, sems)
    return nc



def build_l0_ffn(debug=False, ctx=None):
    if ctx is None:
        nc = bass.Bass("TRN2", target_bir_lowering=False)
        rec = Rec()
        x_hbm = nc.dram_tensor("x_fm", [D, T], F32,
                               kind="ExternalInput").ap()
        xo_hbm = nc.dram_tensor("x_out", [D, T], F32,
                                kind="ExternalOutput").ap()
        pre = ""
    else:
        nc, rec, sems, pb, x_hbm, xo_hbm, pre = ctx
    scratch_kind = "ExternalOutput" if debug else "Internal"
    E, F = 8, 4096
    NFC = F // 128

    wr_h = nc.dram_tensor(pre + "wr", [D, E], F32, kind="ExternalInput").ap()
    w1_h = nc.dram_tensor(pre + "w1", [E, D, F], F32,
                          kind="ExternalInput").ap()
    b1_h = nc.dram_tensor(pre + "b1", [E, F], F32, kind="ExternalInput").ap()
    w2_h = nc.dram_tensor(pre + "w2", [E, F, D], F32,
                          kind="ExternalInput").ap()
    b2_h = nc.dram_tensor(pre + "b2", [E, D], F32, kind="ExternalInput").ap()
    id_h = nc.dram_tensor(pre + "idm", [128, 128], F32,
                          kind="ExternalInput").ap()
    h_hbm = nc.dram_tensor(pre + "h_scr", [D, T], BF16,
                           kind=scratch_kind).ap()
    wt_hbm = nc.dram_tensor(pre + "wt_scr", [E, T], F32,
                            kind=scratch_kind).ap()
    xa_hbm = nc.dram_tensor(pre + "xa_scr", [D, T], F32,
                            kind=scratch_kind).ap()
    if debug:
        dbg_h = nc.dram_tensor("dbg", [128, 4 * E + 4], F32,
                               kind="ExternalOutput").ap()

    es = ExitStack()
    sbt = lambda name, shape, dt: es.enter_context(
        nc.sbuf_tensor(pre + name, shape, dt))
    with es:
        w1sb = sbt("w1sb", [128, NCH * F], BF16)
        w2sb = sbt("w2sb", [128, NFC * D], BF16)
        h1_sb = sbt("h1", [128, NFC * TG], BF16)
        htile = sbt("htile", [128, 2 * NCH * TG], BF16)
        hstage = sbt("hstage", [128, 2 * TG], BF16)
        hrt = htile
        wr_sb = sbt("wrsb", [128, NCH * E], BF16)
        id_sb = sbt("idsb", [128, 128], F32)
        wtt = sbt("wtt", [8, 128], F32)
        lg_sb = sbt("lg", [128, E], F32)
        ex_sb = sbt("ex", [128, E], F32)
        wts_t = sbt("wtst", [128, E], F32)
        rmax = sbt("rmax", [128, 1], F32)
        nmax = sbt("nmax", [128, 1], F32)
        sume = sbt("sume", [128, 1], F32)
        rsum = sbt("rsum", [128, 1], F32)
        b1c = sbt("b1c", [128, NFC], F32)
        b2c = sbt("b2c", [128, NCH], F32)
        xout = sbt("xout", [128, 2 * TG], F32)
        xt = sbt("xt", [128, 2 * TG], F32)
        row = sbt("rowbuf", [1, 4 * TG], F32)
        s1 = sbt("s1", [128, TG], F32)
        s2 = sbt("s2", [128, TG], F32)
        s3 = sbt("s3", [128, TG], F32)
        wrow = row[0:1, 0:TG]
        wbc = s2[:, :]
        tf32 = s1[:, :]
        xres = xt
        onescol_f32 = sbt("ones_c", [128, 1], F32)
        onesrow_f32 = sbt("ones_r", [1, 128], F32)
        epscol = sbt("epscol", [128, 1], F32)

        if ctx is None:
            pb = [es.enter_context(nc.psum_tensor(f"pb{i}", [128, TG],
                                                  F32)) for i in range(8)]
            sems = {}
            for s in SEMNAMES:
                sems[s] = es.enter_context(nc.semaphore(s))

        rec.op("vector", "memset", onescol_f32[:], 1.0)
        rec.op("vector", "memset", epscol[:], 1e-5)
        rec.op("vector", "memset", onesrow_f32[:], 1.0)
        rec.op("sync", "dma_start", out=id_sb[:], in_=id_h, inc="ld", inck=16)
        ld_pre = rec.n("ld")
        rec.op("gpsimd", "dma_start",
               out=wr_sb[:].rearrange("p (c e) -> p c e", e=E),
               in_=wr_h.rearrange("(c p) e -> p c e", p=128),
               inc="ldg", inck=16)
        ldg_wr = rec.n("ldg")

        lnB = dict(xt=xt, sq0=s1, sq1=s2, row=row, t1=s3, pb=pb,
                   onescol_f32=onescol_f32, onesrow_f32=onesrow_f32,
                   epscol=epscol[0:1, 0:1], h_store={})
        emit_ln(rec, lnB, x_hbm, hstage, kind="ln", h_hbm=h_hbm)
        h_stores_done = rec.n("st")

        hrt4 = hrt[:, 0:2 * NCH * 128].rearrange("p (b c n) -> p b c n",
                                                 b=2, n=128)
        rt_mm = {}
        wt_store = {}
        for tt in range(NTT):
            buf = tt % 2
            w = [("st", h_stores_done)]
            if tt >= 2:
                w.append(("mm", rt_mm[tt - 2]))
            ldv = rec.op("sync", "dma_start", out=hrt4[:, buf],
                         in_=h_hbm.rearrange("(c p) t -> p c t", p=128)
                         [:, :, tt * 128:(tt + 1) * 128],
                         waits=w, inc="ld", inck=16)
            bank = pb[4]
            for ic in range(NCH):
                w2_ = [("ld", ldv), ("ldg", ldg_wr)]
                if ic == 0 and tt > 0:
                    w2_.append(("ev", rec.n("ev")))
                rec.op("tensor", "matmul", bank[:, 0:E],
                       hrt4[:, buf, ic, :], wr_sb[:, ic * E:(ic + 1) * E],
                       start=(ic == 0), stop=(ic == NCH - 1),
                       waits=w2_, inc="mm")
            rt_mm[tt] = rec.n("mm")
            cpv = rec.op("vector", "tensor_copy", lg_sb[:], bank[:, 0:E],
                         waits=[("mm", rt_mm[tt])], inc="ev")
            rxv = rec.op("vector", "tensor_reduce", rmax[:, 0:1],
                         bank[:, 0:E], mybir.AxisListType.X, ALU.max,
                         inc="ev")
            nmv = rec.op("vector", "tensor_scalar_mul", nmax[:, 0:1],
                         rmax[:, 0:1], -1.0, waits=[("ev", rxv)], inc="ev")
            nmv = rec.op("vector", "tensor_copy", s3[:, 0:256],
                         s3[:, 256:512], inc="ev")
            sv = rec.op("scalar", "activation", ex_sb[:], lg_sb[:], AF.Exp,
                        bias=nmax[:, 0:1],
                        waits=[("ev", nmv)], inc="act")
            smv = rec.op("vector", "tensor_reduce", sume[:, 0:1], ex_sb[:],
                         mybir.AxisListType.X, ALU.add, waits=[("act", sv)],
                         inc="ev")
            rsv = rec.op("vector", "reciprocal", rsum[:, 0:1], sume[:, 0:1],
                         waits=[("ev", smv)], inc="ev")
            wev = rec.op("vector", "tensor_scalar", wts_t[:], lg_sb[:],
                         rmax[:, 0:1], rsum[:, 0:1], ALU.is_equal, ALU.mult,
                         waits=[("ev", rsv)], inc="ev")
            wev = rec.op("vector", "tensor_copy", s3[:, 0:256],
                         xout[:, 0:256], inc="ev")
            wcevp = [("ev", cev_prev)] if tt > 0 else []
            tmm = rec.op("tensor", "matmul", pb[5][0:8, :128], wts_t[:],
                         id_sb[:], start=True, stop=True,
                         waits=[("ev", wev), ("ld", ld_pre)] + wcevp,
                         inc="mm")
            wstp = [("st", wt_store[tt - 1])] if tt > 0 else []
            cev = rec.op("vector", "tensor_copy", wtt[:], pb[5][0:8, :128],
                         waits=[("mm", tmm)] + wstp, inc="ev")
            cev_prev = cev
            wt_store[tt] = rec.op("gpsimd", "dma_start",
                                  out=wt_hbm[:, tt * 128:(tt + 1) * 128],
                                  in_=wtt[:], waits=[("ev", cev)],
                                  inc="st", inck=16)
            if debug and tt == 0:
                for j, buf_ in enumerate((lg_sb, ex_sb, wts_t)):
                    rec.op("gpsimd", "dma_start",
                           out=dbg_h[:, j * E:(j + 1) * E], in_=buf_[:],
                           waits=[("ev", cev)], inc="st", inck=16)
                rec.op("gpsimd", "dma_start", out=dbg_h[:, 3 * E:3 * E + 1],
                       in_=rmax[:, 0:1], inc="st", inck=16,
                       allow_slow_non_contiguous=True)
                rec.op("gpsimd", "dma_start",
                       out=dbg_h[:, 3 * E + 1:3 * E + 2],
                       in_=sume[:, 0:1], inc="st", inck=16,
                       allow_slow_non_contiguous=True)
                rec.op("gpsimd", "dma_start",
                       out=dbg_h[:, 3 * E + 2:3 * E + 3],
                       in_=rsum[:, 0:1], inc="st", inck=16,
                       allow_slow_non_contiguous=True)
                rec.op("vector", None, waits=[("st", rec.n("st"))])
        wt_done = rec.n("st")
        rt_done = rec.n("mm")

        htile4 = htile[:].rearrange("p (b c n) -> p b c n", b=2, n=TG)
        b13 = b1_h.rearrange("e (fc p o) -> e fc p o", p=128, o=1)
        b23 = b2_h.rearrange("e (c p o) -> e c p o", p=128, o=1)
        exp_mm_done = rt_done
        acc_w = {}
        xres_ev = {}
        out_st = {}
        ht_ld_guard = {}
        prev_relu_last = 0
        last_tg_ev = 0
        h1_last_mm = 0
        for e in range(E):
            wld1 = [("mm", h1_last_mm)] if e > 0 else []
            for ic in range(NCH):
                rec.op("gpsimd", "dma_start",
                       out=w1sb[:, ic * F:(ic + 1) * F],
                       in_=w1_h[e, ic * 128:(ic + 1) * 128, :],
                       waits=wld1 if ic == 0 else (), inc="ldg", inck=16)
            w1load = rec.n("ldg")
            wld = [("mm", exp_mm_done)]
            for fc in range(NFC):
                rec.op("gpsimd", "dma_start",
                       out=w2sb[:, fc * D:(fc + 1) * D],
                       in_=w2_h[e, fc * 128:(fc + 1) * 128, :],
                       waits=wld if fc == 0 else (), inc="ldg", inck=16)
            wload = rec.n("ldg")
            bias_guard = list(wld)
            if e > 0:
                bias_guard.append(("act", prev_relu_last))
                bias_guard.append(("ev", rec.n("ev")))
            for fc in range(NFC):
                rec.op("sync", "dma_start", out=b1c[:, fc:fc + 1],
                       in_=b13[e, fc], waits=bias_guard if fc == 0 else (),
                       inc="ld", inck=16)
            for c in range(NCH):
                rec.op("sync", "dma_start", out=b2c[:, c:c + 1],
                       in_=b23[e, c], inc="ld", inck=16)
            bias_ld = rec.n("ld")

            for tg in range(NTG):
                ei = e * NTG + tg
                buf = ei % 2
                qc0 = tg * TG
                w = [("st", h_stores_done)]
                if ei >= 2:
                    w.append(("mm", ht_ld_guard[ei - 2]))
                else:
                    w.append(("mm", rt_done))
                ldh = rec.op("sync", "dma_start", out=htile4[:, buf],
                             in_=h_hbm.rearrange("(c p) t -> p c t", p=128)
                             [:, :, qc0:qc0 + TG],
                             waits=w, inc="ld", inck=16)
                wbw = [("st", wt_done)]
                if last_tg_ev:
                    wbw.append(("ev", last_tg_ev))
                ldwbc = rec.op("sync", "dma_start", out=wbc,
                               in_=wt_hbm[e:e + 1, qc0:qc0 + TG]
                               .broadcast_to([128, TG]),
                               waits=wbw, inc="ld", inck=16)
                relu_after = {}
                for fc in range(NFC):
                    bank = pb[fc % 2]
                    for ic in range(NCH):
                        w3 = []
                        if ic == 0:
                            w3 = [("ld", max(ldh, bias_ld)),
                                  ("ldg", w1load)]
                            if fc >= 2:
                                w3.append(("act", relu_after[fc - 2]))
                            elif ei > 0:
                                w3.append(("act", prev_relu_last))
                        rec.op("tensor", "matmul", bank[:, :],
                               w1sb[:, ic * F + fc * 128:
                                    ic * F + fc * 128 + 128],
                               htile4[:, buf, ic, :],
                               start=(ic == 0), stop=(ic == NCH - 1),
                               waits=w3, inc="mm")
                    relu_after[fc] = rec.op(
                        "scalar", "activation",
                        h1_sb[:, fc * TG:(fc + 1) * TG], bank[:, :],
                        AF.Relu, bias=b1c[:, fc:fc + 1],
                        waits=[("mm", rec.n("mm"))], inc="act")
                prev_relu_last = relu_after[NFC - 1]
                h1_mm_last = rec.n("mm")
                for oc in range(NCH):
                    gi2 = ei * NCH + oc
                    buf2 = gi2 % 2
                    bank = pb[2 + oc % 2]
                    for fc in range(NFC):
                        w4 = []
                        if fc == 0:
                            w4 = [("act", prev_relu_last),
                                  ("ldg", wload)]
                            if gi2 >= 2:
                                w4.append(("ev", xres_ev[gi2 - 2]))
                        rec.op("tensor", "matmul", bank[:, :],
                               w2sb[:, fc * D + oc * 128:
                                    fc * D + oc * 128 + 128],
                               h1_sb[:, fc * TG:(fc + 1) * TG],
                               start=(fc == 0), stop=(fc == NFC - 1),
                               waits=w4, inc="mm")
                    ymm = rec.n("mm")
                    wx = []
                    if gi2 >= 2:
                        wx.append(("ev", xres_ev[gi2 - 2]))
                    if e > 0:
                        wx.append(("st", acc_w[(oc, tg)]))
                    src = x_hbm if e == 0 else xa_hbm
                    ldx = rec.op("sync", "dma_start",
                                 out=xres[:, buf2 * TG:(buf2 + 1) * TG],
                                 in_=src[oc * 128:(oc + 1) * 128,
                                         qc0:qc0 + TG],
                                 waits=wx, inc="ld", inck=16)
                    rec.op("vector", "scalar_tensor_tensor", tf32,
                           bank[:, :], b2c[:, oc:oc + 1], wbc,
                           ALU.add, ALU.mult,
                           waits=[("mm", ymm), ("ld", max(bias_ld, ldwbc))])
                    wst = []
                    if gi2 >= 2:
                        wst.append(("st", out_st[gi2 - 2]))
                    xres_ev[gi2] = rec.op(
                        "vector", "tensor_add",
                        xout[:, buf2 * TG:(buf2 + 1) * TG], tf32,
                        xres[:, buf2 * TG:(buf2 + 1) * TG],
                        waits=[("ld", ldx)] + wst, inc="ev")
                    dst = xo_hbm if e == E - 1 else xa_hbm
                    out_st[gi2] = rec.op(
                        "gpsimd", "dma_start",
                        out=dst[oc * 128:(oc + 1) * 128, qc0:qc0 + TG],
                        in_=xout[:, buf2 * TG:(buf2 + 1) * TG],
                        waits=[("ev", xres_ev[gi2])], inc="st", inck=16)
                    acc_w[(oc, tg)] = out_st[gi2]
                ht_ld_guard[ei] = h1_mm_last
                last_tg_ev = rec.n("ev")
                h1_last_mm = h1_mm_last
            exp_mm_done = rec.n("mm")

        rec.op("gpsimd", None, waits=[("st", rec.n("st"))])
        if ctx is None:
            rec.replay(nc, sems)
        else:
            fence(rec, sems)
    return nc


def run_l0_ffn(x_fm_list, ins, debug=False, ncores=NCORES):
    nc = build_l0_ffn(debug=debug)
    shared = dict(
        wr=ins["L0_ffn_w_router"], w1=ins["L0_ffn_w1"], b1=ins["L0_ffn_b1"],
        w2=ins["L0_ffn_w2"], b2=ins["L0_ffn_b2"],
        idm=np.eye(128, dtype=np.float32),
    )
    in_maps = []
    for b in range(ncores):
        m = dict(shared)
        m["x_fm"] = x_fm_list[b]
        in_maps.append(m)
    res = run_bass_kernel_spmd(nc, in_maps, core_ids=list(range(ncores)))
    if debug:
        return res.results
    return [r["x_out"] for r in res.results]



def build_l1_attn(ctx=None):
    if ctx is None:
        nc = bass.Bass("TRN2", target_bir_lowering=False)
        rec = Rec()
        x_hbm = nc.dram_tensor("x_fm", [D, T], F32,
                               kind="ExternalInput").ap()
        xo_hbm = nc.dram_tensor("x_out", [D, T], F32,
                                kind="ExternalOutput").ap()
        pre = ""
    else:
        nc, rec, sems, pb, x_hbm, xo_hbm, pre = ctx
    C_GATE = 8.0

    win_h = nc.dram_tensor(pre + "WinT", [D, D], F32,
                           kind="ExternalInput").ap()
    wig_h = nc.dram_tensor(pre + "WigT", [D, D], F32,
                           kind="ExternalInput").ap()
    wrg_h = nc.dram_tensor(pre + "WrgT", [D, D], F32,
                           kind="ExternalInput").ap()
    wout_h = nc.dram_tensor(pre + "WoutT", [D, D], F32,
                            kind="ExternalInput").ap()
    big_h = nc.dram_tensor(pre + "big", [D], F32, kind="ExternalInput").ap()
    brg_h = nc.dram_tensor(pre + "brg", [D], F32, kind="ExternalInput").ap()
    lam_h = nc.dram_tensor(pre + "lam", [D], F32, kind="ExternalInput").ap()
    h0_h = nc.dram_tensor(pre + "h0v", [D], F32, kind="ExternalInput").ap()
    nw_h = nc.dram_tensor(pre + "nw", [D], F32, kind="ExternalInput").ap()

    es = ExitStack()
    sbt = lambda name, shape, dt: es.enter_context(
        nc.sbuf_tensor(pre + name, shape, dt))
    with es:
        h_sb = sbt("h", [128, NCH * T], BF16)
        hs_sb = sbt("hs", [128, NCH * T], BF16)
        a_fm = sbt("afm", [128, T], F32)
        g_fm = sbt("gfm", [128, T], F32)
        wblk = sbt("wblk", [128, 2 * 3 * NCH * 128], BF16)
        woblk = sbt("woblk", [128, 2 * NCH * 128], BF16)
        big_c = sbt("bigc", [128, NCH], F32)
        brg_c = sbt("brgc", [128, NCH], F32)
        lam_c = sbt("lamc", [128, NCH], F32)
        clog_c = sbt("clogc", [128, NCH], F32)
        h0_c = sbt("h0c", [128, NCH], F32)
        nw_c = sbt("nwc", [128, NCH], F32)
        xt = sbt("xt", [128, 2 * TG], F32)
        row = sbt("rowbuf", [1, 4 * TG], F32)
        s1 = sbt("s1", [128, TG], F32)
        s2 = sbt("s2", [128, TG], F32)
        s3 = sbt("s3", [128, TG], F32)
        rbf = sbt("rbf", [128, TG], BF16)
        xres = sbt("xres", [128, 2 * TG], F32)
        onescol_f32 = sbt("ones_c", [128, 1], F32)
        onesrow_f32 = sbt("ones_r", [1, 128], F32)
        epscol = sbt("epscol", [128, 1], F32)
        eps2col = sbt("eps2col", [128, 1], F32)
        xout = xt

        if ctx is None:
            pb = [es.enter_context(nc.psum_tensor(f"pb{i}", [128, TG],
                                                  F32)) for i in range(8)]
            sems = {}
            for s in SEMNAMES:
                sems[s] = es.enter_context(nc.semaphore(s))

        rec.op("vector", "memset", onescol_f32[:], 1.0)
        rec.op("vector", "memset", epscol[:], 1e-5)
        rec.op("vector", "memset", eps2col[:], 1e-6)
        rec.op("vector", "memset", onesrow_f32[:], 1.0)

        big3 = big_h.rearrange("(c p o) -> c p o", p=128, o=1)
        brg3 = brg_h.rearrange("(c p o) -> c p o", p=128, o=1)
        lam3 = lam_h.rearrange("(c p o) -> c p o", p=128, o=1)
        h03 = h0_h.rearrange("(c p o) -> c p o", p=128, o=1)
        nw3 = nw_h.rearrange("(c p o) -> c p o", p=128, o=1)
        for c in range(NCH):
            rec.op("sync", "dma_start", out=big_c[:, c:c + 1], in_=big3[c],
                   inc="ld", inck=16)
            rec.op("sync", "dma_start", out=brg_c[:, c:c + 1], in_=brg3[c],
                   inc="ld", inck=16)
            rec.op("sync", "dma_start", out=lam_c[:, c:c + 1], in_=lam3[c],
                   inc="ld", inck=16)
            rec.op("sync", "dma_start", out=h0_c[:, c:c + 1], in_=h03[c],
                   inc="ld", inck=16)
            rec.op("sync", "dma_start", out=nw_c[:, c:c + 1], in_=nw3[c],
                   inc="ld", inck=16)
        ld_par = rec.n("ld")
        sg = rec.op("scalar", "activation", clog_c[:], lam_c[:], AF.Sigmoid,
                    waits=[("ld", ld_par)], inc="act")
        sg = rec.op("scalar", "activation", s2[:, 0:256], s2[:, 0:256],
                    AF.Square, inc="act")
        ad = rec.op("vector", "tensor_scalar_add", clog_c[:], clog_c[:],
                    1e-8, waits=[("act", sg)], inc="ev")
        ad = rec.op("vector", "tensor_copy", s3[:, 0:256], s3[:, 256:512],
                    inc="ev")
        ln_ = rec.op("scalar", "activation", clog_c[:], clog_c[:], AF.Ln,
                     waits=[("ev", ad)], inc="act")
        ln_ = rec.op("scalar", "activation", s2[:, 0:256], s2[:, 0:256],
                     AF.Square, inc="act")
        cl = rec.op("vector", "tensor_scalar_mul", clog_c[:], clog_c[:],
                    C_GATE, waits=[("act", ln_)], inc="ev")
        cl = rec.op("vector", "tensor_copy", s3[:, 0:256], s3[:, 256:512],
                    inc="ev")
        ev_clog = cl

        emit_ln(rec, dict(xt=xt, sq0=s1, sq1=s2, row=row, t1=s3, pb=pb,
                          onescol_f32=onescol_f32, onesrow_f32=onesrow_f32,
                          epscol=epscol[0:1, 0:1]),
                x_hbm, h_sb, kind="ln")
        ln_done = [("ln_h", rec.n("ln_h"))]

        win4 = win_h.rearrange("(ic p) o -> p ic o", p=128)
        wig4 = wig_h.rearrange("(ic p) o -> p ic o", p=128)
        wrg4 = wrg_h.rearrange("(ic p) o -> p ic o", p=128)
        wout4 = wout_h.rearrange("(ic p) o -> p ic o", p=128)
        wblk6 = wblk[:].rearrange("p (b w ic o) -> p b w ic o", b=2, w=3,
                                  o=128)
        woblk4 = woblk[:].rearrange("p (b ic o) -> p b ic o", b=2, o=128)

        scan_ev = {}
        wld_mm = {}
        prev_oc_gate_ev = 0
        for oc in range(NCH):
            buf = oc % 2
            w = [("mm", wld_mm[oc - 2])] if oc >= 2 else []
            ldv = rec.op("gpsimd", "dma_start", out=wblk6[:, buf, 0],
                         in_=win4[:, :, oc * 128:(oc + 1) * 128],
                         waits=w, inc="ldg", inck=16)
            ldv = rec.op("gpsimd", "dma_start", out=wblk6[:, buf, 1],
                         in_=wig4[:, :, oc * 128:(oc + 1) * 128],
                         inc="ldg", inck=16)
            ldv = rec.op("gpsimd", "dma_start", out=wblk6[:, buf, 2],
                         in_=wrg4[:, :, oc * 128:(oc + 1) * 128],
                         inc="ldg", inck=16)
            gate_ev_prev = 0
            gate_ev_tg = {}
            for tg in range(NTG):
                c0 = tg * TG
                pxp = pb[tg % 2]
                pi = pb[2 + tg % 2]
                pr = pb[4 + tg % 2]
                for w_i, bank in ((0, pxp), (1, pi), (2, pr)):
                    for ic in range(NCH):
                        w3 = []
                        if ic == 0:
                            w3 = [("ldg", ldv)] + ln_done
                            if tg >= 2:
                                w3.append(("ev", gate_ev_tg[tg - 2]))
                            elif prev_oc_gate_ev:
                                w3.append(("ev", prev_oc_gate_ev))
                        rec.op("tensor", "matmul", bank[:, :],
                               wblk6[:, buf, w_i, ic, :],
                               h_sb[:, ic * T + c0: ic * T + c0 + TG],
                               start=(ic == 0), stop=(ic == NCH - 1),
                               waits=w3, inc="mm")
                mmv = rec.n("mm")
                wg_ = [("mm", mmv), ("ev", ev_clog)]
                if tg == 0 and oc >= 1:
                    wg_.append(("ev", scan_ev[oc - 1]))
                rv_ = rec.op("scalar", "activation", s1[:, :], pr[:, :],
                             AF.Sigmoid, bias=brg_c[:, oc:oc + 1],
                             waits=wg_, inc="act")
                mv = rec.op("vector", "tensor_scalar_mul", s2[:, :],
                            s1[:, :], clog_c[:, oc:oc + 1],
                            waits=[("act", rv_)], inc="ev")
                av_ = rec.op("scalar", "activation",
                             a_fm[:, c0:c0 + TG], s2[:, :], AF.Exp,
                             waits=[("ev", mv)], inc="act")
                iv_ = rec.op("scalar", "activation", s3[:, :], pi[:, :],
                             AF.Sigmoid, bias=big_c[:, oc:oc + 1],
                             waits=[("mm", mmv)], inc="act")
                t1v = rec.op("vector", "tensor_mul", s1[:, :],
                             a_fm[:, c0:c0 + TG], a_fm[:, c0:c0 + TG],
                             waits=[("act", av_)])
                rec.op("vector", "tensor_scalar", s1[:, :], s1[:, :],
                       -1.0, 1.000001, ALU.mult, ALU.add, inc="ev")
                sq_ = rec.op("scalar", "activation", s1[:, :], s1[:, :],
                             AF.Sqrt, waits=[("ev", rec.n("ev"))],
                             inc="act")
                rec.op("vector", "tensor_mul", s2[:, :], s3[:, :],
                       pxp[:, :], waits=[("act", sq_)])
                gate_ev_prev = rec.op("vector", "tensor_mul",
                                      g_fm[:, c0:c0 + TG], s1[:, :],
                                      s2[:, :], inc="ev")
                gate_ev_tg[tg] = gate_ev_prev
            wld_mm[oc] = rec.n("mm")
            prev_oc_gate_ev = gate_ev_prev
            scan_ev[oc] = rec.op("vector", "tensor_tensor_scan",
                                 hs_sb[:, oc * T:(oc + 1) * T],
                                 a_fm[:, :], g_fm[:, :],
                                 h0_c[:, oc:oc + 1], ALU.mult, ALU.add,
                                 waits=[("ev", gate_ev_prev),
                                        ("ld", ld_par)], inc="ev")
        scan_all = rec.n("ev")

        rms_mm = {}
        prev_rows = 0
        hn_after = {}
        for tg in range(NTG):
            c0 = tg * TG
            for c in range(NCH):
                w = [("ev", scan_all)]
                if (tg, c - 2) in rms_mm:
                    w.append(("ln_mm", rms_mm[(tg, c - 2)]))
                elif tg > 0 and (tg - 1, NCH - 2 + c) in rms_mm:
                    w.append(("ln_mm", rms_mm[(tg - 1, NCH - 2 + c)]))
                sqv = rec.op("scalar", "activation",
                             s1[:, :] if c % 2 == 0 else s2[:, :],
                             hs_sb[:, c * T + c0: c * T + c0 + TG],
                             AF.Square, waits=w, inc="ln_sq")
                w0 = [("ln_sq", sqv)]
                if c == 0 and prev_rows:
                    w0.append(("ln_ev", prev_rows))
                rec.op("tensor", "matmul", pb[1][0:1, :],
                       onescol_f32[:, 0:1],
                       s1[:, :] if c % 2 == 0 else s2[:, :],
                       start=(c == 0), stop=(c == NCH - 1), waits=w0,
                       inc="ln_mm")
                rms_mm[(tg, c)] = rec.n("ln_mm")
            mm_stats = rec.n("ln_mm")
            vrow = row[0:1, 2 * TG:3 * TG]
            rrow = row[0:1, 3 * TG:4 * TG]
            rec.op("vector", "tensor_scalar_mul", vrow, pb[1][0:1, :],
                   1.0 / D, waits=[("ln_mm", mm_stats)], inc="ln_ev")
            sv = rec.op("scalar", "activation", vrow, vrow, AF.Sqrt,
                        bias=eps2col[0:1, 0:1],
                        waits=[("ln_ev", rec.n("ln_ev"))], inc="ln_sq")
            rvv = rec.op("vector", "reciprocal", rrow, vrow,
                         waits=[("ln_sq", sv)], inc="ln_ev")
            rvv = rec.op("vector", "tensor_copy", s3[:, 0:256],
                         s3[:, 256:512], inc="ln_ev")
            prev_rows = rvv
            wbc_ = [("ln_ev", rvv)]
            if tg > 0:
                wbc_.append(("ln_h", rec.n("ln_h")))
            rec.op("tensor", "matmul", pb[3][:, :], onesrow_f32[0:1, :],
                   rrow, start=True, stop=True, waits=wbc_, inc="ln_mm")
            mm_bc = rec.n("ln_mm")
            rec.op("vector", "tensor_copy", rbf[:, :], pb[3][:, :],
                   waits=[("ln_mm", mm_bc)], inc="ln_ev")
            for c in range(NCH):
                hn_after[(tg, c)] = rec.op(
                    "vector", "tensor_mul",
                    h_sb[:, c * T + c0: c * T + c0 + TG],
                    hs_sb[:, c * T + c0: c * T + c0 + TG], rbf[:, :],
                    inc="ln_h")
        hn_done = rec.n("ln_h")

        wo_mm = {}
        xres_ev = {}
        wo_store = {}
        woblk_ev = {}
        for qg in range(NTG):
            qc0 = qg * TG
            for oc in range(NCH):
                gi = qg * NCH + oc
                bufw = gi % 2
                w = [("mm", wo_mm[gi - 2])] if gi >= 2 else []
                ldv = rec.op("gpsimd", "dma_start", out=woblk4[:, bufw],
                             in_=wout4[:, :, oc * 128:(oc + 1) * 128],
                             waits=w, inc="ldg", inck=16)
                for ic in range(NCH):
                    wv = [("ldg", ldv), ("ld", ld_par)] if ic == 0 else ()
                    woblk_ev[gi] = rec.op(
                        "vector", "tensor_scalar_mul",
                        woblk4[:, bufw, ic, :], woblk4[:, bufw, ic, :],
                        nw_c[:, ic:ic + 1], waits=wv, inc="ev")
                wx = [("ev", xres_ev[gi - 2])] if gi >= 2 else []
                ldx = rec.op("sync", "dma_start",
                             out=xres[:, bufw * TG:(bufw + 1) * TG],
                             in_=x_hbm[oc * 128:(oc + 1) * 128,
                                       qc0:qc0 + TG],
                             waits=wx, inc="ld", inck=16)
                bank = pb[6 + oc % 2]
                for ic in range(NCH):
                    w3 = []
                    if ic == 0:
                        w3 = [("ev", woblk_ev[gi]),
                              ("ln_h", hn_done)]
                        if gi >= 2:
                            w3.append(("ev", xres_ev[gi - 2]))
                    rec.op("tensor", "matmul", bank[:, :],
                           woblk4[:, bufw, ic, :],
                           h_sb[:, ic * T + qc0: ic * T + qc0 + TG],
                           start=(ic == 0), stop=(ic == NCH - 1),
                           waits=w3, inc="mm")
                wo_mm[gi] = rec.n("mm")
                wst = [("st2", wo_store[gi - 2])] if gi >= 2 else []
                xres_ev[gi] = rec.op(
                    "vector", "tensor_add",
                    xout[:, bufw * TG:(bufw + 1) * TG], bank[:, :],
                    xres[:, bufw * TG:(bufw + 1) * TG],
                    waits=[("mm", wo_mm[gi]), ("ld", ldx)] + wst,
                    inc="ev")
                wo_store[gi] = rec.op(
                    "sync", "dma_start",
                    out=xo_hbm[oc * 128:(oc + 1) * 128, qc0:qc0 + TG],
                    in_=xout[:, bufw * TG:(bufw + 1) * TG],
                    waits=[("ev", xres_ev[gi])], inc="st2", inck=16)

        rec.op("gpsimd", None, waits=[("st", rec.n("st")),
                                      ("st2", rec.n("st2"))])
        if ctx is None:
            rec.replay(nc, sems)
        else:
            fence(rec, sems)
    return nc


def run_l1_attn(x_fm_list, ins):
    nc = build_l1_attn()
    shared = dict(
        WinT=np.ascontiguousarray(ins["L1_attn_W_in"].T),
        WigT=np.ascontiguousarray(ins["L1_attn_W_ig"].T),
        WrgT=np.ascontiguousarray(ins["L1_attn_W_rg"].T),
        WoutT=np.ascontiguousarray(ins["L1_attn_W_out"].T),
        big=ins["L1_attn_b_ig"], brg=ins["L1_attn_b_rg"],
        lam=ins["L1_attn_lam"],
        h0v=np.ascontiguousarray(ins["L1_attn_h0"].reshape(-1)),
        nw=ins["L1_attn_norm_w"],
    )
    in_maps = []
    for b in range(NCORES):
        m = dict(shared)
        m["x_fm"] = x_fm_list[b]
        in_maps.append(m)
    res = run_bass_kernel_spmd(nc, in_maps, core_ids=list(range(NCORES)))
    return [r["x_out"] for r in res.results]



def build_l1_ffn(debug=False, ctx=None):
    if ctx is None:
        nc = bass.Bass("TRN2", target_bir_lowering=False)
        rec = Rec()
        x_hbm = nc.dram_tensor("x_fm", [D, T], F32,
                               kind="ExternalInput").ap()
        xo_hbm = nc.dram_tensor("x_out", [D, T], F32,
                                kind="ExternalOutput").ap()
        pre = ""
    else:
        nc, rec, sems, pb, x_hbm, xo_hbm, pre = ctx
    sck = "ExternalOutput" if debug else "Internal"
    H16, DH, DC, HID, E = 16, 64, 256, 2048, 8
    NHC = HID // 128

    wq_h = nc.dram_tensor(pre + "WqP", [D, D], F32,
                          kind="ExternalInput").ap()
    wkc_h = nc.dram_tensor(pre + "Wkc", [D, DC], F32,
                           kind="ExternalInput").ap()
    wvc_h = nc.dram_tensor(pre + "Wvc", [D, DC], F32,
                           kind="ExternalInput").ap()
    wk_h = nc.dram_tensor(pre + "WkP", [DC, D], F32,
                          kind="ExternalInput").ap()
    wv_h = nc.dram_tensor(pre + "Wv", [DC, D], F32,
                          kind="ExternalInput").ap()
    wo_h = nc.dram_tensor(pre + "Wo", [D, D], F32,
                          kind="ExternalInput").ap()
    wsi_h = nc.dram_tensor(pre + "Wsi", [D, 2 * HID], F32,
                           kind="ExternalInput").ap()
    wso_h = nc.dram_tensor(pre + "Wso", [HID, D], F32,
                           kind="ExternalInput").ap()
    w1_h = nc.dram_tensor(pre + "W1s", [D, 2 * HID], F32,
                          kind="ExternalInput").ap()
    w2_h = nc.dram_tensor(pre + "W2e", [E, HID, D], F32,
                          kind="ExternalInput").ap()
    wg_h = nc.dram_tensor(pre + "Wg", [D, 2], F32,
                          kind="ExternalInput").ap()
    we_h = nc.dram_tensor(pre + "We", [D, E], F32,
                          kind="ExternalInput").ap()
    gb_h = nc.dram_tensor(pre + "gb", [1, 2], F32,
                          kind="ExternalInput").ap()
    eb_h = nc.dram_tensor(pre + "eb", [1, E], F32,
                          kind="ExternalInput").ap()
    n1w_h = nc.dram_tensor(pre + "n1w", [D], F32,
                           kind="ExternalInput").ap()
    n2w_h = nc.dram_tensor(pre + "n2w", [D], F32,
                           kind="ExternalInput").ap()
    cos_h = nc.dram_tensor(pre + "cos2", [128, T], F32,
                           kind="ExternalInput").ap()
    ss_h = nc.dram_tensor(pre + "ss2", [128, T], F32,
                          kind="ExternalInput").ap()
    id_h = nc.dram_tensor(pre + "idm", [128, 128], F32,
                          kind="ExternalInput").ap()

    src_hbm = nc.dram_tensor(pre + "src_scr", [D, T], BF16, kind=sck).ap()
    a_hbm = nc.dram_tensor(pre + "a_scr", [D, T], BF16, kind=sck).ap()
    xx_hbm = nc.dram_tensor(pre + "xx_scr", [D, T], BF16, kind=sck).ap()
    wt_hbm = nc.dram_tensor(pre + "wt_scr", [E, T], F32, kind=sck).ap()
    if debug:
        dbg_h = nc.dram_tensor("dbg", [128, 64], F32,
                               kind="ExternalOutput").ap()

    src3 = src_hbm.rearrange("(c p) t -> p c t", p=128)
    a3 = a_hbm.rearrange("(c p) t -> p c t", p=128)
    xx3 = xx_hbm.rearrange("(c p) t -> p c t", p=128)

    es = ExitStack()
    sbt = lambda name, shape, dt: es.enter_context(
        nc.sbuf_tensor(pre + name, shape, dt))
    with es:
        xt = sbt("xt", [128, 2 * TG], F32)
        row = sbt("rowbuf", [1, 4 * TG], F32)
        s1 = sbt("s1", [128, TG], F32)
        s2 = sbt("s2", [128, TG], F32)
        s3 = sbt("s3", [128, TG], F32)
        onescol_f32 = sbt("ones_c", [128, 1], F32)
        onesrow_f32 = sbt("ones_r", [1, 128], F32)
        onesrow_bf = sbt("ones_rb", [1, 128], BF16)
        ones64_f32 = sbt("ones64", [1, 64], F32)
        epscol = sbt("epscol", [128, 1], F32)
        eps2col = sbt("eps2col", [128, 1], F32)
        hstage = sbt("hstage", [128, 2 * TG], BF16)

        if ctx is None:
            pb = [es.enter_context(nc.psum_tensor(f"pb{i}", [128, TG],
                                                  F32)) for i in range(8)]
            sems = {}
            for s in SEMNAMES:
                sems[s] = es.enter_context(nc.semaphore(s))

        rec.op("vector", "memset", onescol_f32[:], 1.0)
        rec.op("vector", "memset", epscol[:], 1e-5)
        rec.op("vector", "memset", eps2col[:], 1e-6)
        rec.op("vector", "memset", onesrow_f32[:], 1.0)
        rec.op("vector", "memset", onesrow_bf[:], 1.0)
        rec.op("vector", "memset", ones64_f32[:], 1.0)

        lnB = dict(xt=xt, sq0=s1, sq1=s2, row=row, t1=s3, pb=pb,
                   onescol_f32=onescol_f32, onesrow_f32=onesrow_f32,
                   epscol=epscol[0:1, 0:1], h_store={})
        emit_ln(rec, lnB, x_hbm, hstage, kind="ln", h_hbm=src_hbm)
        src_done = rec.n("st")

        esKV = ExitStack()
        kc_sb = esKV.enter_context(
            nc.sbuf_tensor(pre + "kc", [128, 2 * T], BF16))
        vc_sb = esKV.enter_context(
            nc.sbuf_tensor(pre + "vc", [128, 2 * T], BF16))
        with ExitStack() as esB:
            sbB = lambda n, s_, d: esB.enter_context(nc.sbuf_tensor(pre + n, s_, d))
            srct = sbB("srctB", [128, 2 * NCH * TG], BF16)
            wkc_sb = sbB("wkcsb", [128, NCH * DC], BF16)
            wvc_sb = sbB("wvcsb", [128, NCH * DC], BF16)
            srct4 = srct[:].rearrange("p (b c n) -> p b c n", b=2, n=TG)
            rec.op("gpsimd", "dma_start",
                   out=wkc_sb[:].rearrange("p (c n) -> p c n", n=DC),
                   in_=wkc_h.rearrange("(c p) n -> p c n", p=128),
                   inc="ldg", inck=16)
            rec.op("gpsimd", "dma_start",
                   out=wvc_sb[:].rearrange("p (c n) -> p c n", n=DC),
                   in_=wvc_h.rearrange("(c p) n -> p c n", p=128),
                   inc="ldg", inck=16)
            ldg_w = rec.n("ldg")
            evac_ev = {}
            ld_guard = {}
            for tg in range(NTG):
                c0 = tg * TG
                buf = tg % 2
                w = [("st", src_done)]
                if tg >= 2:
                    w.append(("mm", ld_guard[tg - 2]))
                ldv = rec.op("sync", "dma_start", out=srct4[:, buf],
                             in_=src3[:, :, c0:c0 + TG], waits=w,
                             inc="ld", inck=16)
                for kind_i, (bank0, wsb, dst) in enumerate(
                        ((0, wkc_sb, kc_sb), (2, wvc_sb, vc_sb))):
                    for kchunk in range(2):
                        bank = pb[bank0 + kchunk]
                        for ic in range(NCH):
                            w3 = []
                            if ic == 0:
                                w3 = [("ldg", ldg_w), ("ld", ldv)]
                                if (tg - 2, kind_i, kchunk) in evac_ev:
                                    w3.append(("ev", evac_ev[(tg - 2,
                                                              kind_i,
                                                              kchunk)]))
                            rec.op("tensor", "matmul", bank[:, :],
                                   wsb[:, ic * DC + kchunk * 128:
                                       ic * DC + kchunk * 128 + 128],
                                   srct4[:, buf, ic, :],
                                   start=(ic == 0), stop=(ic == NCH - 1),
                                   waits=w3, inc="mm")
                        evac_ev[(tg, kind_i, kchunk)] = rec.op(
                            "vector", "tensor_copy",
                            dst[:, kchunk * T + c0: kchunk * T + c0 + TG],
                            bank[:, :], waits=[("mm", rec.n("mm"))],
                            inc="ev")
                ld_guard[tg] = rec.n("mm")
            kcvc_ev = rec.n("ev")
        fence(rec, sems)

        for half in range(2):
            hbase = half * 8
            cbase = half * 4
            with ExitStack() as esH:
                sbH = lambda n, s_, d, _h=half: esH.enter_context(
                    nc.sbuf_tensor(f"{pre}{n}_{_h}", s_, d))
                q_sb = sbH("qh", [128, 4 * T], BF16)
                k_sb = sbH("kh", [128, 4 * T], BF16)
                v_sb = sbH("vh", [128, NTT * 8 * (DH + 1)], BF16)
                cos_sb = sbH("cosb", [128, T], BF16)
                ss_sb = sbH("ssb", [128, T], BF16)
                wqb = sbH("wqb", [128, 2 * NCH * 128], BF16)
                wk_sb = sbH("wkh", [128, 2 * 512], BF16)
                wv_sb = sbH("wvh", [128, 2 * 512], BF16)
                srct = sbH("srctH", [128, 2 * NCH * TG], BF16)
                ebf = sbH("ebf", [128, TG], BF16)
                m1b = sbH("m1b", [128, TG], BF16)
                m2b = sbH("m2b", [128, TG], BF16)
                u_sb = sbH("u", [128, 6 * TG], BF16)
                zr_sb = sbH("zr", [1, TG], F32)
                zbc = sbH("zbc", [64, TG], F32)
                a_sb = sbH("ah", [128, 4 * TG], BF16)
                srct4 = srct[:].rearrange("p (b c n) -> p b c n", b=2, n=TG)
                wqb4 = wqb[:].rearrange("p (b ic o) -> p b ic o", b=2,
                                        o=128)
                v4 = v_sb[:].rearrange("p (tt h d) -> p tt h d", h=8,
                                       d=DH + 1)

                rec.op("gpsimd", "dma_start", out=cos_sb[:], in_=cos_h,
                       inc="ldg", inck=16)
                rec.op("gpsimd", "dma_start", out=ss_sb[:], in_=ss_h,
                       inc="ldg", inck=16)
                rec.op("gpsimd", "dma_start",
                       out=wk_sb[:].rearrange("p (c n) -> p c n", n=512),
                       in_=wk_h.rearrange("(c p) n -> p c n", p=128)
                       [:, :, hbase * 64:(hbase + 8) * 64],
                       inc="ldg", inck=16)
                rec.op("gpsimd", "dma_start",
                       out=wv_sb[:].rearrange("p (c n) -> p c n", n=512),
                       in_=wv_h.rearrange("(c p) n -> p c n", p=128)
                       [:, :, hbase * 64:(hbase + 8) * 64],
                       inc="ldg", inck=16)
                ldg_h = rec.n("ldg")
                rec.op("vector", "memset", v4[:, :, :, DH:DH + 1], 1.0,
                       inc="ev")

                def rope_bf(ps_ap, out_ap, tc0, waits_):
                    rec.op("vector", "tensor_copy", ebf[:, :], ps_ap,
                           waits=list(waits_))
                    rec.op("vector", "tensor_mul", m1b[:, :], ebf[:, :],
                           cos_sb[:, tc0:tc0 + TG])
                    for b_ in range(4):
                        pk = b_ ^ 1
                        rec.op("vector", "tensor_mul",
                               m2b[32 * b_:32 * b_ + 32, :],
                               ebf[32 * pk:32 * pk + 32, :],
                               ss_sb[32 * pk:32 * pk + 32, tc0:tc0 + TG])
                    return rec.op("vector", "tensor_add", out_ap,
                                  m1b[:, :], m2b[:, :], inc="ev")

                kv_ev = {}
                g = 0
                for oc in range(4):
                    for tg in range(NTG):
                        c0 = tg * TG
                        bank = pb[g % 2]
                        for ic in range(2):
                            w3 = []
                            if ic == 0:
                                w3 = [("ldg", ldg_h), ("ev", kcvc_ev)]
                                if g >= 2:
                                    w3.append(("ev", kv_ev[g - 2]))
                            rec.op("tensor", "matmul", bank[:, :],
                                   wk_sb[:, ic * 512 + oc * 128:
                                         ic * 512 + oc * 128 + 128],
                                   kc_sb[:, ic * T + c0: ic * T + c0 + TG],
                                   start=(ic == 0), stop=(ic == 1),
                                   waits=w3, inc="mm")
                        kv_ev[g] = rope_bf(
                            bank[:, :],
                            k_sb[:, oc * T + c0: oc * T + c0 + TG], c0,
                            [("mm", rec.n("mm"))])
                        g += 1
                v_ev = {}
                for tt in range(NTT):
                    bank = pb[2 + tt % 2]
                    for ic in range(2):
                        w3 = []
                        if ic == 0:
                            w3 = [("ldg", ldg_h), ("ev", kcvc_ev)]
                            if tt >= 2:
                                w3.append(("ev", v_ev[tt - 2]))
                        rec.op("tensor", "matmul", bank[:, 0:512],
                               vc_sb[:, ic * T + tt * 128:
                                     ic * T + tt * 128 + 128],
                               wv_sb[:, ic * 512:(ic + 1) * 512],
                               start=(ic == 0), stop=(ic == 1),
                               waits=w3, inc="mm")
                    v_ev[tt] = rec.op(
                        "vector", "tensor_copy", v4[:, tt, :, 0:DH],
                        bank[:, 0:512].rearrange("p (h d) -> p h d", d=DH),
                        waits=[("mm", rec.n("mm"))], inc="ev")
                kv_ready = rec.n("ev")

                wq4 = wq_h.rearrange("(ic p) o -> p ic o", p=128)
                qrope_ev = {}
                wq_mm = {}
                epi_ev_last = 0
                a_store = {}
                for qg in range(NTG):
                    qc0 = qg * TG
                    bufs_ = qg % 2
                    wsl = ([("mm", wq_mm[(qg - 2, 3)])] if qg >= 2 else [])
                    lds = rec.op("sync", "dma_start", out=srct4[:, bufs_],
                                 in_=src3[:, :, qc0:qc0 + TG],
                                 waits=wsl + [("st", src_done)],
                                 inc="ld", inck=16)
                    for oc in range(4):
                        gi = qg * 4 + oc
                        bufw = gi % 2
                        w = ([("mm", wq_mm[(qg - 2, oc)])]
                             if qg >= 2 else [])
                        ldv = rec.op("gpsimd", "dma_start",
                                     out=wqb4[:, bufw],
                                     in_=wq4[:, :,
                                             (cbase + oc) * 128:
                                             (cbase + oc) * 128 + 128],
                                     waits=w, inc="ldg", inck=16)
                        bank = pb[4 + oc % 2]
                        for ic in range(NCH):
                            w3 = []
                            if ic == 0:
                                w3 = [("ldg", ldv), ("ld", lds)]
                                if gi >= 2:
                                    w3.append(("ev", qrope_ev[gi - 2]))
                                if qg > 0 and oc < 2:
                                    w3.append(("act", rec.n("act")))
                            rec.op("tensor", "matmul", bank[:, :],
                                   wqb4[:, bufw, ic, :],
                                   srct4[:, bufs_, ic, :],
                                   start=(ic == 0), stop=(ic == NCH - 1),
                                   waits=w3, inc="mm")
                        wq_mm[(qg, oc)] = rec.n("mm")
                        qrope_ev[gi] = rope_bf(
                            bank[:, :],
                            q_sb[:, oc * T + qc0: oc * T + qc0 + TG],
                            qc0, [("mm", wq_mm[(qg, oc)])])
                    q_ready = rec.n("ev")

                    for hh in range(8):
                        hchunk = hh // 2
                        hrow = (hh % 2) * 64
                        act_head_start = rec.n("act")
                        exp_cnt = {}
                        SBK = (pb[0], pb[1], pb[4], pb[5], pb[6],
                               pb[7])
                        LKA = 6

                        def emit_su4(i):
                            sbank = SBK[i % LKA]
                            w = [("ev", max(q_ready, kv_ready))]
                            if i >= LKA:
                                w.append(("act", exp_cnt[i - LKA]))
                            else:
                                w.append(("act", act_head_start))
                            rec.op("tensor", "matmul", sbank[:, :],
                                   k_sb[hrow:hrow + 64,
                                        hchunk * T + i * 128:
                                        hchunk * T + i * 128 + 128],
                                   q_sb[hrow:hrow + 64,
                                        hchunk * T + qc0:
                                        hchunk * T + qc0 + TG],
                                   start=True, stop=True, waits=w,
                                   inc="mm")
                            smm = rec.n("mm")
                            ub = u_sb[:, (i % LKA) * TG:
                                      (i % LKA + 1) * TG]
                            exp_cnt[i] = rec.op(
                                "scalar", "activation", ub, sbank[:, :],
                                AF.Exp, scale=0.125,
                                waits=[("mm", smm)], inc="act")
                            return ub

                        ubs = {}
                        for j in range(LKA):
                            ubs[j] = emit_su4(j)
                        for i in range(NTT):
                            wv5 = [("act", exp_cnt[i])]
                            if i == 0 and epi_ev_last:
                                wv5.append(("ev", epi_ev_last))
                            rec.op("tensor", "matmul",
                                   pb[2][0:DH + 1, :],
                                   v_sb[:, tt_off8(i) + hh * (DH + 1):
                                        tt_off8(i) + hh * (DH + 1)
                                        + DH + 1],
                                   ubs[i], start=(i == 0),
                                   stop=(i == NTT - 1),
                                   waits=wv5, inc="mm")
                            del ubs[i]
                            if i + LKA < NTT:
                                ubs[i + LKA] = emit_su4(i + LKA)
                        av_mm = rec.n("mm")
                        rec.op("vector", "reciprocal", zr_sb[:, :],
                               pb[2][DH:DH + 1, :],
                               waits=[("mm", av_mm)], inc="ev")
                        flv = rec.op("vector", "tensor_copy",
                                     m1b[:, 0:256], m2b[:, 0:256],
                                     inc="ev")
                        zmm = rec.op("tensor", "matmul", pb[3][0:64, :],
                                     ones64_f32[0:1, :], zr_sb[:, :],
                                     start=True, stop=True,
                                     waits=[("ev", flv)], inc="mm")
                        rec.op("vector", "tensor_copy", zbc[:, :],
                               pb[3][0:64, :], waits=[("mm", zmm)])
                        w = ([("st", a_store[hchunk])]
                             if hchunk in a_store else [])
                        epi_ev_last = rec.op(
                            "vector", "tensor_mul",
                            a_sb[hrow:hrow + 64,
                                 hchunk * TG:(hchunk + 1) * TG],
                            pb[2][0:64, :], zbc[:, :], waits=w, inc="ev")
                        if hh % 2 == 1:
                            a_store[hchunk] = rec.op(
                                "gpsimd", "dma_start",
                                out=a_hbm[(cbase + hchunk) * 128:
                                          (cbase + hchunk) * 128 + 128,
                                          qc0:qc0 + TG],
                                in_=a_sb[:,
                                         hchunk * TG:(hchunk + 1) * TG],
                                waits=[("ev", epi_ev_last)],
                                inc="st", inck=16)
            fence(rec, sems)
        esKV.close()
        a_done = rec.n("st")

        with ExitStack() as esE:
            sbE = lambda n, s_, d: esE.enter_context(nc.sbuf_tensor(pre + n, s_, d))
            wob = sbE("wob", [128, 2 * NCH * 128], BF16)
            at_sb = sbE("at", [128, 2 * NCH * TG], BF16)
            srcf = sbE("srcf", [128, 2 * NCH * TG], F32)
            s1q = sbE("s1q", [128, NCH * TG], F32)
            n1c = sbE("n1c", [128, NCH], F32)
            xxst = sbE("xxst", [128, 2 * TG], BF16)
            wob4 = wob[:].rearrange("p (b ic o) -> p b ic o", b=2, o=128)
            at4 = at_sb[:].rearrange("p (b c n) -> p b c n", b=2, n=TG)
            srcf4 = srcf[:].rearrange("p (b c n) -> p b c n", b=2, n=TG)
            wo4 = wo_h.rearrange("(ic p) o -> p ic o", p=128)
            n13 = n1w_h.rearrange("(c p o) -> c p o", p=128, o=1)
            for c in range(NCH):
                rec.op("sync", "dma_start", out=n1c[:, c:c + 1],
                       in_=n13[c], inc="ld", inck=16)
            n1_ld = rec.n("ld")
            womm = {}
            s1add_ev = {}
            xxcp_ev = {}
            xx_st = {}
            wo_all_mm = {}
            sq_ln_mm = {}
            prev_rows = 0
            for qg in range(NTG):
                qc0 = qg * TG
                bufq = qg % 2
                w = [("st", a_done)]
                if qg >= 2:
                    w.append(("mm", wo_all_mm[qg - 2]))
                lda = rec.op("sync", "dma_start", out=at4[:, bufq],
                             in_=a3[:, :, qc0:qc0 + TG], waits=w,
                             inc="ld", inck=16)
                wsrc = ([("ev", s1add_ev[(qg - 2, NCH - 1)])]
                        if qg >= 2 else [])
                lds = rec.op("gpsimd", "dma_start", out=srcf4[:, bufq],
                             in_=src3[:, :, qc0:qc0 + TG],
                             waits=wsrc + [("st", src_done)],
                             inc="ldg", inck=16)
                for oc in range(NCH):
                    gi = qg * NCH + oc
                    bufw = gi % 2
                    w = [("mm", womm[gi - 2])] if gi >= 2 else []
                    ldw = rec.op("gpsimd", "dma_start", out=wob4[:, bufw],
                                 in_=wo4[:, :, oc * 128:(oc + 1) * 128],
                                 waits=w, inc="ldg", inck=16)
                    bank = pb[4 + oc % 2]
                    for ic in range(NCH):
                        w3 = []
                        if ic == 0:
                            w3 = [("ldg", ldw), ("ld", lda)]
                            if gi >= 2:
                                w3.append(("ev", s1add_ev[(qg, oc - 2)]
                                           if oc >= 2 else
                                           s1add_ev[(qg - 1, NCH - 2 + oc)]))
                        rec.op("tensor", "matmul", bank[:, :],
                               wob4[:, bufw, ic, :], at4[:, bufq, ic, :],
                               start=(ic == 0), stop=(ic == NCH - 1),
                               waits=w3, inc="mm")
                    womm[gi] = rec.n("mm")
                    s1add_ev[(qg, oc)] = rec.op(
                        "vector", "tensor_add",
                        s1q[:, oc * TG:(oc + 1) * TG], bank[:, :],
                        srcf4[:, bufq, oc, :],
                        waits=[("mm", womm[gi]), ("ldg", lds)], inc="ev")
                    sqv = rec.op("scalar", "activation",
                                 s1[:, :] if oc % 2 == 0 else s2[:, :],
                                 s1q[:, oc * TG:(oc + 1) * TG], AF.Square,
                                 waits=[("ev", s1add_ev[(qg, oc)])]
                                 + ([("ln_mm", sq_ln_mm[(qg, oc - 2)])]
                                    if oc >= 2 else
                                    ([("ln_mm", sq_ln_mm[(qg - 1,
                                                          NCH - 2 + oc)])]
                                     if qg > 0 else [])),
                                 inc="ln_sq")
                    w0 = [("ln_sq", sqv)]
                    if oc == 0 and prev_rows:
                        w0.append(("ln_ev", prev_rows))
                    rec.op("tensor", "matmul", pb[1][0:1, :],
                           onescol_f32[:, 0:1],
                           s1[:, :] if oc % 2 == 0 else s2[:, :],
                           start=(oc == 0), stop=(oc == NCH - 1),
                           waits=w0, inc="ln_mm")
                    sq_ln_mm[(qg, oc)] = rec.n("ln_mm")
                wo_all_mm[qg] = rec.n("mm")
                mm_stats = rec.n("ln_mm")
                vrow = row[0:1, 2 * TG:3 * TG]
                rrow = row[0:1, 3 * TG:4 * TG]
                rec.op("vector", "tensor_scalar_mul", vrow, pb[1][0:1, :],
                       1.0 / D, waits=[("ln_mm", mm_stats)], inc="ln_ev")
                sv = rec.op("scalar", "activation", vrow, vrow, AF.Sqrt,
                            bias=eps2col[0:1, 0:1],
                            waits=[("ln_ev", rec.n("ln_ev"))],
                            inc="ln_sq")
                rec.op("vector", "reciprocal", rrow, vrow,
                       waits=[("ln_sq", sv)], inc="ln_ev")
                flv = rec.op("vector", "tensor_copy", s3[:, 0:256],
                             s3[:, 256:512], inc="ln_ev")
                prev_rows = flv
                wbc_ = [("ln_ev", flv)]
                if qg > 0:
                    wbc_.append(("ev", xxcp_ev[(qg - 1, NCH - 1)]))
                rec.op("tensor", "matmul", pb[3][:, :],
                       onesrow_f32[0:1, :], rrow, start=True, stop=True,
                       waits=wbc_, inc="ln_mm")
                mm_bc = rec.n("ln_mm")
                for oc in range(NCH):
                    gi = qg * NCH + oc
                    bufx = gi % 2
                    rec.op("vector", "scalar_tensor_tensor",
                           s1q[:, oc * TG:(oc + 1) * TG],
                           s1q[:, oc * TG:(oc + 1) * TG],
                           n1c[:, oc:oc + 1], pb[3][:, :],
                           ALU.mult, ALU.mult,
                           waits=[("ld", n1_ld), ("ln_mm", mm_bc)])
                    wx = [("st", xx_st[gi - 2])] if gi >= 2 else []
                    xxcp_ev[(qg, oc)] = rec.op(
                        "vector", "tensor_copy",
                        xxst[:, bufx * TG:(bufx + 1) * TG],
                        s1q[:, oc * TG:(oc + 1) * TG],
                        waits=wx, inc="ev")
                    xx_st[gi] = rec.op(
                        "gpsimd", "dma_start",
                        out=xx_hbm[oc * 128:(oc + 1) * 128, qc0:qc0 + TG],
                        in_=xxst[:, bufx * TG:(bufx + 1) * TG],
                        waits=[("ev", xxcp_ev[(qg, oc)])], inc="st",
                        inck=16)
            xx_done = rec.n("st")
        fence(rec, sems)

        with ExitStack() as esG:
            sbG = lambda n, s_, d: esG.enter_context(nc.sbuf_tensor(pre + n, s_, d))
            xxr = sbG("xxr", [128, 2 * NCH * 128], BF16)
            wg_sb = sbG("wgsb", [128, NCH * 2], BF16)
            we_sb = sbG("wesb", [128, NCH * E], BF16)
            gbrow = sbG("gbrow", [1, 2], BF16)
            ebrow = sbG("ebrow", [1, E], BF16)
            id_sb = sbG("idsb", [128, 128], F32)
            lg_sb = sbG("lgG", [128, E], F32)
            glg = sbG("glg", [128, 2], F32)
            grep = sbG("grep", [128, E], F32)
            gmax = sbG("gmaxG", [128, 1], F32)
            ngmax = sbG("ngmax", [128, 1], F32)
            gex = sbG("gex", [128, 2], F32)
            gsum = sbG("gsum", [128, 1], F32)
            wgp = sbG("wgp", [128, 1], F32)
            neginf = sbG("neginf", [128, E], F32)
            elm = sbG("elm", [128, E], F32)
            emax = sbG("emaxG", [128, 1], F32)
            nemax = sbG("nemaxG", [128, 1], F32)
            eex = sbG("eex", [128, E], F32)
            esum = sbG("esumG", [128, 1], F32)
            erec = sbG("erecG", [128, 1], F32)
            pG = sbG("pG", [128, E], F32)
            top1 = sbG("top1", [128, 1], F32)
            m1G = sbG("m1G", [128, E], F32)
            p2G = sbG("p2G", [128, E], F32)
            top2 = sbG("top2", [128, 1], F32)
            m2G = sbG("m2G", [128, E], F32)
            wts_t = sbG("wtstG", [128, E], F32)
            wtt = sbG("wttG", [8, 128], F32)
            xxr4 = xxr[:].rearrange("p (b c n) -> p b c n", b=2, n=128)

            rec.op("vector", "memset", neginf[:], -1e30)
            rec.op("sync", "dma_start", out=id_sb[:], in_=id_h, inc="ld",
                   inck=16)
            id_ld = rec.n("ld")
            rec.op("gpsimd", "dma_start",
                   out=wg_sb[:].rearrange("p (c n) -> p c n", n=2),
                   in_=wg_h.rearrange("(c p) n -> p c n", p=128),
                   inc="ldg", inck=16)
            rec.op("gpsimd", "dma_start",
                   out=we_sb[:].rearrange("p (c n) -> p c n", n=E),
                   in_=we_h.rearrange("(c p) n -> p c n", p=128),
                   inc="ldg", inck=16)
            rec.op("gpsimd", "dma_start", out=gbrow[:], in_=gb_h,
                   inc="ldg", inck=16)
            rec.op("gpsimd", "dma_start", out=ebrow[:], in_=eb_h,
                   inc="ldg", inck=16)
            g_ldg = rec.n("ldg")
            rt_mm = {}
            wt_store = {}
            cev_prev = 0
            for tt in range(NTT):
                buf = tt % 2
                w = [("st", xx_done), ("ldg", g_ldg)]
                if tt >= 2:
                    w.append(("mm", rt_mm[tt - 2]))
                ldv = rec.op("sync", "dma_start", out=xxr4[:, buf],
                             in_=xx3[:, :, tt * 128:(tt + 1) * 128],
                             waits=w, inc="ld", inck=16)
                bank = pb[4]
                gbank = pb[5]
                for ic in range(NCH):
                    w2_ = [("ld", ldv), ("ldg", g_ldg)]
                    if ic == 0 and tt > 0:
                        w2_.append(("ev", rec.n("ev")))
                    rec.op("tensor", "matmul", bank[:, 0:E],
                           xxr4[:, buf, ic, :],
                           we_sb[:, ic * E:(ic + 1) * E],
                           start=(ic == 0), stop=False, waits=w2_,
                           inc="mm")
                    rec.op("tensor", "matmul", gbank[:, 0:2],
                           xxr4[:, buf, ic, :],
                           wg_sb[:, ic * 2:(ic + 1) * 2],
                           start=(ic == 0), stop=False,
                           waits=w2_ if ic == 0 else (), inc="mm")
                rec.op("tensor", "matmul", bank[:, 0:E],
                       onesrow_bf[0:1, :], ebrow[:], start=False,
                       stop=True, inc="mm")
                rec.op("tensor", "matmul", gbank[:, 0:2],
                       onesrow_bf[0:1, :], gbrow[:], start=False,
                       stop=True, inc="mm")
                rt_mm[tt] = rec.n("mm")
                c1 = rec.op("vector", "tensor_copy", lg_sb[:],
                            bank[:, 0:E], waits=[("mm", rt_mm[tt])],
                            inc="ev")
                c2 = rec.op("vector", "tensor_copy", glg[:],
                            gbank[:, 0:2], inc="ev")
                gm = rec.op("vector", "tensor_reduce", gmax[:, 0:1],
                            gbank[:, 0:2], mybir.AxisListType.X, ALU.max,
                            inc="ev")
                ng = rec.op("vector", "tensor_scalar_mul", ngmax[:, 0:1],
                            gmax[:, 0:1], -1.0, waits=[("ev", gm)],
                            inc="ev")
                ng = rec.op("vector", "tensor_copy", s3[:, 0:256],
                            s3[:, 256:512], inc="ev")
                ge = rec.op("scalar", "activation", gex[:], glg[:],
                            AF.Exp, bias=ngmax[:, 0:1],
                            waits=[("ev", ng)], inc="act")
                gs = rec.op("vector", "tensor_reduce", gsum[:, 0:1],
                            gex[:], mybir.AxisListType.X, ALU.add,
                            waits=[("act", ge)], inc="ev")
                wg_ = rec.op("vector", "reciprocal", wgp[:, 0:1],
                             gsum[:, 0:1], waits=[("ev", gs)], inc="ev")
                gr = rec.op("vector", "tensor_copy",
                            grep[:].rearrange("p (g r) -> p g r", r=4),
                            glg[:].rearrange("p (g o) -> p g o", o=1)
                            .broadcast_to([128, 2, 4]),
                            waits=[("ev", c2)], inc="ev")
                mk = rec.op("vector", "tensor_scalar", grep[:], grep[:],
                            gmax[:, 0:1], None, ALU.is_equal,
                            waits=[("ev", gr)], inc="ev")
                msk_ = rec.op("vector", "scalar_tensor_tensor", elm[:],
                              lg_sb[:], 4096.0, grep[:], ALU.add, ALU.mult,
                              waits=[("ev", mk), ("ev", c1)], inc="ev")
                se = rec.op("vector", "tensor_scalar_add", elm[:], elm[:],
                            -4096.0, waits=[("ev", msk_)], inc="ev")
                em = rec.op("vector", "tensor_reduce", emax[:, 0:1],
                            elm[:], mybir.AxisListType.X, ALU.max,
                            waits=[("ev", se)], inc="ev")
                ne = rec.op("vector", "tensor_scalar_mul",
                            nemax[:, 0:1], emax[:, 0:1], -1.0,
                            waits=[("ev", em)], inc="ev")
                ne = rec.op("vector", "tensor_copy", s3[:, 0:256],
                            s3[:, 256:512], inc="ev")
                ee = rec.op("scalar", "activation", eex[:], elm[:],
                            AF.Exp, bias=nemax[:, 0:1],
                            waits=[("ev", ne)], inc="act")
                es_ = rec.op("vector", "tensor_reduce", esum[:, 0:1],
                             eex[:], mybir.AxisListType.X, ALU.add,
                             waits=[("act", ee)], inc="ev")
                er = rec.op("vector", "reciprocal", erec[:, 0:1],
                            esum[:, 0:1], waits=[("ev", es_)], inc="ev")
                pp = rec.op("vector", "tensor_scalar", pG[:], eex[:],
                            erec[:, 0:1], wgp[:, 0:1], ALU.mult,
                            ALU.mult, waits=[("ev", er), ("ev", wg_)],
                            inc="ev")
                t1_ = rec.op("vector", "tensor_reduce", top1[:, 0:1],
                             pG[:], mybir.AxisListType.X, ALU.max,
                             waits=[("ev", pp)], inc="ev")
                m1_ = rec.op("vector", "tensor_scalar", m1G[:], pG[:],
                             top1[:, 0:1], -1.0, ALU.is_equal, ALU.mult,
                             waits=[("ev", t1_)], inc="ev")
                p2_ = rec.op("vector", "tensor_scalar", m1G[:], m1G[:],
                             1.0, None, ALU.add, waits=[("ev", m1_)],
                             inc="ev")
                p2b = rec.op("vector", "tensor_mul", p2G[:], pG[:],
                             m1G[:], waits=[("ev", p2_)], inc="ev")
                t2_ = rec.op("vector", "tensor_reduce", top2[:, 0:1],
                             p2G[:], mybir.AxisListType.X, ALU.max,
                             waits=[("ev", p2b)], inc="ev")
                wv_ = rec.op("vector", "tensor_scalar", m2G[:], pG[:],
                             top2[:, 0:1], None, ALU.is_ge,
                             waits=[("ev", t2_)], inc="ev")
                rec.op("vector", "tensor_mul", wts_t[:], pG[:], m2G[:],
                       waits=[("ev", wv_)], inc="ev")
                wf = rec.op("vector", "tensor_copy", s3[:, 0:256],
                            s3[:, 256:512], inc="ev")
                wcevp = [("ev", cev_prev)] if tt > 0 else []
                tmm = rec.op("tensor", "matmul", pb[6][0:8, :128],
                             wts_t[:], id_sb[:], start=True, stop=True,
                             waits=[("ev", wf), ("ld", id_ld)] + wcevp,
                             inc="mm")
                wstp = [("st", wt_store[tt - 1])] if tt > 0 else []
                cev_prev = rec.op("vector", "tensor_copy", wtt[:],
                                  pb[6][0:8, :128],
                                  waits=[("mm", tmm)] + wstp, inc="ev")
                wt_store[tt] = rec.op(
                    "gpsimd", "dma_start",
                    out=wt_hbm[:, tt * 128:(tt + 1) * 128], in_=wtt[:],
                    waits=[("ev", cev_prev)], inc="st", inck=16)
                if debug and tt == 0:
                    for j, buf_ in enumerate((lg_sb, glg, grep, elm, eex,
                                              pG, m2G)):
                        nel = buf_.shape[1] if hasattr(buf_, "shape") else 8
                        rec.op("gpsimd", "dma_start",
                               out=dbg_h[:, j * 8:j * 8 + nel],
                               in_=buf_[:], waits=[("ev", cev_prev)],
                               inc="st", inck=16,
                               allow_slow_non_contiguous=True)
                    rec.op("vector", None, waits=[("st", rec.n("st"))])
            wt_done = rec.n("st")
        fence(rec, sems)

        TH = T // 2
        NTGH = TH // TG
        with ExitStack() as esF:
            sbF = lambda n, s_, d: esF.enter_context(nc.sbuf_tensor(pre + n, s_, d))
            xxt = sbF("xxt", [128, NCH * TG], BF16)
            hsw = sbF("hsw", [128, NHC * TH], BF16)
            outh = sbF("outh", [128, NCH * TH], BF16)
            w2sb = sbF("w2sb", [128, NHC * D], BF16)
            wsib = sbF("wsib", [128, 2 * 2 * NCH * 128], BF16)
            w1b = sbF("w1b", [128, 2 * 2 * NCH * 128], BF16)
            wsob = sbF("wsob", [128, 2 * NHC * 128], BF16)
            swt = sbF("swt", [128, NHC * TG], BF16)
            rbf2 = hstage[:, 0:TG]
            tbf = hstage[:, TG:2 * TG]
            wbcF = sbF("wbcF", [128, TG], F32)
            n2c = sbF("n2c", [128, NCH], F32)
            xres = xt
            xoutF = sbF("xoutF", [128, 2 * TG], F32)
            xxt3 = xxt[:].rearrange("p (c n) -> p c n", n=TG)
            wsib5 = wsib[:].rearrange("p (b a ic o) -> p b a ic o", b=2,
                                      a=2, o=128)
            w1b5 = w1b[:].rearrange("p (b a ic o) -> p b a ic o", b=2,
                                    a=2, o=128)
            wsob4 = wsob[:].rearrange("p (b ic o) -> p b ic o", b=2,
                                      o=128)
            wsi4 = wsi_h.rearrange("(ic p) o -> p ic o", p=128)
            w14 = w1_h.rearrange("(ic p) o -> p ic o", p=128)
            wso4 = wso_h.rearrange("(ic p) o -> p ic o", p=128)
            n23 = n2w_h.rearrange("(c p o) -> c p o", p=128, o=1)
            for c in range(NCH):
                rec.op("sync", "dma_start", out=n2c[:, c:c + 1],
                       in_=n23[c], inc="ld", inck=16)
            n2_ld = rec.n("ld")

            silu_mm = {}
            sho_mm = {}
            sho_ev = {}
            xx_guard = {}
            fout_ev = {}
            fout_st = {}
            route_ev = []
            ev1_prev = 0
            ev2_prev = 0
            for half in range(2):
                fence(rec, sems)
                for tgl in range(NTGH):
                    tg = half * NTGH + tgl
                    c0 = tg * TG
                    w = [("st", xx_done)]
                    if tg >= 1 and (tg - 1) in xx_guard:
                        w.append(("ev", xx_guard[tg - 1]))
                    ldx = rec.op("sync", "dma_start", out=xxt3[:],
                                 in_=xx3[:, :, c0:c0 + TG], waits=w,
                                 inc="ld", inck=16)
                    for fch in range(NHC):
                        gi = tg * NHC + fch
                        bufw = gi % 2
                        wld = ([("mm", silu_mm[gi - 2])]
                               if gi - 2 in silu_mm else [])
                        rec.op("gpsimd", "dma_start",
                               out=wsib5[:, bufw, 0],
                               in_=wsi4[:, :, fch * 128:fch * 128 + 128],
                               waits=wld, inc="ldg", inck=16)
                        rec.op("gpsimd", "dma_start",
                               out=wsib5[:, bufw, 1],
                               in_=wsi4[:, :, HID + fch * 128:
                                       HID + fch * 128 + 128],
                               inc="ldg", inck=16)
                        rec.op("gpsimd", "dma_start",
                               out=w1b5[:, bufw, 0],
                               in_=w14[:, :, fch * 128:fch * 128 + 128],
                               inc="ldg", inck=16)
                        ldw = rec.op("gpsimd", "dma_start",
                                     out=w1b5[:, bufw, 1],
                                     in_=w14[:, :, HID + fch * 128:
                                             HID + fch * 128 + 128],
                                     inc="ldg", inck=16)
                        banks = (pb[0], pb[1], pb[2], pb[3])
                        srcs = (wsib5[:, bufw, 0], wsib5[:, bufw, 1],
                                w1b5[:, bufw, 0], w1b5[:, bufw, 1])
                        for bi, (bank, wsl) in enumerate(zip(banks,
                                                             srcs)):
                            for ic in range(NCH):
                                w3 = []
                                if ic == 0:
                                    w3 = [("ldg", ldw), ("ld", ldx)]
                                    if bi < 2 and ev1_prev:
                                        w3.append(("ev", ev1_prev))
                                    if bi >= 2 and ev2_prev:
                                        w3.append(("ev", ev2_prev))
                                rec.op("tensor", "matmul", bank[:, :],
                                       wsl[:, ic, :],
                                       xxt3[:, ic, :],
                                       start=(ic == 0),
                                       stop=(ic == NCH - 1),
                                       waits=w3, inc="mm")
                        silu_mm[gi] = rec.n("mm")
                        sva = rec.op("scalar", "activation", s1[:, :],
                                     pb[0][:, :], AF.Silu,
                                     waits=[("mm", silu_mm[gi])]
                                     + ([("ev", ev1_prev)]
                                        if ev1_prev else []),
                                     inc="act")
                        wsw = [("act", sva)]
                        if fch == 0 and sho_mm:
                            wsw.append(("mm", max(sho_mm.values())))
                        ev1_prev = rec.op(
                            "vector", "tensor_mul",
                            swt[:, fch * TG:(fch + 1) * TG], s1[:, :],
                            pb[1][:, :], waits=wsw, inc="ev")
                        svb = rec.op("scalar", "activation", s2[:, :],
                                     pb[2][:, :], AF.Silu,
                                     waits=[("mm", silu_mm[gi])]
                                     + ([("ev", ev2_prev)]
                                        if ev2_prev else []),
                                     inc="act")
                        ev2_prev = rec.op(
                            "vector", "tensor_mul",
                            hsw[:, fch * TH + tgl * TG:
                                fch * TH + tgl * TG + TG],
                            s2[:, :], pb[3][:, :],
                            waits=[("act", svb)], inc="ev")
                    sw_done = rec.n("ev")
                    for oc in range(NCH):
                        gi2 = tg * NCH + oc
                        bufw = gi2 % 2
                        wld = ([("mm", sho_mm[gi2 - 2])]
                               if gi2 - 2 in sho_mm else [])
                        ldw = rec.op("gpsimd", "dma_start",
                                     out=wsob4[:, bufw],
                                     in_=wso4[:, :,
                                             oc * 128:(oc + 1) * 128],
                                     waits=wld, inc="ldg", inck=16)
                        bank = pb[4 + oc % 2]
                        for fc in range(NHC):
                            w3 = []
                            if fc == 0:
                                w3 = [("ldg", ldw), ("ev", sw_done)]
                                if gi2 - 2 in sho_ev:
                                    w3.append(("ev", sho_ev[gi2 - 2]))
                            rec.op("tensor", "matmul", bank[:, :],
                                   wsob4[:, bufw, fc, :],
                                   swt[:, fc * TG:(fc + 1) * TG],
                                   start=(fc == 0),
                                   stop=(fc == NHC - 1), waits=w3,
                                   inc="mm")
                        sho_mm[gi2] = rec.n("mm")
                        sho_ev[gi2] = rec.op(
                            "vector", "scalar_tensor_tensor",
                            outh[:, oc * TH + tgl * TG:
                                 oc * TH + tgl * TG + TG],
                            bank[:, :], 1.0, xxt3[:, oc, :],
                            ALU.mult, ALU.add,
                            waits=[("mm", sho_mm[gi2])], inc="ev")
                    xx_guard[tg] = sho_ev[tg * NCH + NCH - 1]
                sh_all = rec.n("ev")

                for e in range(E):
                    wld = [("mm", rec.n("mm"))]
                    for hc in range(NHC):
                        rec.op("gpsimd", "dma_start",
                               out=w2sb[:, hc * D:(hc + 1) * D],
                               in_=w2_h[e, hc * 128:(hc + 1) * 128, :],
                               waits=wld if hc == 0 else (),
                               inc="ldg", inck=16)
                    w2_ld = rec.n("ldg")
                    for tgl in range(NTGH):
                        tg = half * NTGH + tgl
                        c0 = tg * TG
                        wbw = [("st", wt_done)]
                        if route_ev:
                            wbw.append(("ev", route_ev[-1]))
                        ldwbc = rec.op("sync", "dma_start", out=wbcF[:],
                                       in_=wt_hbm[e:e + 1, c0:c0 + TG]
                                       .broadcast_to([128, TG]),
                                       waits=wbw, inc="ld", inck=16)
                        for oc in range(NCH):
                            bank = pb[oc % 2]
                            for fc in range(NHC):
                                w3 = []
                                if fc == 0:
                                    w3 = [("ldg", w2_ld),
                                          ("ev", sh_all)]
                                    if len(route_ev) >= 2:
                                        w3.append(("ev",
                                                   route_ev[-2]))
                                rec.op("tensor", "matmul", bank[:, :],
                                       w2sb[:, fc * D + oc * 128:
                                            fc * D + oc * 128 + 128],
                                       hsw[:, fc * TH + tgl * TG:
                                           fc * TH + tgl * TG + TG],
                                       start=(fc == 0),
                                       stop=(fc == NHC - 1),
                                       waits=w3, inc="mm")
                            ymm = rec.n("mm")
                            rec.op("vector", "scalar_tensor_tensor",
                                   tbf, bank[:, :], 1.0, wbcF[:],
                                   ALU.mult, ALU.mult,
                                   waits=[("mm", ymm), ("ld", ldwbc)])
                            rev = rec.op(
                                "vector", "tensor_add",
                                outh[:, oc * TH + tgl * TG:
                                     oc * TH + tgl * TG + TG],
                                tbf,
                                outh[:, oc * TH + tgl * TG:
                                     oc * TH + tgl * TG + TG],
                                inc="ev")
                            route_ev.append(rev)
                routed_done = rec.n("ev")

                prev_rows2 = 0
                f4_ln_mm = {}
                for tgl in range(NTGH):
                    tg = half * NTGH + tgl
                    c0 = tg * TG
                    for oc in range(NCH):
                        w = [("ev", routed_done)]
                        if oc >= 2:
                            w.append(("ln_mm", f4_ln_mm[oc - 2]))
                        elif tgl > 0 or half > 0:
                            w.append(("ln_mm", rec.n("ln_mm")))
                        if oc < 2 and fout_ev:
                            w.append(("ev", max(fout_ev.values())))
                        sqv = rec.op("scalar", "activation",
                                     s1[:, :] if oc % 2 == 0
                                     else s2[:, :],
                                     outh[:, oc * TH + tgl * TG:
                                          oc * TH + tgl * TG + TG],
                                     AF.Square, waits=w, inc="ln_sq")
                        w0 = [("ln_sq", sqv)]
                        if oc == 0 and prev_rows2:
                            w0.append(("ln_ev", prev_rows2))
                        rec.op("tensor", "matmul", pb[1][0:1, :],
                               onescol_f32[:, 0:1],
                               s1[:, :] if oc % 2 == 0 else s2[:, :],
                               start=(oc == 0), stop=(oc == NCH - 1),
                               waits=w0, inc="ln_mm")
                        f4_ln_mm[oc] = rec.n("ln_mm")
                    mm_stats = rec.n("ln_mm")
                    vrow = row[0:1, 2 * TG:3 * TG]
                    rrow = row[0:1, 3 * TG:4 * TG]
                    rec.op("vector", "tensor_scalar_mul", vrow,
                           pb[1][0:1, :], 1.0 / D,
                           waits=[("ln_mm", mm_stats)], inc="ln_ev")
                    sv = rec.op("scalar", "activation", vrow, vrow,
                                AF.Sqrt, bias=eps2col[0:1, 0:1],
                                waits=[("ln_ev", rec.n("ln_ev"))],
                                inc="ln_sq")
                    rec.op("vector", "reciprocal", rrow, vrow,
                           waits=[("ln_sq", sv)], inc="ln_ev")
                    flv = rec.op("vector", "tensor_copy", s3[:, 0:256],
                                 s3[:, 256:512], inc="ln_ev")
                    prev_rows2 = flv
                    wbcw = [("ln_ev", flv)]
                    rec.op("tensor", "matmul", pb[3][:, :],
                           onesrow_f32[0:1, :], rrow, start=True,
                           stop=True, waits=wbcw, inc="ln_mm")
                    mm_bc = rec.n("ln_mm")
                    rcp = rec.op("vector", "tensor_copy", rbf2,
                                 pb[3][:, :], waits=[("ln_mm", mm_bc)],
                                 inc="ln_ev")
                    for oc in range(NCH):
                        gi3 = tg * NCH + oc
                        bufx = gi3 % 2
                        wx = ([("ev", fout_ev[gi3 - 2])]
                              if gi3 - 2 in fout_ev else [])
                        ldr = rec.op("sync", "dma_start",
                                     out=xres[:,
                                              bufx * TG:(bufx + 1) * TG],
                                     in_=x_hbm[oc * 128:(oc + 1) * 128,
                                               c0:c0 + TG],
                                     waits=wx, inc="ld", inck=16)
                        rec.op("vector", "tensor_mul", tbf,
                               outh[:, oc * TH + tgl * TG:
                                    oc * TH + tgl * TG + TG],
                               rbf2, waits=[("ln_ev", rcp)])
                        rec.op("vector", "tensor_scalar_mul", s1[:, :],
                               tbf, n2c[:, oc:oc + 1],
                               waits=[("ld", n2_ld)])
                        wst = ([("st2", fout_st[gi3 - 2])]
                               if gi3 - 2 in fout_st else [])
                        fout_ev[gi3] = rec.op(
                            "vector", "tensor_add",
                            xoutF[:, bufx * TG:(bufx + 1) * TG],
                            s1[:, :],
                            xres[:, bufx * TG:(bufx + 1) * TG],
                            waits=[("ld", ldr)] + wst, inc="ev")
                        fout_st[gi3] = rec.op(
                            "sync", "dma_start",
                            out=xo_hbm[oc * 128:(oc + 1) * 128,
                                       c0:c0 + TG],
                            in_=xoutF[:, bufx * TG:(bufx + 1) * TG],
                            waits=[("ev", fout_ev[gi3])], inc="st2",
                            inck=16)

        rec.op("gpsimd", None, waits=[("st", rec.n("st")),
                                      ("st2", rec.n("st2"))])
        if ctx is None:
            rec.replay(nc, sems)
        else:
            fence(rec, sems)
    return nc


def tt_off8(kt):
    return kt * (8 * 65)


def run_l1_ffn(x_fm_list, ins, debug=False, ncores=NCORES):
    perm = _evenodd_perm()
    cos2, ss2 = _rope_tables_l1()
    ss2 = _partner_swap(ss2)
    nc = build_l1_ffn(debug=debug)
    shared = dict(
        WqP=np.ascontiguousarray(ins["L1_ffn_Wq"][:, perm]),
        Wkc=ins["L1_ffn_Wk_c"], Wvc=ins["L1_ffn_Wv_c"],
        WkP=np.ascontiguousarray(ins["L1_ffn_Wk"][:, perm]),
        Wv=ins["L1_ffn_Wv"], Wo=ins["L1_ffn_Wo"],
        Wsi=ins["L1_ffn_W_shared_in"], Wso=ins["L1_ffn_W_shared_out"],
        W1s=ins["L1_ffn_W1_shared"], W2e=ins["L1_ffn_W2_experts"],
        Wg=ins["L1_ffn_Wg_gate"], We=ins["L1_ffn_We_gate"],
        gb=np.ascontiguousarray(ins["L1_ffn_group_bias"].reshape(1, 2)),
        eb=np.ascontiguousarray(ins["L1_ffn_expert_bias"].reshape(1, 8)),
        n1w=ins["L1_ffn_norm1_w"], n2w=ins["L1_ffn_norm2_w"],
        cos2=cos2, ss2=ss2, idm=np.eye(128, dtype=np.float32),
    )
    in_maps = []
    for b in range(ncores):
        m = dict(shared)
        m["x_fm"] = x_fm_list[b]
        in_maps.append(m)
    res = run_bass_kernel_spmd(nc, in_maps, core_ids=list(range(ncores)))
    if debug:
        return res.results
    return [r["x_out"] for r in res.results]



_L0_MASKS, _L0_TILEMAP = None, None


def _get_masks():
    global _L0_MASKS, _L0_TILEMAP
    if _L0_MASKS is None:
        _L0_MASKS, _L0_TILEMAP = _l0_masks()
    return _L0_MASKS, _L0_TILEMAP


def run_l0_attn(x_fm_list, ins):
    masks, tilemap = _get_masks()
    cos2, ss2 = _rope_tables_l0()
    ss2 = _partner_swap(ss2)
    qp = _qperm()
    nc = build_l0_attn(masks.shape[0], tilemap)
    shared = dict(
        Wq=np.ascontiguousarray(ins["L0_attn_Wq"][:, qp]),
        bq=np.ascontiguousarray(ins["L0_attn_bq"][qp]),
        Wk=ins["L0_attn_Wk"],
        Wv=ins["L0_attn_Wv"],
        Wo=np.ascontiguousarray(ins["L0_attn_Wo"][qp, :]),
        bo=ins["L0_attn_bo"],
        hw=np.ascontiguousarray(ins["L0_attn_head_weights"].reshape(1, 16)),
        cos2=cos2, ss2=ss2, masks=masks,
        bkv=np.ascontiguousarray(
            np.stack([ins["L0_attn_bk"], ins["L0_attn_bv"]])),
    )
    in_maps = []
    for b in range(NCORES):
        m = dict(shared)
        m["x_fm"] = x_fm_list[b]
        in_maps.append(m)
    res = run_bass_kernel_spmd(nc, in_maps, core_ids=list(range(NCORES)))
    return [r["x_out"] for r in res.results]


def _host_l0_attn(ins):
    masks, tilemap = _get_masks()
    cos2, ss2 = _rope_tables_l0()
    ss2 = _partner_swap(ss2)
    qp = _qperm()
    return dict(
        Wq=np.ascontiguousarray(ins["L0_attn_Wq"][:, qp]),
        bq=np.ascontiguousarray(ins["L0_attn_bq"][qp]),
        Wk=ins["L0_attn_Wk"],
        Wv=ins["L0_attn_Wv"],
        Wo=np.ascontiguousarray(ins["L0_attn_Wo"][qp, :]),
        bo=ins["L0_attn_bo"],
        hw=np.ascontiguousarray(ins["L0_attn_head_weights"].reshape(1, 16)),
        cos2=cos2, ss2=ss2, masks=masks,
        bkv=np.ascontiguousarray(
            np.stack([ins["L0_attn_bk"], ins["L0_attn_bv"]])),
    )


def _host_l0_ffn(ins):
    return dict(
        wr=ins["L0_ffn_w_router"], w1=ins["L0_ffn_w1"], b1=ins["L0_ffn_b1"],
        w2=ins["L0_ffn_w2"], b2=ins["L0_ffn_b2"],
        idm=np.eye(128, dtype=np.float32),
    )


def _host_l1_attn(ins):
    return dict(
        WinT=np.ascontiguousarray(ins["L1_attn_W_in"].T),
        WigT=np.ascontiguousarray(ins["L1_attn_W_ig"].T),
        WrgT=np.ascontiguousarray(ins["L1_attn_W_rg"].T),
        WoutT=np.ascontiguousarray(ins["L1_attn_W_out"].T),
        big=ins["L1_attn_b_ig"], brg=ins["L1_attn_b_rg"],
        lam=ins["L1_attn_lam"],
        h0v=np.ascontiguousarray(ins["L1_attn_h0"].reshape(-1)),
        nw=ins["L1_attn_norm_w"],
    )


def _host_l1_ffn(ins):
    perm = _evenodd_perm()
    cos2, ss2 = _rope_tables_l1()
    ss2 = _partner_swap(ss2)
    return dict(
        WqP=np.ascontiguousarray(ins["L1_ffn_Wq"][:, perm]),
        Wkc=ins["L1_ffn_Wk_c"], Wvc=ins["L1_ffn_Wv_c"],
        WkP=np.ascontiguousarray(ins["L1_ffn_Wk"][:, perm]),
        Wv=ins["L1_ffn_Wv"], Wo=ins["L1_ffn_Wo"],
        Wsi=ins["L1_ffn_W_shared_in"], Wso=ins["L1_ffn_W_shared_out"],
        W1s=ins["L1_ffn_W1_shared"], W2e=ins["L1_ffn_W2_experts"],
        Wg=ins["L1_ffn_Wg_gate"], We=ins["L1_ffn_We_gate"],
        gb=np.ascontiguousarray(ins["L1_ffn_group_bias"].reshape(1, 2)),
        eb=np.ascontiguousarray(ins["L1_ffn_expert_bias"].reshape(1, 8)),
        n1w=ins["L1_ffn_norm1_w"], n2w=ins["L1_ffn_norm2_w"],
        cos2=cos2, ss2=ss2, idm=np.eye(128, dtype=np.float32),
    )


def build_fused(nmasks, tilemap, nsec=4, debug_d=False):
    nc = bass.Bass("TRN2", target_bir_lowering=False)
    rec = Rec()
    x0 = nc.dram_tensor("x_fm", [D, T], F32, kind="ExternalInput").ap()
    xout = nc.dram_tensor("x_out", [D, T], F32, kind="ExternalOutput").ap()
    xs = [x0]
    for i in range(1, nsec):
        xs.append(nc.dram_tensor(f"x{i}_scr", [D, T], F32,
                                 kind="Internal").ap())
    xs.append(xout)
    es = ExitStack()
    with es:
        pb = [es.enter_context(nc.psum_tensor(f"pb{i}", [128, TG], F32))
              for i in range(8)]
        sems = {s: es.enter_context(nc.semaphore(s)) for s in SEMNAMES}
        emitters = [
            lambda c: build_l0_attn(nmasks, tilemap, ctx=c),
            lambda c: build_l0_ffn(ctx=c),
            lambda c: build_l1_attn(ctx=c),
            lambda c: build_l1_ffn(ctx=c, debug=debug_d),
        ]
        pres = ["A_", "B_", "C_", "D_"]
        for i in range(nsec):
            emitters[i]((nc, rec, sems, pb, xs[i], xs[i + 1], pres[i]))
        rec.replay(nc, sems)
    return nc


def kernel(**ins):
    return _kernel_n(ins, 4)


def _kernel_n(ins, nsec):
    masks, tilemap = _get_masks()
    nc = build_fused(masks.shape[0], tilemap, nsec=nsec)
    shared = {}
    hosts = (("A_", _host_l0_attn), ("B_", _host_l0_ffn),
             ("C_", _host_l1_attn), ("D_", _host_l1_ffn))
    for pre, fn in hosts[:nsec]:
        for k, v in fn(ins).items():
            shared[pre + k] = v
    x = ins["L0_attn_x"]
    in_maps = []
    for b in range(NCORES):
        m = dict(shared)
        m["x_fm"] = np.ascontiguousarray(x[b].T)
        in_maps.append(m)
    res = run_bass_kernel_spmd(nc, in_maps, core_ids=list(range(NCORES)))
    return np.stack([np.ascontiguousarray(r["x_out"].T)
                     for r in res.results])
